# version 08
# baseline (speedup 1.0000x reference)
import os
import numpy as np
import ml_dtypes
import concourse.bass as bass
import concourse.mybir as mybir
from concourse.bass_utils import run_bass_kernel_spmd

F32 = mybir.dt.float32
BF16 = mybir.dt.bfloat16
FP8 = mybir.dt.float8e4
I32 = mybir.dt.int32
AF = mybir.ActivationFunctionType
OP = mybir.AluOpType

LN2 = 0.6931471805599453
D = 1024
H4 = 4096
NCORES = 8

C_EPS = 1e-5
C_HALF = 0.5
C_64_5 = 64.5
C_4096_5 = 4096.5
C_P35L2 = 35.0 * LN2
C_M35L2 = -35.0 * LN2
C_P19L2 = 19.0 * LN2
C_M19L2 = -19.0 * LN2
C_P26L2 = 26.0 * LN2
C_M26L2 = -26.0 * LN2
CONSTS = [C_EPS, C_HALF, C_64_5, C_4096_5, C_P35L2, C_M35L2,
          C_P19L2, C_M19L2, C_P26L2, C_M26L2]


class Prog:

    OWN = {"t": "t", "v": "v", "s": "s"}

    def __init__(self, nc):
        self.nc = nc
        self.q = {"t": [], "v": [], "s": [], "y": [], "g": []}
        self.sem = {}
        self.cnt = {}
        self.lastw = {e: {} for e in self.q}

    def mksem(self, name):
        self.sem[name] = self.nc.alloc_semaphore(name)
        self.cnt[name] = 0

    def latest(self, sem):
        return (sem, self.cnt[sem])

    def wait(self, eng, *deps):
        for dp in deps:
            if dp is None:
                continue
            if isinstance(dp, str):
                dp = (dp, self.cnt[dp])
            sem, c = dp
            if c <= 0:
                continue
            if self.OWN.get(eng) == sem:
                continue
            if self.lastw[eng].get(sem, 0) >= c:
                continue
            self.lastw[eng][sem] = c
            self.q[eng].append(lambda e, h=self.sem[sem], c=c: e.wait_ge(h, c))

    def op(self, eng, fn, inc=None, amt=1):
        if inc is None:
            self.q[eng].append(fn)
            return None
        h = self.sem[inc]
        self.cnt[inc] += amt
        self.q[eng].append(lambda e, fn=fn, h=h, a=amt: fn(e).then_inc(h, a))
        return (inc, self.cnt[inc])

    def barrier_on(self, eng):
        self.wait(eng, *list(self.sem.keys()))


def _fmcols(buf, w, tc):
    return buf.ap().rearrange("p (c w) -> p c w", w=w)[:, :, tc * 128:(tc + 1) * 128]


def build_program(tokc, debug=False, stage=99):
    assert tokc % 512 == 0
    ntile_a = tokc // 512
    nsup = 8 if ntile_a % 8 == 0 else (4 if ntile_a % 4 == 0 else (2 if ntile_a % 2 == 0 else 1))
    ta_per_sup = ntile_a // nsup
    sup_tok = tokc // nsup
    ncc = sup_tok // 128
    tb_per_sup = ncc // 4

    nc = bass.Bass("TRN2", target_bir_lowering=False)

    x_h = nc.dram_tensor("x", [tokc, D], F32, kind="ExternalInput")
    y_h = nc.dram_tensor("y", [tokc, D], F32, kind="ExternalOutput")
    wq_h = nc.dram_tensor("wq", [D, D], F32, kind="ExternalInput")
    wk_h = nc.dram_tensor("wk", [D, D], F32, kind="ExternalInput")
    wv_h = nc.dram_tensor("wv", [D, D], F32, kind="ExternalInput")
    wo_h = nc.dram_tensor("wo", [D, D], F32, kind="ExternalInput")
    bq_h = nc.dram_tensor("bq", [D], F32, kind="ExternalInput")
    bk_h = nc.dram_tensor("bk", [D], F32, kind="ExternalInput")
    bv_h = nc.dram_tensor("bv", [D], F32, kind="ExternalInput")
    bo_h = nc.dram_tensor("bo", [1, D], F32, kind="ExternalInput")
    w1_h = nc.dram_tensor("w1", [H4, D], F32, kind="ExternalInput")
    b1_h = nc.dram_tensor("b1", [H4], F32, kind="ExternalInput")
    w2_h = nc.dram_tensor("w2", [D, H4], F32, kind="ExternalInput")
    b2_h = nc.dram_tensor("b2", [1, D], F32, kind="ExternalInput")
    id_h = nc.dram_tensor("cident", [128, 128], BF16, kind="ExternalInput")
    msel_h = nc.dram_tensor("msel", [D, 16], BF16, kind="ExternalInput")
    mselT_h = nc.dram_tensor("mselT", [16, D], BF16, kind="ExternalInput")
    onesp_h = nc.dram_tensor("onesp", [16, 128], BF16, kind="ExternalInput")
    wqb_h = nc.dram_tensor("wqb", [D, D], BF16)
    wkb_h = nc.dram_tensor("wkb", [D, D], BF16)
    wvb_h = nc.dram_tensor("wvb", [D, D], BF16)
    wob_h = nc.dram_tensor("wob", [D, D], BF16)
    w1t_h = nc.dram_tensor("w1t", [D, H4], BF16)
    w2qt_h = nc.dram_tensor("w2qt", [H4, D], FP8)
    dbg = {}
    if debug:
        for nm, shp, dt in [("d_ln1fm", [128, 8 * 512], BF16),
                            ("d_qkfm", [128, 8 * 512], BF16),
                            ("d_qkv", [128, 8 * 512], BF16),
                            ("d_a", [128, 4 * D], F32),
                            ("d_u", [128, ncc * D], BF16),
                            ("d_h", [128, 512], F32),
                            ("d_hq", [128, 32 * 512], FP8),
                            ("d_o", [128, 4 * D], F32),
                            ("d_oq", [128, 4 * D], F32),
                            ("d_w2q", [128, H4], BF16),
                            ("d_pg", [128, 2048], F32),
                            ("d_pr", [128, 2048], F32),
                            ("d_psrc", [128, H4], F32),
                            ("d_xt", [128, 4 * D], F32),
                            ("d_stats", [128, 16], F32)]:
            dbg[nm] = nc.dram_tensor(nm, shp, dt, kind="ExternalOutput")

    def isz(dt):
        return np.dtype(mybir.dt.np(dt)).itemsize

    A = lambda name, shape, dt: nc.alloc_sbuf_tensor(name, shape, dt)
    ident = A("ident", [128, 128], BF16)
    msel = A("mselb", [128, 8 * 16], BF16)
    mselT = A("mselTb", [16, 8 * 128], BF16)
    onesp = A("onespb", [16, 128], BF16)
    bqc = A("bqc", [128, 8], F32)
    bkc = A("bkc", [128, 8], F32)
    bvc = A("bvc", [128, 8], F32)
    b1c = A("b1c", [128, 32], F32)
    bo_bc = A("bo_bc", [128, D], F32)
    b2q_bc = A("b2q_bc", [128, D], F32)
    cpool = A("cpool", [128, len(CONSTS)], F32)
    stats = A("stats", [128, 64], F32)
    U = A("U", [128, ncc * D], BF16)

    ARENA_BYTES = 190464
    r = nc.bump_sbuf(ARENA_BYTES, align_bytes=512)
    assert r is not None, "SBUF arena allocation failed"
    ar_lo = r[0]

    class Arena:
        def __init__(self, base):
            self.base, self.off = base, 0

        def al(self, name, shape, dt):
            n = int(np.prod(shape[1:])) * isz(dt)
            self.off = (self.off + 63) // 64 * 64
            t = nc.alloc_sbuf_tensor_at(name, shape, dt, offset=self.base + self.off)
            self.off += n
            assert self.off <= ARENA_BYTES, f"arena overflow at {name}: {self.off}"
            return t

    pa = Arena(ar_lo)
    p_stage = pa.al("p_stage", [128, H4], F32)
    p_stage2 = pa.al("p_stage2", [128, H4], F32)
    p_wbf = pa.al("p_wbf", [128, 32 * D], BF16)
    p_qb = [pa.al(f"p_q{i}", [128, 2048], F32) for i in range(5)]
    p_qi = pa.al("p_qi", [128, 2048], I32)
    p_tr8 = pa.al("p_tr8", [128, 2 * 8 * 128], FP8)
    p_trb = pa.al("p_trb", [128, 2 * 8 * 128], BF16)
    p_row = pa.al("p_row", [16, D], F32)
    p_rowb = pa.al("p_rowb", [16, D], BF16)
    p_w1s = pa.al("p_w1s", [128, 2 * H4], BF16)

    aa = Arena(ar_lo)
    WQ = aa.al("WQ", [128, 8 * D], BF16)
    WK = aa.al("WK", [128, 8 * D], BF16)
    WV = aa.al("WV", [128, 8 * D], BF16)
    WO = aa.al("WO", [128, 8 * D], BF16)
    Xt = aa.al("Xt", [128, 4 * D], F32)
    ln1_tok = aa.al("ln1_tok", [128, D], BF16)
    ln1_fm = aa.al("ln1_fm", [128, 8 * 512], BF16)
    sqq = aa.al("sqq", [128, 512], F32)
    sqk = aa.al("sqk", [128, 512], F32)
    sqs = aa.al("sqs", [128, 512], F32)
    Vfm = aa.al("Vfm", [128, 8 * 512], BF16)
    QKfm = aa.al("QKfm", [128, 8 * 512], BF16)
    qkvfm = aa.al("qkvfm", [128, 8 * 512], BF16)
    coefb = aa.al("coefb", [16, 512], BF16)
    Atok = aa.al("Atok", [128, 4 * D], F32)
    lnA = aa.al("lnA", [128, D], F32)

    ba = Arena(ar_lo)
    W1T = ba.al("W1T", [128, 8 * H4], BF16)
    W2QT = ba.al("W2QT", [128, 32 * D], FP8)
    ln2_fm = ba.al("ln2_fm", [128, 8 * 512], BF16)
    ln2_tok = ba.al("ln2_tok", [128, D], BF16)
    Hq = ba.al("Hq", [128, 32 * 512], FP8)
    Otok = ba.al("Otok", [128, 4 * D], F32)
    qsets = []
    for si in range(4):
        slots = {nm: ba.al(f"q{si}_{nm}", [128, 512], F32)
                 for nm in ["hv", "hc", "t64", "d2y", "g"]}
        slots["ii"] = ba.al(f"q{si}_ii", [128, 512], I32)
        qsets.append(slots)

    PT = nc.alloc_psum_tensor("psTR", [128, 1024], BF16)
    PS = [nc.alloc_psum_tensor(f"ps{i}", [128, 512], F32) for i in range(6)]
    PC = nc.alloc_psum_tensor("psC", [128, 512], F32)

    p = Prog(nc)
    for s in ["d", "g", "t", "v", "s", "o"]:
        p.mksem(s)
    for i, v in enumerate(CONSTS):
        nc.const_aps.register(F32, float(v), cpool[:, i:i + 1])

    def dma(q_, dst, src, inc, deps=(), **kw):
        p.wait(q_, *deps)
        return p.op(q_, lambda e, d=dst, s=src: e.dma_start(d, s, **kw),
                    inc=inc, amt=16)

    def ts1(e, out, in_, s1, o1):
        return e.tensor_single_scalar(out, in_, s1, o1)

    for i, v in enumerate(CONSTS):
        p.op("v", lambda e, i=i, v=v: e.memset(cpool[:, i:i + 1], float(v)))
    p.op("v", lambda e: e.memset(stats[:, :], 1.0), inc="v")

    dma("y", ident[:, :], id_h[:, :], "d")
    dma("y", msel.ap().rearrange("p (c h) -> p c h", h=16),
        msel_h.ap().rearrange("(c p) h -> p c h", p=128), "d")
    dma("y", mselT[:, :], mselT_h[:, :], "d")
    dma("y", onesp[:, :], onesp_h[:, :], "d")
    for dst, src in [(bqc, bq_h), (bkc, bk_h), (bvc, bv_h)]:
        p.op("y", lambda e, d=dst, s=src: e.dma_start(
            d[:, :], s.ap().rearrange("(c p) -> p c", p=128),
            allow_slow_non_contiguous=True), inc="d", amt=16)
    p.op("y", lambda e: e.dma_start(
        b1c[:, :], b1_h.ap().rearrange("(c p) -> p c", p=128),
        allow_slow_non_contiguous=True), inc="d", amt=16)

    def chains_stage1(lanes):
        for L in lanes:
            w = L["width"]; b = L["bufset"]
            for nm in ["hv", "hc", "t64", "d2y", "g"]:
                L[nm] = b[nm][:, :w]
            L["ii"] = b["ii"][:, :w]
            L["sl"] = 2.0 * LN2 if L["signed"] else LN2
        for L in lanes:
            if L.get("relu_bias") is not None:
                p.wait("s", L.get("psum_tok"), L.get("war_v"))
                L["tk1"] = p.op("s", lambda e, L=L: e.activation(
                    L["hv"], L["psum_src"], AF.Relu, bias=L["relu_bias"],
                    scale=1.0), inc="s")
                L["xs"] = L["hv"]
            elif L["signed"]:
                p.wait("s", L.get("src_tok"), L.get("war_v"))
                L["tk1"] = p.op("s", lambda e, L=L: e.activation(
                    L["d2y"], L["xsrc"], AF.Square, bias=0.0,
                    scale=2.0 ** 16), inc="s")
                L["xs"] = L["xsrc"]
            else:
                L["tk1"] = None
                L["xs"] = L["xsrc"]
        for L in lanes:
            if L.get("relu_bias") is not None:
                p.wait("v", L["tk1"], L.get("war_s"))
                L["tk2"] = p.op("v", lambda e, L=L: ts1(
                    e, L["hc"], L["hv"], L["clamp_lo"], OP.max), inc="v")
            elif L["signed"]:
                csq = float(L["clamp_lo"]) ** 2 * 2.0 ** 32
                p.wait("v", L["tk1"], L.get("war_s"))
                L["tk2"] = p.op("v", lambda e, L=L, c=csq: ts1(
                    e, L["hc"], L["d2y"], c, OP.max), inc="v")
            else:
                p.wait("v", L.get("src_tok"), L.get("war_s"))
                L["tk2"] = p.op("v", lambda e, L=L: ts1(
                    e, L["hc"], L["xsrc"], L["clamp_lo"], OP.max), inc="v")
        for L in lanes:
            p.wait("s", L["tk2"], L.get("war_v") if L["tk1"] is None else None)
            sc = 1.0 if L["signed"] else 2.0 ** 32
            p.op("s", lambda e, L=L, sc=sc: e.activation(
                L["t64"], L["hc"], AF.Ln, bias=0.0, scale=sc), inc="s")
        for L in lanes:
            L["tk4"] = p.op("s", lambda e, L=L: e.activation(
                L["ii"], L["t64"], AF.Copy, bias=0.0, scale=1.0 / L["sl"]), inc="s")
        for L in lanes:
            p.wait("v", L["tk4"])
            p.op("v", lambda e, L=L: ts1(e, L["hc"], L["ii"], L["sl"], OP.mult), inc="v")
        for L in lanes:
            p.op("v", lambda e, L=L: e.tensor_tensor(
                L["d2y"], L["hc"], L["t64"], OP.is_gt), inc="v")
        for L in lanes:
            L["tk7"] = p.op("v", lambda e, L=L: e.tensor_tensor(
                L["hc"], L["ii"], L["d2y"], OP.subtract), inc="v")
        for L in lanes:
            p.wait("s", L["tk7"])
            p.op("s", lambda e, L=L: e.activation(
                L["t64"], L["hc"], AF.Exp, bias=L["bias_p"], scale=-LN2), inc="s")
        for L in lanes:
            L["g_tok"] = p.op("s", lambda e, L=L: e.activation(
                L["g"], L["hc"], AF.Exp, bias=L["bias_m"], scale=LN2), inc="s")
        return lanes

    def chains_stage2(lanes):
        for L in lanes:
            p.wait("v", L["g_tok"])
            L["tk10"] = p.op("v", lambda e, L=L: e.tensor_tensor(
                L["d2y"], L["xs"], L["t64"], OP.mult), inc="v")
        for L in lanes:
            p.wait("s", L["tk10"])
            p.op("s", lambda e, L=L: e.activation(
                L["ii"], L["d2y"], AF.Copy, bias=L["off_c"], scale=1.0), inc="s")
        for L in lanes:
            L["end_s"] = p.op("s", lambda e, L=L: e.activation(
                L["t64"], L["d2y"], AF.Copy, bias=L["off_c"], scale=1.0), inc="s")
        for L in lanes:
            p.wait("v", L["end_s"])
            p.op("v", lambda e, L=L: e.tensor_tensor(
                L["hc"], L["ii"], L["t64"], OP.is_gt), inc="v")
        for L in lanes:
            p.op("v", lambda e, L=L: e.tensor_tensor(
                L["t64"], L["ii"], L["hc"], OP.subtract), inc="v")
        for L in lanes:
            if L["off_sub"]:
                L["end_v"] = p.op("v", lambda e, L=L: e.scalar_tensor_tensor(
                    L["dst"], L["t64"], -float(L["off_sub"]), L["g"],
                    OP.add, OP.mult), inc="v")
            else:
                L["end_v"] = p.op("v", lambda e, L=L: e.tensor_tensor(
                    L["dst"], L["t64"], L["g"], OP.mult), inc="v")
        return lanes

    def quant_chain(width, xsrc, dst_ap, bias_p, bias_m, off_c, off_sub,
                    clamp_lo, signed, bufset, psum_tok=None, psum_src=None,
                    relu_bias=None, war_v=None, war_s=None, src_tok=None):
        L = dict(width=width, bufset=bufset, signed=signed, bias_p=bias_p,
                 bias_m=bias_m, clamp_lo=clamp_lo, war_v=war_v, war_s=war_s,
                 xsrc=xsrc, src_tok=src_tok, psum_src=psum_src,
                 psum_tok=psum_tok, relu_bias=relu_bias,
                 dst=dst_ap, off_c=off_c, off_sub=off_sub)
        chains_stage1([L])
        chains_stage2([L])
        return {"relu": L["tk1"], "end_v": L["end_v"], "end_s": L["end_s"]}

    def layer_norm_stats(idx_list, src, dump, src_tok=None, prev=None,
                         dump_war=None):
        n = len(idx_list)
        prev = prev or {}
        acc_tok = None
        for i, cidx in enumerate(idx_list):
            sl = src[:, cidx * D:(cidx + 1) * D]
            p.wait("v", src_tok if i == 0 else None,
                   prev.get("rstd") if i == 0 else None)
            p.op("v", lambda e, sl=sl, i=i: e.reduce_sum(
                stats[:, i:i + 1], sl, axis=mybir.AxisListType.X), inc="v")
            p.wait("s", src_tok if i == 0 else None,
                   prev.get("var") if i == 0 else None,
                   dump_war if i == 0 else None)
            acc_tok = p.op("s", lambda e, sl=sl, i=i: e.activation(
                dump[:, :], sl, AF.Square, bias=0.0, scale=1.0,
                accum_out=stats[:, 16 + i:17 + i]), inc="s")
        p.wait("v", acc_tok)
        m_tok = p.op("v", lambda e, n=n: ts1(e, stats[:, 0:n], stats[:, 0:n],
                                             1.0 / D, OP.mult), inc="v")
        p.op("v", lambda e, n=n: ts1(e, stats[:, 16:16 + n], stats[:, 16:16 + n],
                                     1.0 / D, OP.mult), inc="v")
        p.wait("s", m_tok)
        msq_tok = p.op("s", lambda e, n=n: e.activation(
            stats[:, 48:48 + n], stats[:, 0:n], AF.Square, bias=0.0,
            scale=1.0), inc="s")
        p.wait("v", msq_tok)
        var_tok = p.op("v", lambda e, n=n: e.tensor_tensor(
            stats[:, 16:16 + n], stats[:, 16:16 + n], stats[:, 48:48 + n],
            OP.subtract), inc="v")
        p.wait("s", var_tok)
        p.op("s", lambda e, n=n: e.activation(stats[:, 48:48 + n], stats[:, 16:16 + n],
                                              AF.Ln, bias=C_EPS, scale=1.0), inc="s")
        rstd_tok = p.op("s", lambda e, n=n: e.activation(
            stats[:, 32:32 + n], stats[:, 48:48 + n], AF.Exp, bias=0.0,
            scale=-0.5), inc="s")
        return {"rstd": rstd_tok, "var": var_tok}

    for wsrc, wdst in [(wq_h, wqb_h), (wk_h, wkb_h), (wv_h, wvb_h), (wo_h, wob_h)]:
        for c in range(4):
            dma("g", wdst[c * 256:(c + 1) * 256, :], wsrc[c * 256:(c + 1) * 256, :], "g")
    w1_ld, w1_ev, w1_st = {}, {}, {}

    def w1_prefetch(dc):
        for hg in range(4):
            gidx = dc * 4 + hg
            slot = gidx % 8
            tok = None
            for j in range(8):
                hcn = hg * 8 + j
                tok = dma("g", p_w1s[:, slot * 1024 + j * 128: slot * 1024 + (j + 1) * 128],
                          w1_h[hcn * 128:(hcn + 1) * 128, dc * 128:(dc + 1) * 128],
                          "g", deps=[w1_ev.get(gidx - 8)] if j == 0 else ())
            w1_ld[gidx] = tok

    w1_prefetch(0)
    w1_prefetch(1)
    if stage < 1:
        return _finish(nc, p)
    p.op("v", lambda e: e.memset(p_row[:, :], 0.0), inc="v")
    tok_bo = dma("y", p_row[0:1, :], bo_h[:, :], "d", deps=["v"])
    p.wait("v", tok_bo)
    p.op("v", lambda e: e.tensor_copy(p_rowb[:, :], p_row[:, :]), inc="v")
    p.wait("t", "v")
    for half in range(2):
        p.op("t", lambda e, half=half: e.matmul(
            PS[0][:, :], onesp[:, :], p_rowb[:, half * 512:(half + 1) * 512],
            start=True, stop=True), inc="t")
        p.wait("v", "t")
        p.op("v", lambda e, half=half: e.tensor_copy(
            bo_bc[:, half * 512:(half + 1) * 512], PS[0][:, :]), inc="v")
        p.wait("t", "v")
    tok_b2 = dma("y", p_row[0:1, :], b2_h[:, :], "d", deps=["v"])
    p.wait("v", tok_b2)
    p.op("v", lambda e: e.tensor_copy(p_rowb[:, :], p_row[:, :]), inc="v")
    p.wait("t", "v")
    for half in range(2):
        p.op("t", lambda e, half=half: e.matmul(
            PS[0][:, :], onesp[:, :], p_rowb[:, half * 512:(half + 1) * 512],
            start=True, stop=True), inc="t")
        p.wait("v", "t")
        p.op("v", lambda e, half=half: e.tensor_copy(
            p_stage[:, half * 512:(half + 1) * 512], PS[0][:, :]), inc="v")
        p.wait("t", "v")
    pset = {nm: p_qb[k] for k, nm in enumerate(["hv", "hc", "t64", "d2y", "g"])}
    pset["ii"] = p_qi
    quant_chain(D, p_stage[:, :D], b2q_bc[:, :], C_P26L2, C_M26L2, C_4096_5, 4096,
                2.0 ** -14, signed=True, bufset=pset,
                war_v=p.latest("v"), war_s=p.latest("s"), src_tok=p.latest("v"))
    if stage < 2:
        return _finish(nc, p)
    prep_sets = []
    for si in range(4):
        ps_ = {nm: p_qb[k].ap()[:, si * 512:(si + 1) * 512]
               for k, nm in enumerate(["hv", "hc", "t64", "d2y", "g"])}
        ps_["ii"] = p_qi.ap()[:, si * 512:(si + 1) * 512]
        prep_sets.append(ps_)
    prep_hist = {0: None, 1: None, 2: None, 3: None}
    st8_tok = {}

    def w2_transposes(dc):
        p.wait("t", prep_hist[2]["end_v"], prep_hist[3]["end_v"])
        for hg in range(4):
            gidx = dc * 4 + hg
            buf = gidx % 2
            tr_tok = None
            for j in range(8):
                hcn = hg * 8 + j
                src = p_wbf[:, dc * H4 + hcn * 128: dc * H4 + (hcn + 1) * 128]
                tr_tok = p.op("t", lambda e, src=src, j=j: e.transpose(
                    PT[:, j * 128:(j + 1) * 128], src, ident[:, :]),
                    inc="t" if j == 7 else None)
            p.wait("v", tr_tok, st8_tok.get(gidx - 2))
            ev = p.op("v", lambda e, buf=buf: e.tensor_copy(
                p_tr8[:, buf * 1024:(buf + 1) * 1024], PT[:, :]), inc="v")
            p.wait("t", ev)
            for j in range(8):
                hcn = hg * 8 + j
                st8_tok[gidx] = dma(
                    "y", w2qt_h[hcn * 128:(hcn + 1) * 128, dc * 128:(dc + 1) * 128],
                    p_tr8[:, buf * 1024 + j * 128: buf * 1024 + (j + 1) * 128],
                    "d", deps=[ev] if j == 0 else ())
            slot = gidx % 8
            p.wait("t", w1_ld[gidx])
            trb = None
            for j in range(8):
                src = p_w1s[:, slot * 1024 + j * 128: slot * 1024 + (j + 1) * 128]
                trb = p.op("t", lambda e, src=src, j=j: e.transpose(
                    PT[:, j * 128:(j + 1) * 128], src, ident[:, :]),
                    inc="t" if j == 7 else None)
            p.wait("v", trb, w1_st.get(gidx - 2))
            evb = p.op("v", lambda e, buf=buf: e.tensor_copy(
                p_trb[:, buf * 1024:(buf + 1) * 1024], PT[:, :]), inc="v")
            w1_ev[gidx] = evb
            p.wait("t", evb)
            p.wait("g", evb)
            w1_st[gidx] = dma("g", w1t_h[dc * 128:(dc + 1) * 128,
                                         hg * 1024:(hg + 1) * 1024],
                              p_trb[:, buf * 1024:(buf + 1) * 1024], "g")
        if 2 <= dc + 1 <= 7:
            w1_prefetch(dc + 1)

    stg = [p_stage, p_stage2]
    dtoks = {0: dma("y", p_stage[:, :], w2_h[0:128, :], "d", deps=["v", "s"]),
             1: dma("y", p_stage2[:, :], w2_h[128:256, :], "d")}
    dc_flush = {}
    for dc in range(8):
        if 1 <= dc <= 6:
            dtoks[dc + 1] = dma("y", stg[(dc + 1) % 2][:, :],
                                w2_h[(dc + 1) * 128:(dc + 2) * 128, :], "d",
                                deps=dc_flush[dc - 1])
        pend = None
        for m2 in range(4):
            lanes = []
            for j in range(2):
                sl8 = 2 * m2 + j
                si = 2 * (m2 % 2) + j
                hist = prep_hist[si]
                lanes.append(dict(
                    width=512, bufset=prep_sets[si], signed=True,
                    bias_p=C_P19L2, bias_m=C_M19L2, clamp_lo=2.0 ** -6,
                    xsrc=stg[dc % 2][:, sl8 * 512:(sl8 + 1) * 512],
                    src_tok=dtoks[dc],
                    war_v=hist["end_v"] if hist else None,
                    war_s=hist["end_s"] if hist else None,
                    dst=p_wbf[:, dc * H4 + sl8 * 512: dc * H4 + (sl8 + 1) * 512],
                    off_c=C_64_5, off_sub=64, si=si))
            chains_stage1(lanes)
            if pend is not None:
                chains_stage2(pend)
                for L in pend:
                    prep_hist[L["si"]] = {"end_v": L["end_v"], "end_s": L["end_s"]}
            if dc > 0 and m2 == 1:
                w2_transposes(dc - 1)
            pend = lanes
        chains_stage2(pend)
        for L in pend:
            prep_hist[L["si"]] = {"end_v": L["end_v"], "end_s": L["end_s"]}
        dc_flush[dc] = [prep_hist[2]["end_v"], prep_hist[3]["end_v"],
                        prep_hist[2]["end_s"], prep_hist[3]["end_s"]]
    w2_transposes(7)
    if debug:
        p.wait("y", "v")
        dma("y", dbg["d_w2q"][:, :], p_wbf[:, 0:H4], "o")
        dma("y", dbg["d_pg"][:, :], p_qb[4][:, :], "o")
        dma("y", dbg["d_pr"][:, :], p_qb[2][:, :], "o")
        dma("y", dbg["d_psrc"][:, :], p_stage[:, :], "o")
    if stage < 3:
        return _finish(nc, p)

    if stage < 4:
        return _finish(nc, p)
    stats_prev = None
    tok_u_ready = None
    for sup in range(nsup):
        base_tok = sup * sup_tok
        p.barrier_on("y")
        for Wd, Ws in [(WQ, wqb_h), (WK, wkb_h), (WV, wvb_h), (WO, wob_h)]:
            for c in range(8):
                dma("y", Wd[:, c * D:(c + 1) * D], Ws[c * 128:(c + 1) * 128, :], "d")
        tok_wload = p.latest("d")
        tok_qkv_last = None
        tok_wo_last = None
        tok_coef_last = None
        tok_bc_last = None
        tok_uadd = [None] * 4
        for lt in range(ta_per_sup):
            t0 = base_tok + lt * 512
            xt_tok = []
            for tc in range(4):
                xt_tok.append(dma("y", Xt[:, tc * D:(tc + 1) * D],
                                  x_h[t0 + tc * 128:t0 + tc * 128 + 128, :], "d",
                                  deps=["v", "s"]))
            stats_prev = layer_norm_stats([0, 1, 2, 3], Xt, lnA,
                                          src_tok=xt_tok[3], prev=stats_prev,
                                          dump_war=tok_uadd[3])
            rstd_tok = stats_prev["rstd"]
            tr_tok = None
            ev_toks = []
            for tc in range(4):
                p.wait("v", rstd_tok, tok_qkv_last)
                ap_tok = p.op("v", lambda e, tc=tc: e.tensor_scalar(
                    ln1_tok[:, :], Xt[:, tc * D:(tc + 1) * D],
                    stats[:, tc:tc + 1], stats[:, 32 + tc:33 + tc],
                    OP.subtract, OP.mult), inc="v")
                p.wait("t", ap_tok, ev_toks[-1] if ev_toks else None)
                for fc in range(8):
                    tr_tok = p.op("t", lambda e, fc=fc: e.transpose(
                        PT[:, fc * 128:(fc + 1) * 128],
                        ln1_tok[:, fc * 128:(fc + 1) * 128], ident[:, :]),
                        inc="t" if fc == 7 else None)
                p.wait("v", tr_tok)
                ev_toks.append(p.op("v", lambda e, tc=tc: e.tensor_copy(
                    _fmcols(ln1_fm, 512, tc), PT[:, :]), inc="v"))
            if debug:
                dma("y", dbg["d_ln1fm"][:, :], ln1_fm[:, :], "o", deps=[ev_toks[-1]])
            rd = {}
            exp_tok = {}
            vfm_tok = {}
            sqs_tok = None
            for oc in range(8):
                par = oc % 2
                psq, psk, psv = PS[3 * par], PS[3 * par + 1], PS[3 * par + 2]
                if oc >= 2:
                    p.wait("t", *rd[oc - 2])
                else:
                    p.wait("t", ev_toks[-1], tok_wload,
                           tok_wo_last if sup == 0 else None)
                mm_toks = []
                for W_, psx in [(WQ, psq), (WK, psk), (WV, psv)]:
                    for fc in range(8):
                        lhsT = W_[:, fc * D + oc * 128: fc * D + (oc + 1) * 128]
                        rhs = ln1_fm[:, fc * 512:(fc + 1) * 512]
                        tk = p.op("t", lambda e, lhsT=lhsT, rhs=rhs, psx=psx, fc=fc: e.matmul(
                            psx[:, :], lhsT, rhs, start=(fc == 0), stop=(fc == 7)),
                            inc="t" if fc == 7 else None)
                        if tk:
                            mm_toks.append(tk)
                tok_qkv_last = mm_toks[-1]
                p.wait("v", mm_toks[2])
                vfm_tok[oc] = p.op("v", lambda e, oc=oc, psv=psv: ts1(
                    e, Vfm[:, oc * 512:(oc + 1) * 512], psv[:, :],
                    bvc[:, oc:oc + 1], OP.add), inc="v")
                p.wait("s", mm_toks[0], sqs_tok)
                p.op("s", lambda e, oc=oc, psq=psq: e.activation(
                    sqq[:, :], psq[:, :], AF.Square, bias=bqc[:, oc:oc + 1],
                    scale=1.0), inc="s")
                p.wait("s", mm_toks[1])
                sq_tok = p.op("s", lambda e, oc=oc, psk=psk: e.activation(
                    sqk[:, :], psk[:, :], AF.Square, bias=bkc[:, oc:oc + 1],
                    scale=1.0), inc="s")
                p.wait("v", sq_tok, exp_tok.get(oc - 1), tok_coef_last if oc == 0 else None)
                sqs_tok = p.op("v", lambda e: e.tensor_add(sqs[:, :], sqq[:, :],
                                                           sqk[:, :]), inc="v")
                p.wait("s", sqs_tok)
                exp_tok[oc] = p.op("s", lambda e, oc=oc: e.activation(
                    QKfm[:, oc * 512:(oc + 1) * 512], sqs[:, :], AF.Exp,
                    bias=0.0, scale=-0.5), inc="s")
                rd[oc] = (vfm_tok[oc], sq_tok)
            if debug:
                dma("y", dbg["d_qkfm"][:, :], QKfm[:, :], "o", deps=[exp_tok[7]])
            p.wait("t", tok_bc_last)
            for oc in range(8):
                p.wait("t", exp_tok[oc])
                tok_coef_last = p.op("t", lambda e, oc=oc: e.matmul(
                    PC[0:16, :], msel[:, oc * 16:(oc + 1) * 16],
                    QKfm[:, oc * 512:(oc + 1) * 512],
                    start=(oc == 0), stop=(oc == 7)),
                    inc="t" if oc == 7 else None)
            p.wait("v", tok_coef_last)
            coefb_tok = p.op("v", lambda e: e.tensor_copy(coefb[:, :], PC[0:16, :]), inc="v")
            stt_toks = {}
            for oc in range(8):
                bank = PS[3] if oc % 2 == 0 else PC
                p.wait("t", coefb_tok if oc < 2 else stt_toks[oc - 2],
                       rd[7][1] if oc == 0 else None)
                bc_tok = p.op("t", lambda e, oc=oc, bank=bank: e.matmul(
                    bank[0:128, :], mselT[:, oc * 128:(oc + 1) * 128],
                    coefb[:, :], start=True, stop=True), inc="t")
                p.wait("v", bc_tok, tok_wo_last if oc == 0 else None)
                stt_toks[oc] = p.op("v", lambda e, oc=oc, bank=bank: e.scalar_tensor_tensor(
                    qkvfm[:, oc * 512:(oc + 1) * 512], bank[0:128, :], 1.0,
                    Vfm[:, oc * 512:(oc + 1) * 512], OP.bypass, OP.mult), inc="v")
            tok_bc_last_list = []
            ev2 = {}
            for tp in range(4):
                for oh in range(2):
                    pse = PS[4 + oh]
                    p.wait("t", stt_toks[7], ev2.get(tp - 1),
                           rd[7][1] if tp == 0 else None,
                           rd[7][0] if tp == 0 else None)
                    for fc in range(8):
                        lhsT = qkvfm[:, fc * 512 + tp * 128: fc * 512 + (tp + 1) * 128]
                        rhs = WO[:, fc * D + oh * 512: fc * D + (oh + 1) * 512]
                        tok_wo_last = p.op("t", lambda e, lhsT=lhsT, rhs=rhs, pse=pse, fc=fc: e.matmul(
                            pse[:, :], lhsT, rhs, start=(fc == 0), stop=(fc == 7)),
                            inc="t" if fc == 7 else None)
                    p.wait("v", tok_wo_last, tok_uadd[tp])
                    ev2[tp] = p.op("v", lambda e, tp=tp, oh=oh, pse=pse: e.tensor_tensor(
                        Atok[:, tp * D + oh * 512: tp * D + (oh + 1) * 512],
                        pse[:, :], bo_bc[:, oh * 512:(oh + 1) * 512],
                        OP.add), inc="v")
            if debug:
                dma("y", dbg["d_a"][:, :], Atok[:, :], "o", deps=[ev2[3]])
            stats_prev = layer_norm_stats([0, 1, 2, 3], Atok, lnA,
                                          src_tok=ev2[3], prev=stats_prev)
            rstd_tok = stats_prev["rstd"]
            for tc in range(4):
                cc = lt * 4 + tc
                p.wait("v", rstd_tok)
                p.op("v", lambda e, tc=tc: e.tensor_scalar(
                    lnA[:, :], Atok[:, tc * D:(tc + 1) * D],
                    stats[:, tc:tc + 1], stats[:, 32 + tc:33 + tc],
                    OP.subtract, OP.mult), inc="v")
                tok_uadd[tc] = p.op("v", lambda e, tc=tc, cc=cc: e.tensor_tensor(
                    U[:, cc * D:(cc + 1) * D], Xt[:, tc * D:(tc + 1) * D],
                    lnA[:, :], OP.add), inc="v")
                tok_u_ready = tok_uadd[tc]
        if debug:
            dma("y", dbg["d_u"][:, :], U[:, :], "o", deps=[tok_uadd[3]])
        if stage < 5:
            continue
        p.barrier_on("y")
        for c in range(8):
            dma("y", W1T[:, c * H4:(c + 1) * H4], w1t_h[c * 128:(c + 1) * 128, :], "d")
        for c in range(32):
            dma("y", W2QT[:, c * D:(c + 1) * D], w2qt_h[c * 128:(c + 1) * 128, :], "d")
        tok_wload = p.latest("d")
        tok_b2_last = None
        tok_mm1_last = None
        tok_ost = {}
        tr_tok = None
        chain_hist = {0: None, 1: None, 2: None, 3: None}
        for bt in range(tb_per_sup):
            cc0 = bt * 4
            stats_prev = layer_norm_stats([cc0, cc0 + 1, cc0 + 2, cc0 + 3], U,
                                          ln2_tok, src_tok=tok_u_ready,
                                          prev=stats_prev, dump_war=tr_tok)
            rstd_tok = stats_prev["rstd"]
            ev_toks = []
            for k in range(4):
                p.wait("v", rstd_tok, tok_mm1_last)
                ap_tok = p.op("v", lambda e, k=k, cc0=cc0: e.tensor_scalar(
                    ln2_tok[:, :], U[:, (cc0 + k) * D:(cc0 + k + 1) * D],
                    stats[:, k:k + 1], stats[:, 32 + k:33 + k],
                    OP.subtract, OP.mult), inc="v")
                p.wait("t", ap_tok, ev_toks[-1] if ev_toks else None,
                       tok_b2_last if bt > 0 and k == 0 else None)
                for fc in range(8):
                    tr_tok = p.op("t", lambda e, fc=fc: e.transpose(
                        PT[:, fc * 128:(fc + 1) * 128],
                        ln2_tok[:, fc * 128:(fc + 1) * 128], ident[:, :]),
                        inc="t" if fc == 7 else None)
                p.wait("v", tr_tok)
                ev_toks.append(p.op("v", lambda e, k=k: e.tensor_copy(
                    _fmcols(ln2_fm, 512, k), PT[:, :]), inc="v"))
            relu_toks = {}
            pend = None
            last_v = None
            for m_ in range(16):
                lanes = []
                for j in range(2):
                    hcn = 2 * m_ + j
                    psh = PS[hcn % 4]
                    p.wait("t", ev_toks[-1] if hcn == 0 else None,
                           tok_wload if hcn == 0 else None,
                           relu_toks.get(hcn - 4))
                    mmtok = None
                    for dc in range(8):
                        lhsT = W1T[:, dc * H4 + hcn * 128: dc * H4 + (hcn + 1) * 128]
                        rhs = ln2_fm[:, dc * 512:(dc + 1) * 512]
                        mmtok = p.op("t", lambda e, lhsT=lhsT, rhs=rhs, psh=psh, dc=dc: e.matmul(
                            psh[:, :], lhsT, rhs, start=(dc == 0), stop=(dc == 7)),
                            inc="t" if dc == 7 else None)
                    tok_mm1_last = mmtok
                    si = 2 * (m_ % 2) + j
                    hist = chain_hist[si]
                    lanes.append(dict(
                        width=512, bufset=qsets[si], signed=False,
                        bias_p=C_P35L2, bias_m=C_M35L2, clamp_lo=2.0 ** -6,
                        psum_src=psh[:, :], psum_tok=mmtok,
                        relu_bias=b1c[:, hcn:hcn + 1],
                        war_v=hist["end_v"] if hist else None,
                        war_s=hist["end_s"] if hist else None,
                        dst=Hq[:, hcn * 512:(hcn + 1) * 512],
                        off_c=C_HALF, off_sub=0, hcn=hcn, si=si))
                chains_stage1(lanes)
                for L in lanes:
                    relu_toks[L["hcn"]] = L["tk1"]
                if pend is not None:
                    chains_stage2(pend)
                    for L in pend:
                        chain_hist[L["si"]] = {"end_v": L["end_v"], "end_s": L["end_s"]}
                    last_v = pend[-1]["end_v"]
                pend = lanes
            chains_stage2(pend)
            for L in pend:
                chain_hist[L["si"]] = {"end_v": L["end_v"], "end_s": L["end_s"]}
            last_v = pend[-1]["end_v"]
            if debug and bt == 0:
                dma("y", dbg["d_h"][:, :], qsets[1]["hv"][:, :512], "o",
                    deps=[last_v])
                dma("y", dbg["d_hq"][:, :], Hq[:, :], "o", deps=[last_v])
            ev3 = {}
            gi = 0
            for tp in range(4):
                for dh in range(2):
                    pso = PS[2 + gi % 4]
                    if gi == 0:
                        p.wait("t", last_v, relu_toks[30])
                    elif gi == 1:
                        p.wait("t", relu_toks[31])
                    elif gi >= 4:
                        p.wait("t", ev3[gi - 4])
                    for hcn in range(32):
                        lhsT = Hq[:, hcn * 512 + tp * 128: hcn * 512 + (tp + 1) * 128]
                        rhs = W2QT[:, hcn * D + dh * 512: hcn * D + (dh + 1) * 512]
                        tok_b2_last = p.op("t", lambda e, lhsT=lhsT, rhs=rhs, pso=pso, hcn=hcn: e.matmul(
                            pso[:, :], lhsT, rhs, start=(hcn == 0), stop=(hcn == 31)),
                            inc="t" if hcn == 31 else None)
                    p.wait("v", tok_b2_last, tok_ost.get(gi // 2))
                    ev3[gi] = p.op("v", lambda e, tp=tp, dh=dh, pso=pso: e.tensor_tensor(
                        Otok[:, tp * D + dh * 512: tp * D + (dh + 1) * 512],
                        pso[:, :], b2q_bc[:, dh * 512:(dh + 1) * 512],
                        OP.add), inc="v")
                    gi += 1
            if debug and bt == 0:
                dma("y", dbg["d_o"][:, :], Otok[:, :], "o", deps=[ev3[7]])
            pend = None
            st2last = None
            for m_ in range(4):
                lanes = []
                for j in range(2):
                    seg = 2 * m_ + j
                    si = 2 * (m_ % 2) + j
                    hist = chain_hist[si]
                    lanes.append(dict(
                        width=512, bufset=qsets[si], signed=True,
                        bias_p=C_P26L2, bias_m=C_M26L2, clamp_lo=2.0 ** -14,
                        xsrc=Otok[:, seg * 512:(seg + 1) * 512],
                        src_tok=ev3[seg],
                        war_v=hist["end_v"] if hist else None,
                        war_s=hist["end_s"] if hist else None,
                        dst=Otok[:, seg * 512:(seg + 1) * 512],
                        off_c=C_4096_5, off_sub=4096, si=si))
                chains_stage1(lanes)
                if pend is not None:
                    chains_stage2(pend)
                    for L in pend:
                        chain_hist[L["si"]] = {"end_v": L["end_v"], "end_s": L["end_s"]}
                pend = lanes
            chains_stage2(pend)
            for L in pend:
                chain_hist[L["si"]] = {"end_v": L["end_v"], "end_s": L["end_s"]}
            st2last = pend[-1]["end_v"]
            if debug and bt == 0:
                dma("y", dbg["d_oq"][:, :], Otok[:, :], "o", deps=[st2last])
            stats_prev = layer_norm_stats([0, 1, 2, 3], Otok, ln2_tok,
                                          src_tok=st2last, prev=stats_prev,
                                          dump_war=tr_tok)
            rstd_tok = stats_prev["rstd"]
            for k in range(4):
                cc = cc0 + k
                p.wait("v", rstd_tok, tok_ost.get(k))
                p.op("v", lambda e, k=k: e.tensor_scalar(
                    Otok[:, k * D:(k + 1) * D], Otok[:, k * D:(k + 1) * D],
                    stats[:, k:k + 1], stats[:, 32 + k:33 + k],
                    OP.subtract, OP.mult), inc="v")
                ytok = p.op("v", lambda e, k=k, cc=cc: e.tensor_tensor(
                    Otok[:, k * D:(k + 1) * D], U[:, cc * D:(cc + 1) * D],
                    Otok[:, k * D:(k + 1) * D], OP.add), inc="v")
                r0 = base_tok + cc * 128
                tok_ost[k] = dma("y", y_h[r0:r0 + 128, :], Otok[:, k * D:(k + 1) * D],
                                 "o", deps=[ytok])

    p.wait("y", "o")
    return _finish(nc, p)


def _finish(nc, p):
    p.barrier_on("y")
    with nc.Block() as block:
        block.tensor(lambda e: [fn(e) for fn in p.q["t"]])
        block.vector(lambda e: [fn(e) for fn in p.q["v"]])
        block.scalar(lambda e: [fn(e) for fn in p.q["s"]])
        block.sync(lambda e: [fn(e) for fn in p.q["y"]])
        block.gpsimd(lambda e: [fn(e) for fn in p.q["g"]])
    return nc


def _make_const_inputs():
    hd = 64
    msel = np.zeros((D, 16), dtype=np.float32)
    for d_ in range(D):
        msel[d_, d_ // hd] = 1.0
    onesp = np.zeros((16, 128), dtype=np.float32)
    onesp[0, :] = 1.0
    return {
        "cident": np.eye(128, dtype=np.float32).astype(ml_dtypes.bfloat16),
        "msel": msel.astype(ml_dtypes.bfloat16),
        "mselT": msel.T.copy().astype(ml_dtypes.bfloat16),
        "onesp": onesp.astype(ml_dtypes.bfloat16),
    }


def kernel(L0_attn_x, L0_attn_Wq, L0_attn_bq, L0_attn_Wk, L0_attn_bk,
           L0_attn_Wv, L0_attn_bv, L0_attn_Wo, L0_attn_bo,
           L0_ffn_w1, L0_ffn_b1, L0_ffn_w2, L0_ffn_b2):
    x = np.ascontiguousarray(np.asarray(L0_attn_x, dtype=np.float32))
    B, S, Dm = x.shape
    tok = B * S
    assert Dm == D and tok % NCORES == 0
    tokc = tok // NCORES
    xf = x.reshape(tok, D)

    nc = build_program(tokc)
    consts = _make_const_inputs()
    common = {
        "wq": np.ascontiguousarray(L0_attn_Wq, dtype=np.float32),
        "wk": np.ascontiguousarray(L0_attn_Wk, dtype=np.float32),
        "wv": np.ascontiguousarray(L0_attn_Wv, dtype=np.float32),
        "wo": np.ascontiguousarray(L0_attn_Wo, dtype=np.float32),
        "bq": np.ascontiguousarray(L0_attn_bq, dtype=np.float32),
        "bk": np.ascontiguousarray(L0_attn_bk, dtype=np.float32),
        "bv": np.ascontiguousarray(L0_attn_bv, dtype=np.float32),
        "bo": np.ascontiguousarray(L0_attn_bo, dtype=np.float32).reshape(1, D),
        "w1": np.ascontiguousarray(L0_ffn_w1, dtype=np.float32),
        "b1": np.ascontiguousarray(L0_ffn_b1, dtype=np.float32),
        "w2": np.ascontiguousarray(L0_ffn_w2, dtype=np.float32),
        "b2": np.ascontiguousarray(L0_ffn_b2, dtype=np.float32).reshape(1, D),
        **consts,
    }
    in_maps = []
    for c in range(NCORES):
        m = dict(common)
        m["x"] = np.ascontiguousarray(xf[c * tokc:(c + 1) * tokc])
        in_maps.append(m)
    res = run_bass_kernel_spmd(nc, in_maps, core_ids=list(range(NCORES)))
    out = np.concatenate([r["y"] for r in res.results], axis=0)
    return out.reshape(B, S, D).astype(np.float32)


# version 09
# speedup vs baseline: 1.0737x; 1.0737x over previous
import os
import numpy as np
import ml_dtypes
import concourse.bass as bass
import concourse.mybir as mybir
from concourse.bass_utils import run_bass_kernel_spmd

F32 = mybir.dt.float32
BF16 = mybir.dt.bfloat16
FP8 = mybir.dt.float8e4
I32 = mybir.dt.int32
AF = mybir.ActivationFunctionType
OP = mybir.AluOpType

LN2 = 0.6931471805599453
D = 1024
H4 = 4096
NCORES = 8

C_EPS = 1e-5
C_HALF = 0.5
C_64_5 = 64.5
C_4096_5 = 4096.5
C_P35L2 = 35.0 * LN2
C_M35L2 = -35.0 * LN2
C_P19L2 = 19.0 * LN2
C_M19L2 = -19.0 * LN2
C_P26L2 = 26.0 * LN2
C_M26L2 = -26.0 * LN2
CONSTS = [C_EPS, C_HALF, C_64_5, C_4096_5, C_P35L2, C_M35L2,
          C_P19L2, C_M19L2, C_P26L2, C_M26L2]


class Prog:

    OWN = {"t": "t", "v": "v", "s": "s"}

    def __init__(self, nc):
        self.nc = nc
        self.q = {"t": [], "v": [], "s": [], "y": [], "g": []}
        self.sem = {}
        self.cnt = {}
        self.lastw = {e: {} for e in self.q}

    def mksem(self, name):
        self.sem[name] = self.nc.alloc_semaphore(name)
        self.cnt[name] = 0

    def latest(self, sem):
        return (sem, self.cnt[sem])

    def wait(self, eng, *deps):
        for dp in deps:
            if dp is None:
                continue
            if isinstance(dp, str):
                dp = (dp, self.cnt[dp])
            sem, c = dp
            if c <= 0:
                continue
            if self.OWN.get(eng) == sem:
                continue
            if self.lastw[eng].get(sem, 0) >= c:
                continue
            self.lastw[eng][sem] = c
            self.q[eng].append(lambda e, h=self.sem[sem], c=c: e.wait_ge(h, c))

    def op(self, eng, fn, inc=None, amt=1):
        if inc is None:
            self.q[eng].append(fn)
            return None
        h = self.sem[inc]
        self.cnt[inc] += amt
        self.q[eng].append(lambda e, fn=fn, h=h, a=amt: fn(e).then_inc(h, a))
        return (inc, self.cnt[inc])

    def barrier_on(self, eng):
        self.wait(eng, *list(self.sem.keys()))


def _fmcols(buf, w, tc):
    return buf.ap().rearrange("p (c w) -> p c w", w=w)[:, :, tc * 128:(tc + 1) * 128]


def build_program(tokc, debug=False, stage=99):
    assert tokc % 512 == 0
    ntile_a = tokc // 512
    nsup = 8 if ntile_a % 8 == 0 else (4 if ntile_a % 4 == 0 else (2 if ntile_a % 2 == 0 else 1))
    ta_per_sup = ntile_a // nsup
    sup_tok = tokc // nsup
    ncc = sup_tok // 128
    tb_per_sup = ncc // 4

    nc = bass.Bass("TRN2", target_bir_lowering=False)

    x_h = nc.dram_tensor("x", [tokc, D], F32, kind="ExternalInput")
    y_h = nc.dram_tensor("y", [tokc, D], F32, kind="ExternalOutput")
    wq_h = nc.dram_tensor("wq", [D, D], F32, kind="ExternalInput")
    wk_h = nc.dram_tensor("wk", [D, D], F32, kind="ExternalInput")
    wv_h = nc.dram_tensor("wv", [D, D], F32, kind="ExternalInput")
    wo_h = nc.dram_tensor("wo", [D, D], F32, kind="ExternalInput")
    bq_h = nc.dram_tensor("bq", [D], F32, kind="ExternalInput")
    bk_h = nc.dram_tensor("bk", [D], F32, kind="ExternalInput")
    bv_h = nc.dram_tensor("bv", [D], F32, kind="ExternalInput")
    bo_h = nc.dram_tensor("bo", [1, D], F32, kind="ExternalInput")
    w1_h = nc.dram_tensor("w1", [H4, D], F32, kind="ExternalInput")
    b1_h = nc.dram_tensor("b1", [H4], F32, kind="ExternalInput")
    w2_h = nc.dram_tensor("w2", [D, H4], F32, kind="ExternalInput")
    b2_h = nc.dram_tensor("b2", [1, D], F32, kind="ExternalInput")
    id_h = nc.dram_tensor("cident", [128, 128], BF16, kind="ExternalInput")
    msel_h = nc.dram_tensor("msel", [D, 16], BF16, kind="ExternalInput")
    mselT_h = nc.dram_tensor("mselT", [16, D], BF16, kind="ExternalInput")
    onesp_h = nc.dram_tensor("onesp", [16, 128], BF16, kind="ExternalInput")
    wqb_h = nc.dram_tensor("wqb", [D, D], BF16)
    wkb_h = nc.dram_tensor("wkb", [D, D], BF16)
    wvb_h = nc.dram_tensor("wvb", [D, D], BF16)
    wob_h = nc.dram_tensor("wob", [D, D], BF16)
    w1t_h = nc.dram_tensor("w1t", [D, H4], BF16)
    w2qt_h = nc.dram_tensor("w2qt", [H4, D], FP8)
    dbg = {}
    if debug:
        for nm, shp, dt in [("d_ln1fm", [128, 8 * 512], BF16),
                            ("d_qkfm", [128, 8 * 512], BF16),
                            ("d_qkv", [128, 8 * 512], BF16),
                            ("d_a", [128, 4 * D], F32),
                            ("d_u", [128, ncc * D], BF16),
                            ("d_h", [128, 512], F32),
                            ("d_hq", [128, 32 * 512], FP8),
                            ("d_o", [128, 4 * D], F32),
                            ("d_oq", [128, 4 * D], F32),
                            ("d_w2q", [128, H4], BF16),
                            ("d_pg", [128, 2048], F32),
                            ("d_pr", [128, 2048], F32),
                            ("d_psrc", [128, H4], F32),
                            ("d_xt", [128, 4 * D], F32),
                            ("d_stats", [128, 16], F32)]:
            dbg[nm] = nc.dram_tensor(nm, shp, dt, kind="ExternalOutput")

    def isz(dt):
        return np.dtype(mybir.dt.np(dt)).itemsize

    A = lambda name, shape, dt: nc.alloc_sbuf_tensor(name, shape, dt)
    ident = A("ident", [128, 128], BF16)
    msel = A("mselb", [128, 8 * 16], BF16)
    mselT = A("mselTb", [16, 8 * 128], BF16)
    onesp = A("onespb", [16, 128], BF16)
    bqc = A("bqc", [128, 8], F32)
    bkc = A("bkc", [128, 8], F32)
    bvc = A("bvc", [128, 8], F32)
    b1c = A("b1c", [128, 32], F32)
    bo_bc = A("bo_bc", [128, D], F32)
    b2q_bc = A("b2q_bc", [128, D], F32)
    cpool = A("cpool", [128, len(CONSTS)], F32)
    stats = A("stats", [128, 64], F32)
    U = A("U", [128, ncc * D], BF16)

    ARENA_BYTES = 190464
    r = nc.bump_sbuf(ARENA_BYTES, align_bytes=512)
    assert r is not None, "SBUF arena allocation failed"
    ar_lo = r[0]

    class Arena:
        def __init__(self, base):
            self.base, self.off = base, 0
            self.map = {}

        def al(self, name, shape, dt):
            n = int(np.prod(shape[1:])) * isz(dt)
            self.off = (self.off + 63) // 64 * 64
            t = nc.alloc_sbuf_tensor_at(name, shape, dt, offset=self.base + self.off)
            self.map[name] = self.base + self.off
            self.off += n
            assert self.off <= ARENA_BYTES, f"arena overflow at {name}: {self.off}"
            return t

        def at(self, name, shape, dt, abs_off):
            return nc.alloc_sbuf_tensor_at(name, shape, dt, offset=abs_off)

    pa = Arena(ar_lo)
    p_stage = pa.al("p_stage", [128, H4], F32)
    p_stage2 = pa.al("p_stage2", [128, H4], F32)
    p_wbf = pa.al("p_wbf", [128, 32 * D], BF16)
    p_qb = [pa.al(f"p_q{i}", [128, 2048], F32) for i in range(5)]
    p_qi = pa.al("p_qi", [128, 2048], I32)
    p_tr8 = pa.al("p_tr8", [128, 2 * 8 * 128], FP8)
    p_trb = pa.al("p_trb", [128, 2 * 8 * 128], BF16)
    p_row = pa.al("p_row", [16, D], F32)
    p_rowb = pa.al("p_rowb", [16, D], BF16)
    p_w1s = pa.al("p_w1s", [128, 2 * H4], BF16)

    aa = Arena(ar_lo)
    WQ = aa.al("WQ", [128, 8 * D], BF16)
    WK = aa.al("WK", [128, 8 * D], BF16)
    WV = aa.al("WV", [128, 8 * D], BF16)
    WO = aa.al("WO", [128, 8 * D], BF16)
    Xt = aa.al("Xt", [128, 4 * D], F32)
    ln1_tok = aa.al("ln1_tok", [128, D], BF16)
    ln1_fm = aa.al("ln1_fm", [128, 8 * 512], BF16)
    sqq = aa.al("sqq", [128, 512], F32)
    sqk = aa.al("sqk", [128, 512], F32)
    sqs = aa.al("sqs", [128, 512], F32)
    Vfm = aa.al("Vfm", [128, 8 * 512], BF16)
    QKfm = aa.al("QKfm", [128, 8 * 512], BF16)
    qkvfm = aa.al("qkvfm", [128, 8 * 512], BF16)
    coefb = aa.al("coefb", [16, 512], BF16)
    Atok = aa.al("Atok", [128, 4 * D], F32)
    lnA = aa.al("lnA", [128, D], F32)

    ba = Arena(ar_lo)
    W1T = ba.al("W1T", [128, 8 * H4], BF16)
    W2QT = ba.al("W2QT", [128, 32 * D], FP8)
    ln2_fm = ba.al("ln2_fm", [128, 8 * 512], BF16)
    ln2_tok = ba.al("ln2_tok", [128, D], BF16)
    Hq = ba.al("Hq", [128, 32 * 512], FP8)
    Otok = ba.al("Otok", [128, 4 * D], F32)
    qsets = []
    for si in range(4):
        slots = {nm: ba.al(f"q{si}_{nm}", [128, 512], F32)
                 for nm in ["hv", "hc", "t64", "d2y"]}
        slots["g"] = ba.al(f"q{si}_g", [128, 512], BF16)
        slots["ii"] = ba.al(f"q{si}_ii", [128, 512], I32)
        qsets.append(slots)
    for si in (4, 5):
        k = si - 4
        ob = ba.map["Otok"] + k * 8192
        slots = {nm: ba.at(f"q{si}_{nm}", [128, 512], F32, ob + i * 2048)
                 for i, nm in enumerate(["hv", "hc", "t64", "d2y"])}
        slots["g"] = ba.at(f"q{si}_g", [128, 512], BF16,
                           ba.map["ln2_tok"] + k * 1024)
        slots["ii"] = ba.al(f"q{si}_ii", [128, 512], I32)
        qsets.append(slots)

    PT = nc.alloc_psum_tensor("psTR", [128, 1024], BF16)
    PS = [nc.alloc_psum_tensor(f"ps{i}", [128, 512], F32) for i in range(6)]
    PC = nc.alloc_psum_tensor("psC", [128, 512], F32)

    p = Prog(nc)
    for s in ["d", "g", "t", "v", "s", "o"]:
        p.mksem(s)
    for i, v in enumerate(CONSTS):
        nc.const_aps.register(F32, float(v), cpool[:, i:i + 1])

    def dma(q_, dst, src, inc, deps=(), **kw):
        p.wait(q_, *deps)
        return p.op(q_, lambda e, d=dst, s=src: e.dma_start(d, s, **kw),
                    inc=inc, amt=16)

    def ts1(e, out, in_, s1, o1):
        return e.tensor_single_scalar(out, in_, s1, o1)

    for i, v in enumerate(CONSTS):
        p.op("v", lambda e, i=i, v=v: e.memset(cpool[:, i:i + 1], float(v)))
    p.op("v", lambda e: e.memset(stats[:, :], 1.0), inc="v")

    dma("y", ident[:, :], id_h[:, :], "d")
    dma("y", msel.ap().rearrange("p (c h) -> p c h", h=16),
        msel_h.ap().rearrange("(c p) h -> p c h", p=128), "d")
    dma("y", mselT[:, :], mselT_h[:, :], "d")
    dma("y", onesp[:, :], onesp_h[:, :], "d")
    for dst, src in [(bqc, bq_h), (bkc, bk_h), (bvc, bv_h)]:
        p.op("y", lambda e, d=dst, s=src: e.dma_start(
            d[:, :], s.ap().rearrange("(c p) -> p c", p=128),
            allow_slow_non_contiguous=True), inc="d", amt=16)
    p.op("y", lambda e: e.dma_start(
        b1c[:, :], b1_h.ap().rearrange("(c p) -> p c", p=128),
        allow_slow_non_contiguous=True), inc="d", amt=16)

    def chains_stage1(lanes):
        for L in lanes:
            w = L["width"]; b = L["bufset"]
            for nm in ["hv", "hc", "t64", "d2y", "g"]:
                L[nm] = b[nm][:, :w]
            L["ii"] = b["ii"][:, :w]
            L["sl"] = 2.0 * LN2 if L["signed"] else LN2
        for L in lanes:
            if L.get("relu_bias") is not None:
                p.wait("s", L.get("psum_tok"), L.get("war_v"))
                L["tk1"] = p.op("s", lambda e, L=L: e.activation(
                    L["hv"], L["psum_src"], AF.Relu, bias=L["relu_bias"],
                    scale=1.0), inc="s")
                L["xs"] = L["hv"]
            elif L["signed"]:
                p.wait("s", L.get("src_tok"), L.get("war_v"))
                L["tk1"] = p.op("s", lambda e, L=L: e.activation(
                    L["d2y"], L["xsrc"], AF.Square, bias=0.0,
                    scale=2.0 ** 16), inc="s")
                L["xs"] = L["xsrc"]
            else:
                L["tk1"] = None
                L["xs"] = L["xsrc"]
        for L in lanes:
            if L.get("relu_bias") is not None:
                p.wait("v", L["tk1"], L.get("war_s"))
                L["tk2"] = p.op("v", lambda e, L=L: ts1(
                    e, L["hc"], L["hv"], L["clamp_lo"], OP.max), inc="v")
            elif L["signed"]:
                csq = float(L["clamp_lo"]) ** 2 * 2.0 ** 32
                p.wait("v", L["tk1"], L.get("war_s"))
                L["tk2"] = p.op("v", lambda e, L=L, c=csq: ts1(
                    e, L["hc"], L["d2y"], c, OP.max), inc="v")
            else:
                p.wait("v", L.get("src_tok"), L.get("war_s"))
                L["tk2"] = p.op("v", lambda e, L=L: ts1(
                    e, L["hc"], L["xsrc"], L["clamp_lo"], OP.max), inc="v")
        for L in lanes:
            p.wait("s", L["tk2"], L.get("war_v") if L["tk1"] is None else None)
            sc = 1.0 if L["signed"] else 2.0 ** 32
            p.op("s", lambda e, L=L, sc=sc: e.activation(
                L["t64"], L["hc"], AF.Ln, bias=0.0, scale=sc), inc="s")
        for L in lanes:
            L["tk4"] = p.op("s", lambda e, L=L: e.activation(
                L["ii"], L["t64"], AF.Copy, bias=0.0, scale=1.0 / L["sl"]), inc="s")
        for L in lanes:
            p.wait("v", L["tk4"])
            p.op("v", lambda e, L=L: ts1(e, L["hc"], L["ii"], L["sl"], OP.mult), inc="v")
        for L in lanes:
            p.op("v", lambda e, L=L: e.tensor_tensor(
                L["d2y"], L["hc"], L["t64"], OP.is_gt), inc="v")
        for L in lanes:
            L["tk7"] = p.op("v", lambda e, L=L: e.tensor_tensor(
                L["hc"], L["ii"], L["d2y"], OP.subtract), inc="v")
        for L in lanes:
            p.wait("s", L["tk7"])
            p.op("s", lambda e, L=L: e.activation(
                L["t64"], L["hc"], AF.Exp, bias=L["bias_p"], scale=-LN2), inc="s")
        for L in lanes:
            L["g_tok"] = p.op("s", lambda e, L=L: e.activation(
                L["g"], L["hc"], AF.Exp, bias=L["bias_m"], scale=LN2), inc="s")
        return lanes

    def chains_stage2(lanes):
        for L in lanes:
            p.wait("v", L["g_tok"])
            p.op("v", lambda e, L=L: e.tensor_tensor(
                L["d2y"], L["xs"], L["t64"], OP.mult), inc="v")
        for L in lanes:
            L["tk11"] = p.op("v", lambda e, L=L: ts1(
                e, L["ii"], L["d2y"], L["off_c"], OP.add), inc="v")
        for L in lanes:
            p.wait("s", L["tk11"])
            L["end_s"] = p.op("s", lambda e, L=L: e.activation(
                L["t64"], L["d2y"], AF.Copy, bias=L["off_c"], scale=1.0), inc="s")
        for L in lanes:
            p.wait("v", L["end_s"])
            p.op("v", lambda e, L=L: e.tensor_tensor(
                L["hc"], L["ii"], L["t64"], OP.is_gt), inc="v")
        for L in lanes:
            p.op("v", lambda e, L=L: e.tensor_tensor(
                L["t64"], L["ii"], L["hc"], OP.subtract), inc="v")
        for L in lanes:
            if L["off_sub"]:
                L["end_v"] = p.op("v", lambda e, L=L: e.scalar_tensor_tensor(
                    L["dst"], L["t64"], -float(L["off_sub"]), L["g"],
                    OP.add, OP.mult), inc="v")
            else:
                L["end_v"] = p.op("v", lambda e, L=L: e.tensor_tensor(
                    L["dst"], L["t64"], L["g"], OP.mult), inc="v")
        return lanes

    def quant_chain(width, xsrc, dst_ap, bias_p, bias_m, off_c, off_sub,
                    clamp_lo, signed, bufset, psum_tok=None, psum_src=None,
                    relu_bias=None, war_v=None, war_s=None, src_tok=None):
        L = dict(width=width, bufset=bufset, signed=signed, bias_p=bias_p,
                 bias_m=bias_m, clamp_lo=clamp_lo, war_v=war_v, war_s=war_s,
                 xsrc=xsrc, src_tok=src_tok, psum_src=psum_src,
                 psum_tok=psum_tok, relu_bias=relu_bias,
                 dst=dst_ap, off_c=off_c, off_sub=off_sub)
        chains_stage1([L])
        chains_stage2([L])
        return {"relu": L["tk1"], "end_v": L["end_v"], "end_s": L["end_s"]}

    def layer_norm_stats(idx_list, src, dump, src_tok=None, prev=None,
                         dump_war=None):
        n = len(idx_list)
        prev = prev or {}
        acc_tok = None
        for i, cidx in enumerate(idx_list):
            sl = src[:, cidx * D:(cidx + 1) * D]
            p.wait("v", src_tok if i == 0 else None,
                   prev.get("rstd") if i == 0 else None)
            p.op("v", lambda e, sl=sl, i=i: e.reduce_sum(
                stats[:, i:i + 1], sl, axis=mybir.AxisListType.X), inc="v")
            p.wait("s", src_tok if i == 0 else None,
                   prev.get("var") if i == 0 else None,
                   dump_war if i == 0 else None)
            acc_tok = p.op("s", lambda e, sl=sl, i=i: e.activation(
                dump[:, :], sl, AF.Square, bias=0.0, scale=1.0,
                accum_out=stats[:, 16 + i:17 + i]), inc="s")
        p.wait("v", acc_tok)
        m_tok = p.op("v", lambda e, n=n: ts1(e, stats[:, 0:n], stats[:, 0:n],
                                             1.0 / D, OP.mult), inc="v")
        p.op("v", lambda e, n=n: ts1(e, stats[:, 16:16 + n], stats[:, 16:16 + n],
                                     1.0 / D, OP.mult), inc="v")
        p.wait("s", m_tok)
        msq_tok = p.op("s", lambda e, n=n: e.activation(
            stats[:, 48:48 + n], stats[:, 0:n], AF.Square, bias=0.0,
            scale=1.0), inc="s")
        p.wait("v", msq_tok)
        var_tok = p.op("v", lambda e, n=n: e.tensor_tensor(
            stats[:, 16:16 + n], stats[:, 16:16 + n], stats[:, 48:48 + n],
            OP.subtract), inc="v")
        p.wait("s", var_tok)
        p.op("s", lambda e, n=n: e.activation(stats[:, 48:48 + n], stats[:, 16:16 + n],
                                              AF.Ln, bias=C_EPS, scale=1.0), inc="s")
        rstd_tok = p.op("s", lambda e, n=n: e.activation(
            stats[:, 32:32 + n], stats[:, 48:48 + n], AF.Exp, bias=0.0,
            scale=-0.5), inc="s")
        return {"rstd": rstd_tok, "var": var_tok}

    for wsrc, wdst in [(wq_h, wqb_h), (wk_h, wkb_h), (wv_h, wvb_h), (wo_h, wob_h)]:
        for c in range(4):
            dma("g", wdst[c * 256:(c + 1) * 256, :], wsrc[c * 256:(c + 1) * 256, :], "g")
    w1_ld, w1_ev, w1_st = {}, {}, {}

    def w1_prefetch(dc):
        for hg in range(4):
            gidx = dc * 4 + hg
            slot = gidx % 8
            tok = None
            for j in range(8):
                hcn = hg * 8 + j
                tok = dma("g", p_w1s[:, slot * 1024 + j * 128: slot * 1024 + (j + 1) * 128],
                          w1_h[hcn * 128:(hcn + 1) * 128, dc * 128:(dc + 1) * 128],
                          "g", deps=[w1_ev.get(gidx - 8)] if j == 0 else ())
            w1_ld[gidx] = tok

    w1_prefetch(0)
    w1_prefetch(1)
    if stage < 1:
        return _finish(nc, p)
    p.op("v", lambda e: e.memset(p_row[:, :], 0.0), inc="v")
    tok_bo = dma("y", p_row[0:1, :], bo_h[:, :], "d", deps=["v"])
    p.wait("v", tok_bo)
    p.op("v", lambda e: e.tensor_copy(p_rowb[:, :], p_row[:, :]), inc="v")
    p.wait("t", "v")
    for half in range(2):
        p.op("t", lambda e, half=half: e.matmul(
            PS[0][:, :], onesp[:, :], p_rowb[:, half * 512:(half + 1) * 512],
            start=True, stop=True), inc="t")
        p.wait("v", "t")
        p.op("v", lambda e, half=half: e.tensor_copy(
            bo_bc[:, half * 512:(half + 1) * 512], PS[0][:, :]), inc="v")
        p.wait("t", "v")
    tok_b2 = dma("y", p_row[0:1, :], b2_h[:, :], "d", deps=["v"])
    p.wait("v", tok_b2)
    p.op("v", lambda e: e.tensor_copy(p_rowb[:, :], p_row[:, :]), inc="v")
    p.wait("t", "v")
    for half in range(2):
        p.op("t", lambda e, half=half: e.matmul(
            PS[0][:, :], onesp[:, :], p_rowb[:, half * 512:(half + 1) * 512],
            start=True, stop=True), inc="t")
        p.wait("v", "t")
        p.op("v", lambda e, half=half: e.tensor_copy(
            p_stage[:, half * 512:(half + 1) * 512], PS[0][:, :]), inc="v")
        p.wait("t", "v")
    pset = {nm: p_qb[k] for k, nm in enumerate(["hv", "hc", "t64", "d2y", "g"])}
    pset["ii"] = p_qi
    quant_chain(D, p_stage[:, :D], b2q_bc[:, :], C_P26L2, C_M26L2, C_4096_5, 4096,
                2.0 ** -14, signed=True, bufset=pset,
                war_v=p.latest("v"), war_s=p.latest("s"), src_tok=p.latest("v"))
    if stage < 2:
        return _finish(nc, p)
    prep_sets = []
    for si in range(4):
        ps_ = {nm: p_qb[k].ap()[:, si * 512:(si + 1) * 512]
               for k, nm in enumerate(["hv", "hc", "t64", "d2y", "g"])}
        ps_["ii"] = p_qi.ap()[:, si * 512:(si + 1) * 512]
        prep_sets.append(ps_)
    prep_hist = {0: None, 1: None, 2: None, 3: None}
    st8_tok = {}

    def w2_transposes(dc):
        p.wait("t", prep_hist[2]["end_v"], prep_hist[3]["end_v"])
        for hg in range(4):
            gidx = dc * 4 + hg
            buf = gidx % 2
            tr_tok = None
            for j in range(8):
                hcn = hg * 8 + j
                src = p_wbf[:, dc * H4 + hcn * 128: dc * H4 + (hcn + 1) * 128]
                tr_tok = p.op("t", lambda e, src=src, j=j: e.transpose(
                    PT[:, j * 128:(j + 1) * 128], src, ident[:, :]),
                    inc="t" if j == 7 else None)
            p.wait("v", tr_tok, st8_tok.get(gidx - 2))
            ev = p.op("v", lambda e, buf=buf: e.tensor_copy(
                p_tr8[:, buf * 1024:(buf + 1) * 1024], PT[:, :]), inc="v")
            p.wait("t", ev)
            for j in range(8):
                hcn = hg * 8 + j
                st8_tok[gidx] = dma(
                    "y", w2qt_h[hcn * 128:(hcn + 1) * 128, dc * 128:(dc + 1) * 128],
                    p_tr8[:, buf * 1024 + j * 128: buf * 1024 + (j + 1) * 128],
                    "d", deps=[ev] if j == 0 else ())
            slot = gidx % 8
            p.wait("t", w1_ld[gidx])
            trb = None
            for j in range(8):
                src = p_w1s[:, slot * 1024 + j * 128: slot * 1024 + (j + 1) * 128]
                trb = p.op("t", lambda e, src=src, j=j: e.transpose(
                    PT[:, j * 128:(j + 1) * 128], src, ident[:, :]),
                    inc="t" if j == 7 else None)
            p.wait("v", trb, w1_st.get(gidx - 2))
            evb = p.op("v", lambda e, buf=buf: e.tensor_copy(
                p_trb[:, buf * 1024:(buf + 1) * 1024], PT[:, :]), inc="v")
            w1_ev[gidx] = evb
            p.wait("t", evb)
            p.wait("g", evb)
            w1_st[gidx] = dma("g", w1t_h[dc * 128:(dc + 1) * 128,
                                         hg * 1024:(hg + 1) * 1024],
                              p_trb[:, buf * 1024:(buf + 1) * 1024], "g")
        if 2 <= dc + 1 <= 7:
            w1_prefetch(dc + 1)

    stg = [p_stage, p_stage2]
    dtoks = {0: dma("y", p_stage[:, :], w2_h[0:128, :], "d", deps=["v", "s"]),
             1: dma("y", p_stage2[:, :], w2_h[128:256, :], "d")}
    dc_flush = {}
    for dc in range(8):
        if 1 <= dc <= 6:
            dtoks[dc + 1] = dma("y", stg[(dc + 1) % 2][:, :],
                                w2_h[(dc + 1) * 128:(dc + 2) * 128, :], "d",
                                deps=dc_flush[dc - 1])
        pend = None
        for m2 in range(4):
            lanes = []
            for j in range(2):
                sl8 = 2 * m2 + j
                si = 2 * (m2 % 2) + j
                hist = prep_hist[si]
                lanes.append(dict(
                    width=512, bufset=prep_sets[si], signed=True,
                    bias_p=C_P19L2, bias_m=C_M19L2, clamp_lo=2.0 ** -6,
                    xsrc=stg[dc % 2][:, sl8 * 512:(sl8 + 1) * 512],
                    src_tok=dtoks[dc],
                    war_v=hist["end_v"] if hist else None,
                    war_s=hist["end_s"] if hist else None,
                    dst=p_wbf[:, dc * H4 + sl8 * 512: dc * H4 + (sl8 + 1) * 512],
                    off_c=C_64_5, off_sub=64, si=si))
            chains_stage1(lanes)
            if pend is not None:
                chains_stage2(pend)
                for L in pend:
                    prep_hist[L["si"]] = {"end_v": L["end_v"], "end_s": L["end_s"]}
            if dc > 0 and m2 == 1:
                w2_transposes(dc - 1)
            pend = lanes
        chains_stage2(pend)
        for L in pend:
            prep_hist[L["si"]] = {"end_v": L["end_v"], "end_s": L["end_s"]}
        dc_flush[dc] = [prep_hist[2]["end_v"], prep_hist[3]["end_v"],
                        prep_hist[2]["end_s"], prep_hist[3]["end_s"]]
    w2_transposes(7)
    if debug:
        p.wait("y", "v")
        dma("y", dbg["d_w2q"][:, :], p_wbf[:, 0:H4], "o")
        dma("y", dbg["d_pg"][:, :], p_qb[4][:, :], "o")
        dma("y", dbg["d_pr"][:, :], p_qb[2][:, :], "o")
        dma("y", dbg["d_psrc"][:, :], p_stage[:, :], "o")
    if stage < 3:
        return _finish(nc, p)

    if stage < 4:
        return _finish(nc, p)
    stats_prev = None
    tok_u_ready = None
    for sup in range(nsup):
        base_tok = sup * sup_tok
        p.barrier_on("y")
        for Wd, Ws in [(WQ, wqb_h), (WK, wkb_h), (WV, wvb_h), (WO, wob_h)]:
            for c in range(8):
                dma("y", Wd[:, c * D:(c + 1) * D], Ws[c * 128:(c + 1) * 128, :], "d")
        tok_wload = p.latest("d")
        tok_qkv_last = None
        tok_wo_last = None
        tok_coef_last = None
        tok_bc_last = None
        tok_uadd = [None] * 4
        for lt in range(ta_per_sup):
            t0 = base_tok + lt * 512
            xt_tok = []
            for tc in range(4):
                xt_tok.append(dma("y", Xt[:, tc * D:(tc + 1) * D],
                                  x_h[t0 + tc * 128:t0 + tc * 128 + 128, :], "d",
                                  deps=["v", "s"]))
            stats_prev = layer_norm_stats([0, 1, 2, 3], Xt, lnA,
                                          src_tok=xt_tok[3], prev=stats_prev,
                                          dump_war=tok_uadd[3])
            rstd_tok = stats_prev["rstd"]
            tr_tok = None
            ev_toks = []
            for tc in range(4):
                p.wait("v", rstd_tok, tok_qkv_last)
                ap_tok = p.op("v", lambda e, tc=tc: e.tensor_scalar(
                    ln1_tok[:, :], Xt[:, tc * D:(tc + 1) * D],
                    stats[:, tc:tc + 1], stats[:, 32 + tc:33 + tc],
                    OP.subtract, OP.mult), inc="v")
                p.wait("t", ap_tok, ev_toks[-1] if ev_toks else None)
                for fc in range(8):
                    tr_tok = p.op("t", lambda e, fc=fc: e.transpose(
                        PT[:, fc * 128:(fc + 1) * 128],
                        ln1_tok[:, fc * 128:(fc + 1) * 128], ident[:, :]),
                        inc="t" if fc == 7 else None)
                p.wait("v", tr_tok)
                ev_toks.append(p.op("v", lambda e, tc=tc: e.tensor_copy(
                    _fmcols(ln1_fm, 512, tc), PT[:, :]), inc="v"))
            if debug:
                dma("y", dbg["d_ln1fm"][:, :], ln1_fm[:, :], "o", deps=[ev_toks[-1]])
            rd = {}
            exp_tok = {}
            vfm_tok = {}
            sqs_tok = None
            for oc in range(8):
                par = oc % 2
                psq, psk, psv = PS[3 * par], PS[3 * par + 1], PS[3 * par + 2]
                if oc >= 2:
                    p.wait("t", *rd[oc - 2])
                else:
                    p.wait("t", ev_toks[-1], tok_wload,
                           tok_wo_last if sup == 0 else None)
                mm_toks = []
                for W_, psx in [(WQ, psq), (WK, psk), (WV, psv)]:
                    for fc in range(8):
                        lhsT = W_[:, fc * D + oc * 128: fc * D + (oc + 1) * 128]
                        rhs = ln1_fm[:, fc * 512:(fc + 1) * 512]
                        tk = p.op("t", lambda e, lhsT=lhsT, rhs=rhs, psx=psx, fc=fc: e.matmul(
                            psx[:, :], lhsT, rhs, start=(fc == 0), stop=(fc == 7)),
                            inc="t" if fc == 7 else None)
                        if tk:
                            mm_toks.append(tk)
                tok_qkv_last = mm_toks[-1]
                p.wait("v", mm_toks[2])
                vfm_tok[oc] = p.op("v", lambda e, oc=oc, psv=psv: ts1(
                    e, Vfm[:, oc * 512:(oc + 1) * 512], psv[:, :],
                    bvc[:, oc:oc + 1], OP.add), inc="v")
                p.wait("s", mm_toks[0], sqs_tok)
                p.op("s", lambda e, oc=oc, psq=psq: e.activation(
                    sqq[:, :], psq[:, :], AF.Square, bias=bqc[:, oc:oc + 1],
                    scale=1.0), inc="s")
                p.wait("s", mm_toks[1])
                sq_tok = p.op("s", lambda e, oc=oc, psk=psk: e.activation(
                    sqk[:, :], psk[:, :], AF.Square, bias=bkc[:, oc:oc + 1],
                    scale=1.0), inc="s")
                p.wait("v", sq_tok, exp_tok.get(oc - 1), tok_coef_last if oc == 0 else None)
                sqs_tok = p.op("v", lambda e: e.tensor_add(sqs[:, :], sqq[:, :],
                                                           sqk[:, :]), inc="v")
                p.wait("s", sqs_tok)
                exp_tok[oc] = p.op("s", lambda e, oc=oc: e.activation(
                    QKfm[:, oc * 512:(oc + 1) * 512], sqs[:, :], AF.Exp,
                    bias=0.0, scale=-0.5), inc="s")
                rd[oc] = (vfm_tok[oc], sq_tok)
            if debug:
                dma("y", dbg["d_qkfm"][:, :], QKfm[:, :], "o", deps=[exp_tok[7]])
            p.wait("t", tok_bc_last)
            for oc in range(8):
                p.wait("t", exp_tok[oc])
                tok_coef_last = p.op("t", lambda e, oc=oc: e.matmul(
                    PC[0:16, :], msel[:, oc * 16:(oc + 1) * 16],
                    QKfm[:, oc * 512:(oc + 1) * 512],
                    start=(oc == 0), stop=(oc == 7)),
                    inc="t" if oc == 7 else None)
            p.wait("v", tok_coef_last)
            coefb_tok = p.op("v", lambda e: e.tensor_copy(coefb[:, :], PC[0:16, :]), inc="v")
            stt_toks = {}
            for oc in range(8):
                bank = PS[3] if oc % 2 == 0 else PC
                p.wait("t", coefb_tok if oc < 2 else stt_toks[oc - 2],
                       rd[7][1] if oc == 0 else None)
                bc_tok = p.op("t", lambda e, oc=oc, bank=bank: e.matmul(
                    bank[0:128, :], mselT[:, oc * 128:(oc + 1) * 128],
                    coefb[:, :], start=True, stop=True), inc="t")
                p.wait("v", bc_tok, tok_wo_last if oc == 0 else None)
                stt_toks[oc] = p.op("v", lambda e, oc=oc, bank=bank: e.scalar_tensor_tensor(
                    qkvfm[:, oc * 512:(oc + 1) * 512], bank[0:128, :], 1.0,
                    Vfm[:, oc * 512:(oc + 1) * 512], OP.bypass, OP.mult), inc="v")
            tok_bc_last_list = []
            ev2 = {}
            for tp in range(4):
                for oh in range(2):
                    pse = PS[4 + oh]
                    p.wait("t", stt_toks[7], ev2.get(tp - 1),
                           rd[7][1] if tp == 0 else None,
                           rd[7][0] if tp == 0 else None)
                    for fc in range(8):
                        lhsT = qkvfm[:, fc * 512 + tp * 128: fc * 512 + (tp + 1) * 128]
                        rhs = WO[:, fc * D + oh * 512: fc * D + (oh + 1) * 512]
                        tok_wo_last = p.op("t", lambda e, lhsT=lhsT, rhs=rhs, pse=pse, fc=fc: e.matmul(
                            pse[:, :], lhsT, rhs, start=(fc == 0), stop=(fc == 7)),
                            inc="t" if fc == 7 else None)
                    p.wait("v", tok_wo_last, tok_uadd[tp])
                    ev2[tp] = p.op("v", lambda e, tp=tp, oh=oh, pse=pse: e.tensor_tensor(
                        Atok[:, tp * D + oh * 512: tp * D + (oh + 1) * 512],
                        pse[:, :], bo_bc[:, oh * 512:(oh + 1) * 512],
                        OP.add), inc="v")
            if debug:
                dma("y", dbg["d_a"][:, :], Atok[:, :], "o", deps=[ev2[3]])
            stats_prev = layer_norm_stats([0, 1, 2, 3], Atok, lnA,
                                          src_tok=ev2[3], prev=stats_prev)
            rstd_tok = stats_prev["rstd"]
            for tc in range(4):
                cc = lt * 4 + tc
                p.wait("v", rstd_tok)
                p.op("v", lambda e, tc=tc: e.tensor_scalar(
                    lnA[:, :], Atok[:, tc * D:(tc + 1) * D],
                    stats[:, tc:tc + 1], stats[:, 32 + tc:33 + tc],
                    OP.subtract, OP.mult), inc="v")
                tok_uadd[tc] = p.op("v", lambda e, tc=tc, cc=cc: e.tensor_tensor(
                    U[:, cc * D:(cc + 1) * D], Xt[:, tc * D:(tc + 1) * D],
                    lnA[:, :], OP.add), inc="v")
                tok_u_ready = tok_uadd[tc]
        if debug:
            dma("y", dbg["d_u"][:, :], U[:, :], "o", deps=[tok_uadd[3]])
        if stage < 5:
            continue
        p.barrier_on("y")
        for c in range(8):
            dma("y", W1T[:, c * H4:(c + 1) * H4], w1t_h[c * 128:(c + 1) * 128, :], "d")
        for c in range(32):
            dma("y", W2QT[:, c * D:(c + 1) * D], w2qt_h[c * 128:(c + 1) * 128, :], "d")
        tok_wload = p.latest("d")
        tok_b2_last = None
        tok_mm1_last = None
        tok_ost = {}
        tr_tok = None
        chain_hist = {si: None for si in range(6)}
        for bt in range(tb_per_sup):
            cc0 = bt * 4
            stats_prev = layer_norm_stats([cc0, cc0 + 1, cc0 + 2, cc0 + 3], U,
                                          ln2_tok, src_tok=tok_u_ready,
                                          prev=stats_prev, dump_war=tr_tok)
            rstd_tok = stats_prev["rstd"]
            ev_toks = []
            for k in range(4):
                p.wait("v", rstd_tok, tok_mm1_last)
                ap_tok = p.op("v", lambda e, k=k, cc0=cc0: e.tensor_scalar(
                    ln2_tok[:, :], U[:, (cc0 + k) * D:(cc0 + k + 1) * D],
                    stats[:, k:k + 1], stats[:, 32 + k:33 + k],
                    OP.subtract, OP.mult), inc="v")
                p.wait("t", ap_tok, ev_toks[-1] if ev_toks else None,
                       tok_b2_last if bt > 0 and k == 0 else None)
                for fc in range(8):
                    tr_tok = p.op("t", lambda e, fc=fc: e.transpose(
                        PT[:, fc * 128:(fc + 1) * 128],
                        ln2_tok[:, fc * 128:(fc + 1) * 128], ident[:, :]),
                        inc="t" if fc == 7 else None)
                p.wait("v", tr_tok)
                ev_toks.append(p.op("v", lambda e, k=k: e.tensor_copy(
                    _fmcols(ln2_fm, 512, k), PT[:, :]), inc="v"))
            relu_toks = {}
            pend = None
            last_v = None
            for g_ in range(11):
                lanes = []
                for ln_ in range(3):
                    hcn = 3 * g_ + ln_
                    if hcn >= 32:
                        break
                    psh = PS[hcn % 4]
                    p.wait("t", ev_toks[-1] if hcn == 0 else None,
                           tok_wload if hcn == 0 else None,
                           relu_toks.get(hcn - 4))
                    mmtok = None
                    for dc in range(8):
                        lhsT = W1T[:, dc * H4 + hcn * 128: dc * H4 + (hcn + 1) * 128]
                        rhs = ln2_fm[:, dc * 512:(dc + 1) * 512]
                        mmtok = p.op("t", lambda e, lhsT=lhsT, rhs=rhs, psh=psh, dc=dc: e.matmul(
                            psh[:, :], lhsT, rhs, start=(dc == 0), stop=(dc == 7)),
                            inc="t" if dc == 7 else None)
                    tok_mm1_last = mmtok
                    si = 3 * (g_ % 2) + ln_
                    hist = chain_hist[si]
                    lanes.append(dict(
                        width=512, bufset=qsets[si], signed=False,
                        bias_p=C_P35L2, bias_m=C_M35L2, clamp_lo=2.0 ** -6,
                        psum_src=psh[:, :], psum_tok=mmtok,
                        relu_bias=b1c[:, hcn:hcn + 1],
                        war_v=hist["end_v"] if hist else None,
                        war_s=hist["end_s"] if hist else None,
                        dst=Hq[:, hcn * 512:(hcn + 1) * 512],
                        off_c=C_HALF, off_sub=0, hcn=hcn, si=si))
                chains_stage1(lanes)
                for L in lanes:
                    relu_toks[L["hcn"]] = L["tk1"]
                if pend is not None:
                    chains_stage2(pend)
                    for L in pend:
                        chain_hist[L["si"]] = {"end_v": L["end_v"], "end_s": L["end_s"]}
                    last_v = pend[-1]["end_v"]
                pend = lanes
            chains_stage2(pend)
            for L in pend:
                chain_hist[L["si"]] = {"end_v": L["end_v"], "end_s": L["end_s"]}
            last_v = pend[-1]["end_v"]
            if debug and bt == 0:
                dma("y", dbg["d_h"][:, :], qsets[1]["hv"][:, :512], "o",
                    deps=[last_v])
                dma("y", dbg["d_hq"][:, :], Hq[:, :], "o", deps=[last_v])
            ev3 = {}
            gi = 0
            for tp in range(4):
                for dh in range(2):
                    pso = PS[2 + gi % 4]
                    if gi == 0:
                        p.wait("t", last_v, relu_toks[30])
                    elif gi == 1:
                        p.wait("t", relu_toks[31])
                    elif gi >= 4:
                        p.wait("t", ev3[gi - 4])
                    for hcn in range(32):
                        lhsT = Hq[:, hcn * 512 + tp * 128: hcn * 512 + (tp + 1) * 128]
                        rhs = W2QT[:, hcn * D + dh * 512: hcn * D + (dh + 1) * 512]
                        tok_b2_last = p.op("t", lambda e, lhsT=lhsT, rhs=rhs, pso=pso, hcn=hcn: e.matmul(
                            pso[:, :], lhsT, rhs, start=(hcn == 0), stop=(hcn == 31)),
                            inc="t" if hcn == 31 else None)
                    p.wait("v", tok_b2_last, tok_ost.get(gi // 2))
                    ev3[gi] = p.op("v", lambda e, tp=tp, dh=dh, pso=pso: e.tensor_tensor(
                        Otok[:, tp * D + dh * 512: tp * D + (dh + 1) * 512],
                        pso[:, :], b2q_bc[:, dh * 512:(dh + 1) * 512],
                        OP.add), inc="v")
                    gi += 1
            if debug and bt == 0:
                dma("y", dbg["d_o"][:, :], Otok[:, :], "o", deps=[ev3[7]])
            pend = None
            st2last = None
            for m_ in range(4):
                lanes = []
                for j in range(2):
                    seg = 2 * m_ + j
                    si = 2 * (m_ % 2) + j
                    hist = chain_hist[si]
                    lanes.append(dict(
                        width=512, bufset=qsets[si], signed=True,
                        bias_p=C_P26L2, bias_m=C_M26L2, clamp_lo=2.0 ** -14,
                        xsrc=Otok[:, seg * 512:(seg + 1) * 512],
                        src_tok=ev3[seg],
                        war_v=hist["end_v"] if hist else None,
                        war_s=hist["end_s"] if hist else None,
                        dst=Otok[:, seg * 512:(seg + 1) * 512],
                        off_c=C_4096_5, off_sub=4096, si=si))
                chains_stage1(lanes)
                if pend is not None:
                    chains_stage2(pend)
                    for L in pend:
                        chain_hist[L["si"]] = {"end_v": L["end_v"], "end_s": L["end_s"]}
                pend = lanes
            chains_stage2(pend)
            for L in pend:
                chain_hist[L["si"]] = {"end_v": L["end_v"], "end_s": L["end_s"]}
            st2last = pend[-1]["end_v"]
            if debug and bt == 0:
                dma("y", dbg["d_oq"][:, :], Otok[:, :], "o", deps=[st2last])
            stats_prev = layer_norm_stats([0, 1, 2, 3], Otok, ln2_tok,
                                          src_tok=st2last, prev=stats_prev,
                                          dump_war=tr_tok)
            rstd_tok = stats_prev["rstd"]
            for k in range(4):
                cc = cc0 + k
                p.wait("v", rstd_tok, tok_ost.get(k))
                p.op("v", lambda e, k=k: e.tensor_scalar(
                    Otok[:, k * D:(k + 1) * D], Otok[:, k * D:(k + 1) * D],
                    stats[:, k:k + 1], stats[:, 32 + k:33 + k],
                    OP.subtract, OP.mult), inc="v")
                ytok = p.op("v", lambda e, k=k, cc=cc: e.tensor_tensor(
                    Otok[:, k * D:(k + 1) * D], U[:, cc * D:(cc + 1) * D],
                    Otok[:, k * D:(k + 1) * D], OP.add), inc="v")
                r0 = base_tok + cc * 128
                tok_ost[k] = dma("y", y_h[r0:r0 + 128, :], Otok[:, k * D:(k + 1) * D],
                                 "o", deps=[ytok])

    p.wait("y", "o")
    return _finish(nc, p)


def _finish(nc, p):
    p.barrier_on("y")
    with nc.Block() as block:
        block.tensor(lambda e: [fn(e) for fn in p.q["t"]])
        block.vector(lambda e: [fn(e) for fn in p.q["v"]])
        block.scalar(lambda e: [fn(e) for fn in p.q["s"]])
        block.sync(lambda e: [fn(e) for fn in p.q["y"]])
        block.gpsimd(lambda e: [fn(e) for fn in p.q["g"]])
    return nc


def _make_const_inputs():
    hd = 64
    msel = np.zeros((D, 16), dtype=np.float32)
    for d_ in range(D):
        msel[d_, d_ // hd] = 1.0
    onesp = np.zeros((16, 128), dtype=np.float32)
    onesp[0, :] = 1.0
    return {
        "cident": np.eye(128, dtype=np.float32).astype(ml_dtypes.bfloat16),
        "msel": msel.astype(ml_dtypes.bfloat16),
        "mselT": msel.T.copy().astype(ml_dtypes.bfloat16),
        "onesp": onesp.astype(ml_dtypes.bfloat16),
    }


def kernel(L0_attn_x, L0_attn_Wq, L0_attn_bq, L0_attn_Wk, L0_attn_bk,
           L0_attn_Wv, L0_attn_bv, L0_attn_Wo, L0_attn_bo,
           L0_ffn_w1, L0_ffn_b1, L0_ffn_w2, L0_ffn_b2):
    x = np.ascontiguousarray(np.asarray(L0_attn_x, dtype=np.float32))
    B, S, Dm = x.shape
    tok = B * S
    assert Dm == D and tok % NCORES == 0
    tokc = tok // NCORES
    xf = x.reshape(tok, D)

    nc = build_program(tokc)
    consts = _make_const_inputs()
    common = {
        "wq": np.ascontiguousarray(L0_attn_Wq, dtype=np.float32),
        "wk": np.ascontiguousarray(L0_attn_Wk, dtype=np.float32),
        "wv": np.ascontiguousarray(L0_attn_Wv, dtype=np.float32),
        "wo": np.ascontiguousarray(L0_attn_Wo, dtype=np.float32),
        "bq": np.ascontiguousarray(L0_attn_bq, dtype=np.float32),
        "bk": np.ascontiguousarray(L0_attn_bk, dtype=np.float32),
        "bv": np.ascontiguousarray(L0_attn_bv, dtype=np.float32),
        "bo": np.ascontiguousarray(L0_attn_bo, dtype=np.float32).reshape(1, D),
        "w1": np.ascontiguousarray(L0_ffn_w1, dtype=np.float32),
        "b1": np.ascontiguousarray(L0_ffn_b1, dtype=np.float32),
        "w2": np.ascontiguousarray(L0_ffn_w2, dtype=np.float32),
        "b2": np.ascontiguousarray(L0_ffn_b2, dtype=np.float32).reshape(1, D),
        **consts,
    }
    in_maps = []
    for c in range(NCORES):
        m = dict(common)
        m["x"] = np.ascontiguousarray(xf[c * tokc:(c + 1) * tokc])
        in_maps.append(m)
    res = run_bass_kernel_spmd(nc, in_maps, core_ids=list(range(NCORES)))
    out = np.concatenate([r["y"] for r in res.results], axis=0)
    return out.reshape(B, S, D).astype(np.float32)


# version 10
# speedup vs baseline: 1.2542x; 1.1680x over previous
import os
import numpy as np
import ml_dtypes
import concourse.bass as bass
import concourse.mybir as mybir
from concourse.bass_utils import run_bass_kernel_spmd

F32 = mybir.dt.float32
BF16 = mybir.dt.bfloat16
FP8 = mybir.dt.float8e4
I32 = mybir.dt.int32
AF = mybir.ActivationFunctionType
OP = mybir.AluOpType

LN2 = 0.6931471805599453
D = 1024
H4 = 4096
NCORES = 8

C_EPS = 1e-5
C_HALF = 0.5
C_64_5 = 64.5
C_4096_5 = 4096.5
C_P35L2 = 35.0 * LN2
C_M35L2 = -35.0 * LN2
C_P19L2 = 19.0 * LN2
C_M19L2 = -19.0 * LN2
C_P26L2 = 26.0 * LN2
C_M26L2 = -26.0 * LN2
CONSTS = [C_EPS, C_HALF, C_64_5, C_4096_5, C_P35L2, C_M35L2,
          C_P19L2, C_M19L2, C_P26L2, C_M26L2]


class Prog:

    OWN = {"t": "t", "v": "v", "s": "s"}

    def __init__(self, nc):
        self.nc = nc
        self.q = {"t": [], "v": [], "s": [], "y": [], "g": []}
        self.sem = {}
        self.cnt = {}
        self.lastw = {e: {} for e in self.q}

    def mksem(self, name):
        self.sem[name] = self.nc.alloc_semaphore(name)
        self.cnt[name] = 0

    def latest(self, sem):
        return (sem, self.cnt[sem])

    def wait(self, eng, *deps):
        for dp in deps:
            if dp is None:
                continue
            if isinstance(dp, str):
                dp = (dp, self.cnt[dp])
            sem, c = dp
            if c <= 0:
                continue
            if self.OWN.get(eng) == sem:
                continue
            if self.lastw[eng].get(sem, 0) >= c:
                continue
            self.lastw[eng][sem] = c
            self.q[eng].append(lambda e, h=self.sem[sem], c=c: e.wait_ge(h, c))

    def op(self, eng, fn, inc=None, amt=1):
        if inc is None:
            self.q[eng].append(fn)
            return None
        h = self.sem[inc]
        self.cnt[inc] += amt
        self.q[eng].append(lambda e, fn=fn, h=h, a=amt: fn(e).then_inc(h, a))
        return (inc, self.cnt[inc])

    def barrier_on(self, eng):
        self.wait(eng, *list(self.sem.keys()))


def _fmcols(buf, w, tc):
    return buf.ap().rearrange("p (c w) -> p c w", w=w)[:, :, tc * 128:(tc + 1) * 128]


def build_program(tokc, debug=False, stage=99):
    assert tokc % 512 == 0
    ntile_a = tokc // 512
    nsup = 2 if ntile_a % 2 == 0 else 1
    ta_per_sup = ntile_a // nsup
    sup_tok = tokc // nsup
    ncc = sup_tok // 128
    tb_per_sup = ncc // 4

    nc = bass.Bass("TRN2", target_bir_lowering=False)

    x_h = nc.dram_tensor("x", [tokc, D], F32, kind="ExternalInput")
    y_h = nc.dram_tensor("y", [tokc, D], F32, kind="ExternalOutput")
    wq_h = nc.dram_tensor("wq", [D, D], F32, kind="ExternalInput")
    wk_h = nc.dram_tensor("wk", [D, D], F32, kind="ExternalInput")
    wv_h = nc.dram_tensor("wv", [D, D], F32, kind="ExternalInput")
    wo_h = nc.dram_tensor("wo", [D, D], F32, kind="ExternalInput")
    bq_h = nc.dram_tensor("bq", [D], F32, kind="ExternalInput")
    bk_h = nc.dram_tensor("bk", [D], F32, kind="ExternalInput")
    bv_h = nc.dram_tensor("bv", [D], F32, kind="ExternalInput")
    bo_h = nc.dram_tensor("bo", [1, D], F32, kind="ExternalInput")
    w1_h = nc.dram_tensor("w1", [H4, D], F32, kind="ExternalInput")
    b1_h = nc.dram_tensor("b1", [H4], F32, kind="ExternalInput")
    w2_h = nc.dram_tensor("w2", [D, H4], F32, kind="ExternalInput")
    b2_h = nc.dram_tensor("b2", [1, D], F32, kind="ExternalInput")
    id_h = nc.dram_tensor("cident", [128, 128], BF16, kind="ExternalInput")
    msel_h = nc.dram_tensor("msel", [D, 16], BF16, kind="ExternalInput")
    mselT_h = nc.dram_tensor("mselT", [16, D], BF16, kind="ExternalInput")
    onesp_h = nc.dram_tensor("onesp", [16, 128], BF16, kind="ExternalInput")
    wqb_h = nc.dram_tensor("wqb", [D, D], BF16)
    wkb_h = nc.dram_tensor("wkb", [D, D], BF16)
    wvb_h = nc.dram_tensor("wvb", [D, D], BF16)
    wob_h = nc.dram_tensor("wob", [D, D], BF16)
    w1t_h = nc.dram_tensor("w1t", [D, H4], BF16)
    w2qt_h = nc.dram_tensor("w2qt", [H4, D], FP8)
    dbg = {}
    if debug:
        for nm, shp, dt in [("d_ln1fm", [128, 8 * 512], BF16),
                            ("d_qkfm", [128, 8 * 512], BF16),
                            ("d_qkv", [128, 8 * 512], BF16),
                            ("d_a", [128, 4 * D], F32),
                            ("d_u", [128, ncc * D], BF16),
                            ("d_h", [128, 512], F32),
                            ("d_hq", [128, 32 * 512], FP8),
                            ("d_o", [128, 4 * D], F32),
                            ("d_oq", [128, 4 * D], F32),
                            ("d_w2q", [128, H4], BF16),
                            ("d_pg", [128, 2048], F32),
                            ("d_pr", [128, 2048], F32),
                            ("d_psrc", [128, H4], F32),
                            ("d_xt", [128, 4 * D], F32),
                            ("d_stats", [128, 16], F32)]:
            dbg[nm] = nc.dram_tensor(nm, shp, dt, kind="ExternalOutput")

    def isz(dt):
        return np.dtype(mybir.dt.np(dt)).itemsize

    A = lambda name, shape, dt: nc.alloc_sbuf_tensor(name, shape, dt)
    ident = A("ident", [128, 128], BF16)
    msel = A("mselb", [128, 8 * 16], BF16)
    mselT = A("mselTb", [16, 8 * 128], BF16)
    onesp = A("onespb", [16, 128], BF16)
    bqc = A("bqc", [128, 8], F32)
    bkc = A("bkc", [128, 8], F32)
    bvc = A("bvc", [128, 8], F32)
    b1c = A("b1c", [128, 32], F32)
    bo_bc = A("bo_bc", [128, D], F32)
    b2q_bc = A("b2q_bc", [128, D], F32)
    cpool = A("cpool", [128, len(CONSTS)], F32)
    stats = A("stats", [128, 64], F32)
    U = A("U", [128, ncc * D], BF16)

    ARENA_BYTES = 157696
    r = nc.bump_sbuf(ARENA_BYTES, align_bytes=512)
    assert r is not None, "SBUF arena allocation failed"
    ar_lo = r[0]

    class Arena:
        def __init__(self, base):
            self.base, self.off = base, 0
            self.map = {}

        def al(self, name, shape, dt):
            n = int(np.prod(shape[1:])) * isz(dt)
            self.off = (self.off + 63) // 64 * 64
            t = nc.alloc_sbuf_tensor_at(name, shape, dt, offset=self.base + self.off)
            self.map[name] = self.base + self.off
            self.off += n
            assert self.off <= ARENA_BYTES, f"arena overflow at {name}: {self.off}"
            return t

        def at(self, name, shape, dt, abs_off):
            return nc.alloc_sbuf_tensor_at(name, shape, dt, offset=abs_off)

    pa = Arena(ar_lo)
    p_stage = pa.al("p_stage", [128, H4], F32)
    p_stage2 = pa.al("p_stage2", [128, H4], F32)
    p_wbf = pa.al("p_wbf", [128, 2 * H4], BF16)
    p_qb = [pa.al(f"p_q{i}", [128, 2048], F32) for i in range(5)]
    p_qi = pa.al("p_qi", [128, 2048], I32)
    p_tr8 = pa.al("p_tr8", [128, 2 * 8 * 128], FP8)
    p_trb = pa.al("p_trb", [128, 2 * 8 * 128], BF16)
    p_row = pa.al("p_row", [16, D], F32)
    p_rowb = pa.al("p_rowb", [16, D], BF16)
    p_w1s = pa.al("p_w1s", [128, 2 * H4], BF16)

    aa = Arena(ar_lo)
    WQ = aa.al("WQ", [128, 8 * D], BF16)
    WK = aa.al("WK", [128, 8 * D], BF16)
    WV = aa.al("WV", [128, 8 * D], BF16)
    WO = aa.al("WO", [128, 8 * D], BF16)
    Xt = aa.al("Xt", [128, 4 * D], F32)
    ln1_tok = aa.al("ln1_tok", [128, D], BF16)
    ln1_fm = aa.al("ln1_fm", [128, 8 * 512], BF16)
    sqq = aa.al("sqq", [128, 512], F32)
    sqk = aa.al("sqk", [128, 512], F32)
    sqs = aa.al("sqs", [128, 512], F32)
    Vfm = aa.al("Vfm", [128, 8 * 512], BF16)
    QKfm = aa.al("QKfm", [128, 8 * 512], BF16)
    qkvfm = aa.al("qkvfm", [128, 8 * 512], BF16)
    coefb = aa.al("coefb", [16, 512], BF16)
    Atok = aa.al("Atok", [128, 4 * D], F32)
    lnA = aa.al("lnA", [128, D], F32)

    ba = Arena(ar_lo)
    W1Q = ba.al("W1Q", [128, 2 * 8 * 1024], BF16)
    W2QT = ba.al("W2QT", [128, 32 * D], FP8)
    ln2_fm = ba.al("ln2_fm", [128, 8 * 512], BF16)
    ln2_tok = ba.al("ln2_tok", [128, D], BF16)
    Hq = ba.al("Hq", [128, 32 * 512], FP8)
    Otok = ba.al("Otok", [128, 4 * D], F32)
    qsets = []
    for si in range(4):
        slots = {nm: ba.al(f"q{si}_{nm}", [128, 512], F32)
                 for nm in ["hv", "hc", "t64", "d2y"]}
        slots["g"] = ba.al(f"q{si}_g", [128, 512], BF16)
        slots["ii"] = ba.al(f"q{si}_ii", [128, 512], I32)
        qsets.append(slots)
    for si in (4, 5):
        k = si - 4
        ob = ba.map["Otok"] + k * 8192
        slots = {nm: ba.at(f"q{si}_{nm}", [128, 512], F32, ob + i * 2048)
                 for i, nm in enumerate(["hv", "hc", "t64", "d2y"])}
        slots["g"] = ba.at(f"q{si}_g", [128, 512], BF16,
                           ba.map["ln2_tok"] + k * 1024)
        slots["ii"] = ba.al(f"q{si}_ii", [128, 512], I32)
        qsets.append(slots)

    PT = nc.alloc_psum_tensor("psTR", [128, 1024], BF16)
    PS = [nc.alloc_psum_tensor(f"ps{i}", [128, 512], F32) for i in range(6)]
    PC = nc.alloc_psum_tensor("psC", [128, 512], F32)

    p = Prog(nc)
    for s in ["d", "g", "t", "v", "s", "o"]:
        p.mksem(s)
    for i, v in enumerate(CONSTS):
        nc.const_aps.register(F32, float(v), cpool[:, i:i + 1])

    def dma(q_, dst, src, inc, deps=(), **kw):
        p.wait(q_, *deps)
        return p.op(q_, lambda e, d=dst, s=src: e.dma_start(d, s, **kw),
                    inc=inc, amt=16)

    def ts1(e, out, in_, s1, o1):
        return e.tensor_single_scalar(out, in_, s1, o1)

    for i, v in enumerate(CONSTS):
        p.op("v", lambda e, i=i, v=v: e.memset(cpool[:, i:i + 1], float(v)))
    p.op("v", lambda e: e.memset(stats[:, :], 1.0), inc="v")

    dma("y", ident[:, :], id_h[:, :], "d")
    dma("y", msel.ap().rearrange("p (c h) -> p c h", h=16),
        msel_h.ap().rearrange("(c p) h -> p c h", p=128), "d")
    dma("y", mselT[:, :], mselT_h[:, :], "d")
    dma("y", onesp[:, :], onesp_h[:, :], "d")
    for dst, src in [(bqc, bq_h), (bkc, bk_h), (bvc, bv_h)]:
        p.op("y", lambda e, d=dst, s=src: e.dma_start(
            d[:, :], s.ap().rearrange("(c p) -> p c", p=128),
            allow_slow_non_contiguous=True), inc="d", amt=16)
    p.op("y", lambda e: e.dma_start(
        b1c[:, :], b1_h.ap().rearrange("(c p) -> p c", p=128),
        allow_slow_non_contiguous=True), inc="d", amt=16)

    def chains_stage1(lanes):
        for L in lanes:
            w = L["width"]; b = L["bufset"]
            for nm in ["hv", "hc", "t64", "d2y", "g"]:
                L[nm] = b[nm][:, :w]
            L["ii"] = b["ii"][:, :w]
            L["sl"] = 2.0 * LN2 if L["signed"] else LN2
        for L in lanes:
            if L.get("relu_bias") is not None:
                p.wait("s", L.get("psum_tok"), L.get("war_v"))
                L["tk1"] = p.op("s", lambda e, L=L: e.activation(
                    L["hv"], L["psum_src"], AF.Relu, bias=L["relu_bias"],
                    scale=1.0), inc="s")
                L["xs"] = L["hv"]
            elif L["signed"]:
                p.wait("s", L.get("src_tok"), L.get("war_v"))
                L["tk1"] = p.op("s", lambda e, L=L: e.activation(
                    L["d2y"], L["xsrc"], AF.Square, bias=0.0,
                    scale=2.0 ** 16), inc="s")
                L["xs"] = L["xsrc"]
            else:
                L["tk1"] = None
                L["xs"] = L["xsrc"]
        for L in lanes:
            if L.get("relu_bias") is not None:
                p.wait("v", L["tk1"], L.get("war_s"))
                L["tk2"] = p.op("v", lambda e, L=L: ts1(
                    e, L["hc"], L["hv"], L["clamp_lo"], OP.max), inc="v")
            elif L["signed"]:
                csq = float(L["clamp_lo"]) ** 2 * 2.0 ** 32
                p.wait("v", L["tk1"], L.get("war_s"))
                L["tk2"] = p.op("v", lambda e, L=L, c=csq: ts1(
                    e, L["hc"], L["d2y"], c, OP.max), inc="v")
            else:
                p.wait("v", L.get("src_tok"), L.get("war_s"))
                L["tk2"] = p.op("v", lambda e, L=L: ts1(
                    e, L["hc"], L["xsrc"], L["clamp_lo"], OP.max), inc="v")
        for L in lanes:
            p.wait("s", L["tk2"], L.get("war_v") if L["tk1"] is None else None)
            sc = 1.0 if L["signed"] else 2.0 ** 32
            p.op("s", lambda e, L=L, sc=sc: e.activation(
                L["t64"], L["hc"], AF.Ln, bias=0.0, scale=sc), inc="s")
        for L in lanes:
            L["tk4"] = p.op("s", lambda e, L=L: e.activation(
                L["ii"], L["t64"], AF.Copy, bias=0.0, scale=1.0 / L["sl"]), inc="s")
        for L in lanes:
            p.wait("v", L["tk4"])
            p.op("v", lambda e, L=L: e.scalar_tensor_tensor(
                L["d2y"], L["ii"], L["sl"], L["t64"], OP.mult, OP.is_gt), inc="v")
        for L in lanes:
            L["tk7"] = p.op("v", lambda e, L=L: e.tensor_tensor(
                L["hc"], L["ii"], L["d2y"], OP.subtract), inc="v")
        for L in lanes:
            p.wait("s", L["tk7"])
            p.op("s", lambda e, L=L: e.activation(
                L["t64"], L["hc"], AF.Exp, bias=L["bias_p"], scale=-LN2), inc="s")
        for L in lanes:
            L["g_tok"] = p.op("s", lambda e, L=L: e.activation(
                L["g"], L["hc"], AF.Exp, bias=L["bias_m"], scale=LN2), inc="s")
        return lanes

    def chains_stage2(lanes):
        for L in lanes:
            p.wait("v", L["g_tok"])
            p.op("v", lambda e, L=L: e.tensor_tensor(
                L["d2y"], L["xs"], L["t64"], OP.mult), inc="v")
        for L in lanes:
            L["tk11"] = p.op("v", lambda e, L=L: ts1(
                e, L["ii"], L["d2y"], L["off_c"], OP.add), inc="v")
        for L in lanes:
            L["end_s"] = p.op("v", lambda e, L=L: e.scalar_tensor_tensor(
                L["hc"], L["d2y"], L["off_c"], L["ii"], OP.add, OP.is_lt), inc="v")
        for L in lanes:
            p.op("v", lambda e, L=L: e.tensor_tensor(
                L["t64"], L["ii"], L["hc"], OP.subtract), inc="v")
        for L in lanes:
            if L["off_sub"]:
                L["end_v"] = p.op("v", lambda e, L=L: e.scalar_tensor_tensor(
                    L["dst"], L["t64"], -float(L["off_sub"]), L["g"],
                    OP.add, OP.mult), inc="v")
            else:
                L["end_v"] = p.op("v", lambda e, L=L: e.tensor_tensor(
                    L["dst"], L["t64"], L["g"], OP.mult), inc="v")
        return lanes

    def quant_chain(width, xsrc, dst_ap, bias_p, bias_m, off_c, off_sub,
                    clamp_lo, signed, bufset, psum_tok=None, psum_src=None,
                    relu_bias=None, war_v=None, war_s=None, src_tok=None):
        L = dict(width=width, bufset=bufset, signed=signed, bias_p=bias_p,
                 bias_m=bias_m, clamp_lo=clamp_lo, war_v=war_v, war_s=war_s,
                 xsrc=xsrc, src_tok=src_tok, psum_src=psum_src,
                 psum_tok=psum_tok, relu_bias=relu_bias,
                 dst=dst_ap, off_c=off_c, off_sub=off_sub)
        chains_stage1([L])
        chains_stage2([L])
        return {"relu": L["tk1"], "end_v": L["end_v"], "end_s": L["end_s"]}

    def layer_norm_stats(idx_list, src, dump, src_tok=None, prev=None,
                         dump_war=None):
        n = len(idx_list)
        prev = prev or {}
        acc_tok = None
        for i, cidx in enumerate(idx_list):
            sl = src[:, cidx * D:(cidx + 1) * D]
            p.wait("v", src_tok if i == 0 else None,
                   prev.get("rstd") if i == 0 else None)
            p.op("v", lambda e, sl=sl, i=i: e.reduce_sum(
                stats[:, i:i + 1], sl, axis=mybir.AxisListType.X), inc="v")
            p.wait("s", src_tok if i == 0 else None,
                   prev.get("var") if i == 0 else None,
                   dump_war if i == 0 else None)
            acc_tok = p.op("s", lambda e, sl=sl, i=i: e.activation(
                dump[:, :], sl, AF.Square, bias=0.0, scale=1.0,
                accum_out=stats[:, 16 + i:17 + i]), inc="s")
        p.wait("v", acc_tok)
        m_tok = p.op("v", lambda e, n=n: ts1(e, stats[:, 0:n], stats[:, 0:n],
                                             1.0 / D, OP.mult), inc="v")
        p.op("v", lambda e, n=n: ts1(e, stats[:, 16:16 + n], stats[:, 16:16 + n],
                                     1.0 / D, OP.mult), inc="v")
        p.wait("s", m_tok)
        msq_tok = p.op("s", lambda e, n=n: e.activation(
            stats[:, 48:48 + n], stats[:, 0:n], AF.Square, bias=0.0,
            scale=1.0), inc="s")
        p.wait("v", msq_tok)
        var_tok = p.op("v", lambda e, n=n: e.tensor_tensor(
            stats[:, 16:16 + n], stats[:, 16:16 + n], stats[:, 48:48 + n],
            OP.subtract), inc="v")
        p.wait("s", var_tok)
        p.op("s", lambda e, n=n: e.activation(stats[:, 48:48 + n], stats[:, 16:16 + n],
                                              AF.Ln, bias=C_EPS, scale=1.0), inc="s")
        rstd_tok = p.op("s", lambda e, n=n: e.activation(
            stats[:, 32:32 + n], stats[:, 48:48 + n], AF.Exp, bias=0.0,
            scale=-0.5), inc="s")
        return {"rstd": rstd_tok, "var": var_tok}

    w1_ld, w1_ev, w1_st = {}, {}, {}

    def w1_prefetch(dc):
        for hg in range(4):
            gidx = dc * 4 + hg
            slot = gidx % 8
            tok = None
            for j in range(8):
                hcn = hg * 8 + j
                tok = dma("g", p_w1s[:, slot * 1024 + j * 128: slot * 1024 + (j + 1) * 128],
                          w1_h[hcn * 128:(hcn + 1) * 128, dc * 128:(dc + 1) * 128],
                          "g", deps=[w1_ev.get(gidx - 8)] if j == 0 else ())
            w1_ld[gidx] = tok

    w1_prefetch(0)
    w1_prefetch(1)
    for wsrc, wdst in [(wq_h, wqb_h), (wk_h, wkb_h), (wv_h, wvb_h), (wo_h, wob_h)]:
        for c in range(4):
            dma("g", wdst[c * 256:(c + 1) * 256, :], wsrc[c * 256:(c + 1) * 256, :], "g")
    if stage < 1:
        return _finish(nc, p)
    p.op("v", lambda e: e.memset(p_row[:, :], 0.0), inc="v")
    tok_bo = dma("y", p_row[0:1, :], bo_h[:, :], "d", deps=["v"])
    p.wait("v", tok_bo)
    p.op("v", lambda e: e.tensor_copy(p_rowb[:, :], p_row[:, :]), inc="v")
    p.wait("t", "v")
    for half in range(2):
        p.op("t", lambda e, half=half: e.matmul(
            PS[0][:, :], onesp[:, :], p_rowb[:, half * 512:(half + 1) * 512],
            start=True, stop=True), inc="t")
        p.wait("v", "t")
        p.op("v", lambda e, half=half: e.tensor_copy(
            bo_bc[:, half * 512:(half + 1) * 512], PS[0][:, :]), inc="v")
        p.wait("t", "v")
    tok_b2 = dma("y", p_row[0:1, :], b2_h[:, :], "d", deps=["v"])
    p.wait("v", tok_b2)
    p.op("v", lambda e: e.tensor_copy(p_rowb[:, :], p_row[:, :]), inc="v")
    p.wait("t", "v")
    for half in range(2):
        p.op("t", lambda e, half=half: e.matmul(
            PS[0][:, :], onesp[:, :], p_rowb[:, half * 512:(half + 1) * 512],
            start=True, stop=True), inc="t")
        p.wait("v", "t")
        p.op("v", lambda e, half=half: e.tensor_copy(
            p_stage[:, half * 512:(half + 1) * 512], PS[0][:, :]), inc="v")
        p.wait("t", "v")
    pset = {nm: p_qb[k] for k, nm in enumerate(["hv", "hc", "t64", "d2y", "g"])}
    pset["ii"] = p_qi
    quant_chain(D, p_stage[:, :D], b2q_bc[:, :], C_P26L2, C_M26L2, C_4096_5, 4096,
                2.0 ** -14, signed=True, bufset=pset,
                war_v=p.latest("v"), war_s=p.latest("s"), src_tok=p.latest("v"))
    if stage < 2:
        return _finish(nc, p)
    prep_sets = []
    for si in range(4):
        ps_ = {nm: p_qb[k].ap()[:, si * 512:(si + 1) * 512]
               for k, nm in enumerate(["hv", "hc", "t64", "d2y", "g"])}
        ps_["ii"] = p_qi.ap()[:, si * 512:(si + 1) * 512]
        prep_sets.append(ps_)
    prep_hist = {0: None, 1: None, 2: None, 3: None}
    st8_tok = {}

    tr_last = {}

    def w2_transposes(dc):
        p.wait("t", prep_hist[2]["end_v"], prep_hist[3]["end_v"])
        for hg in range(4):
            gidx = dc * 4 + hg
            buf = gidx % 2
            tr_tok = None
            for j in range(8):
                hcn = hg * 8 + j
                src = p_wbf[:, (dc % 2) * H4 + hcn * 128: (dc % 2) * H4 + (hcn + 1) * 128]
                tr_tok = p.op("t", lambda e, src=src, j=j: e.transpose(
                    PT[:, j * 128:(j + 1) * 128], src, ident[:, :]),
                    inc="t" if j == 7 else None)
            tr_last[dc] = tr_tok
            p.wait("v", tr_tok, st8_tok.get(gidx - 2))
            ev = p.op("v", lambda e, buf=buf: e.tensor_copy(
                p_tr8[:, buf * 1024:(buf + 1) * 1024], PT[:, :]), inc="v")
            p.wait("t", ev)
            for j in range(8):
                hcn = hg * 8 + j
                st8_tok[gidx] = dma(
                    "y", w2qt_h[hcn * 128:(hcn + 1) * 128, dc * 128:(dc + 1) * 128],
                    p_tr8[:, buf * 1024 + j * 128: buf * 1024 + (j + 1) * 128],
                    "d", deps=[ev] if j == 0 else ())
            slot = gidx % 8
            p.wait("t", w1_ld[gidx])
            trb = None
            for j in range(8):
                src = p_w1s[:, slot * 1024 + j * 128: slot * 1024 + (j + 1) * 128]
                trb = p.op("t", lambda e, src=src, j=j: e.transpose(
                    PT[:, j * 128:(j + 1) * 128], src, ident[:, :]),
                    inc="t" if j == 7 else None)
            p.wait("v", trb, w1_st.get(gidx - 2))
            evb = p.op("v", lambda e, buf=buf: e.tensor_copy(
                p_trb[:, buf * 1024:(buf + 1) * 1024], PT[:, :]), inc="v")
            w1_ev[gidx] = evb
            p.wait("t", evb)
            p.wait("g", evb)
            w1_st[gidx] = dma("g", w1t_h[dc * 128:(dc + 1) * 128,
                                         hg * 1024:(hg + 1) * 1024],
                              p_trb[:, buf * 1024:(buf + 1) * 1024], "g")
        if 2 <= dc + 1 <= 7:
            w1_prefetch(dc + 1)

    stg = [p_stage, p_stage2]
    dtoks = {0: dma("y", p_stage[:, :], w2_h[0:128, :], "d", deps=["v", "s"]),
             1: dma("y", p_stage2[:, :], w2_h[128:256, :], "d")}
    dc_flush = {}
    for dc in range(8):
        if 1 <= dc <= 6:
            dtoks[dc + 1] = dma("y", stg[(dc + 1) % 2][:, :],
                                w2_h[(dc + 1) * 128:(dc + 2) * 128, :], "d",
                                deps=dc_flush[dc - 1])
        p.wait("v", tr_last.get(dc - 2))
        pend = None
        for m2 in range(4):
            lanes = []
            for j in range(2):
                sl8 = 2 * m2 + j
                si = 2 * (m2 % 2) + j
                hist = prep_hist[si]
                lanes.append(dict(
                    width=512, bufset=prep_sets[si], signed=True,
                    bias_p=C_P19L2, bias_m=C_M19L2, clamp_lo=2.0 ** -6,
                    xsrc=stg[dc % 2][:, sl8 * 512:(sl8 + 1) * 512],
                    src_tok=dtoks[dc],
                    war_v=hist["end_v"] if hist else None,
                    war_s=hist["end_s"] if hist else None,
                    dst=p_wbf[:, (dc % 2) * H4 + sl8 * 512: (dc % 2) * H4 + (sl8 + 1) * 512],
                    off_c=C_64_5, off_sub=64, si=si))
            chains_stage1(lanes)
            if pend is not None:
                chains_stage2(pend)
                for L in pend:
                    prep_hist[L["si"]] = {"end_v": L["end_v"], "end_s": L["end_s"]}
            if dc > 0 and m2 == 1:
                w2_transposes(dc - 1)
            pend = lanes
        chains_stage2(pend)
        for L in pend:
            prep_hist[L["si"]] = {"end_v": L["end_v"], "end_s": L["end_s"]}
        dc_flush[dc] = [prep_hist[2]["end_v"], prep_hist[3]["end_v"],
                        prep_hist[2]["end_s"], prep_hist[3]["end_s"]]
    w2_transposes(7)
    if debug:
        p.wait("y", "v")
        dma("y", dbg["d_w2q"][:, :], p_wbf[:, 0:H4], "o")
        dma("y", dbg["d_pg"][:, :], p_qb[4][:, :], "o")
        dma("y", dbg["d_pr"][:, :], p_qb[2][:, :], "o")
        dma("y", dbg["d_psrc"][:, :], p_stage[:, :], "o")
    if stage < 3:
        return _finish(nc, p)

    if stage < 4:
        return _finish(nc, p)
    stats_prev = None
    tok_u_ready = None
    for sup in range(nsup):
        base_tok = sup * sup_tok
        p.barrier_on("y")
        for Wd, Ws in [(WQ, wqb_h), (WK, wkb_h), (WV, wvb_h), (WO, wob_h)]:
            for c in range(8):
                dma("y", Wd[:, c * D:(c + 1) * D], Ws[c * 128:(c + 1) * 128, :], "d")
        tok_wload = p.latest("d")
        tok_qkv_last = None
        tok_wo_last = None
        tok_coef_last = None
        tok_bc_last = None
        tok_uadd = [None] * 4
        for lt in range(ta_per_sup):
            t0 = base_tok + lt * 512
            xt_tok = []
            for tc in range(4):
                xt_tok.append(dma("y", Xt[:, tc * D:(tc + 1) * D],
                                  x_h[t0 + tc * 128:t0 + tc * 128 + 128, :], "d",
                                  deps=[tok_uadd[tc],
                                        stats_prev["rstd"] if stats_prev else None]))
            stats_prev = layer_norm_stats([0, 1, 2, 3], Xt, lnA,
                                          src_tok=xt_tok[3], prev=stats_prev,
                                          dump_war=tok_uadd[3])
            rstd_tok = stats_prev["rstd"]
            tr_tok = None
            ev_toks = []
            for tc in range(4):
                p.wait("v", rstd_tok, tok_qkv_last)
                ap_tok = p.op("v", lambda e, tc=tc: e.tensor_scalar(
                    ln1_tok[:, :], Xt[:, tc * D:(tc + 1) * D],
                    stats[:, tc:tc + 1], stats[:, 32 + tc:33 + tc],
                    OP.subtract, OP.mult), inc="v")
                p.wait("t", ap_tok, ev_toks[-1] if ev_toks else None)
                for fc in range(8):
                    tr_tok = p.op("t", lambda e, fc=fc: e.transpose(
                        PT[:, fc * 128:(fc + 1) * 128],
                        ln1_tok[:, fc * 128:(fc + 1) * 128], ident[:, :]),
                        inc="t" if fc == 7 else None)
                p.wait("v", tr_tok)
                ev_toks.append(p.op("v", lambda e, tc=tc: e.tensor_copy(
                    _fmcols(ln1_fm, 512, tc), PT[:, :]), inc="v"))
            if debug:
                dma("y", dbg["d_ln1fm"][:, :], ln1_fm[:, :], "o", deps=[ev_toks[-1]])
            rd = {}
            exp_tok = {}
            vfm_tok = {}
            sqs_tok = None
            for oc in range(8):
                par = oc % 2
                psq, psk, psv = PS[3 * par], PS[3 * par + 1], PS[3 * par + 2]
                if oc >= 2:
                    p.wait("t", *rd[oc - 2])
                else:
                    p.wait("t", ev_toks[-1], tok_wload, tok_wo_last)
                mm_toks = []
                for W_, psx in [(WQ, psq), (WK, psk), (WV, psv)]:
                    for fc in range(8):
                        lhsT = W_[:, fc * D + oc * 128: fc * D + (oc + 1) * 128]
                        rhs = ln1_fm[:, fc * 512:(fc + 1) * 512]
                        tk = p.op("t", lambda e, lhsT=lhsT, rhs=rhs, psx=psx, fc=fc: e.matmul(
                            psx[:, :], lhsT, rhs, start=(fc == 0), stop=(fc == 7)),
                            inc="t" if fc == 7 else None)
                        if tk:
                            mm_toks.append(tk)
                tok_qkv_last = mm_toks[-1]
                p.wait("v", mm_toks[2])
                vfm_tok[oc] = p.op("v", lambda e, oc=oc, psv=psv: ts1(
                    e, Vfm[:, oc * 512:(oc + 1) * 512], psv[:, :],
                    bvc[:, oc:oc + 1], OP.add), inc="v")
                p.wait("s", mm_toks[0], sqs_tok)
                p.op("s", lambda e, oc=oc, psq=psq: e.activation(
                    sqq[:, :], psq[:, :], AF.Square, bias=bqc[:, oc:oc + 1],
                    scale=1.0), inc="s")
                p.wait("s", mm_toks[1])
                sq_tok = p.op("s", lambda e, oc=oc, psk=psk: e.activation(
                    sqk[:, :], psk[:, :], AF.Square, bias=bkc[:, oc:oc + 1],
                    scale=1.0), inc="s")
                p.wait("v", sq_tok, exp_tok.get(oc - 1), tok_coef_last if oc == 0 else None)
                sqs_tok = p.op("v", lambda e: e.tensor_add(sqs[:, :], sqq[:, :],
                                                           sqk[:, :]), inc="v")
                p.wait("s", sqs_tok)
                exp_tok[oc] = p.op("s", lambda e, oc=oc: e.activation(
                    QKfm[:, oc * 512:(oc + 1) * 512], sqs[:, :], AF.Exp,
                    bias=0.0, scale=-0.5), inc="s")
                rd[oc] = (vfm_tok[oc], sq_tok)
            if debug:
                dma("y", dbg["d_qkfm"][:, :], QKfm[:, :], "o", deps=[exp_tok[7]])
            p.wait("t", tok_bc_last)
            for oc in range(8):
                p.wait("t", exp_tok[oc])
                tok_coef_last = p.op("t", lambda e, oc=oc: e.matmul(
                    PC[0:16, :], msel[:, oc * 16:(oc + 1) * 16],
                    QKfm[:, oc * 512:(oc + 1) * 512],
                    start=(oc == 0), stop=(oc == 7)),
                    inc="t" if oc == 7 else None)
            p.wait("v", tok_coef_last)
            coefb_tok = p.op("v", lambda e: e.tensor_copy(coefb[:, :], PC[0:16, :]), inc="v")
            stt_toks = {}
            for oc in range(8):
                bank = PS[3] if oc % 2 == 0 else PC
                p.wait("t", coefb_tok if oc < 2 else stt_toks[oc - 2],
                       rd[7][1] if oc == 0 else None)
                bc_tok = p.op("t", lambda e, oc=oc, bank=bank: e.matmul(
                    bank[0:128, :], mselT[:, oc * 128:(oc + 1) * 128],
                    coefb[:, :], start=True, stop=True), inc="t")
                p.wait("v", bc_tok, tok_wo_last if oc == 0 else None)
                stt_toks[oc] = p.op("v", lambda e, oc=oc, bank=bank: e.scalar_tensor_tensor(
                    qkvfm[:, oc * 512:(oc + 1) * 512], bank[0:128, :], 1.0,
                    Vfm[:, oc * 512:(oc + 1) * 512], OP.bypass, OP.mult), inc="v")
            tok_bc_last_list = []
            ev2 = {}
            for tp in range(4):
                for oh in range(2):
                    pse = PS[4 + oh]
                    p.wait("t", stt_toks[7], ev2.get(tp - 1),
                           rd[7][1] if tp == 0 else None,
                           rd[7][0] if tp == 0 else None)
                    for fc in range(8):
                        lhsT = qkvfm[:, fc * 512 + tp * 128: fc * 512 + (tp + 1) * 128]
                        rhs = WO[:, fc * D + oh * 512: fc * D + (oh + 1) * 512]
                        tok_wo_last = p.op("t", lambda e, lhsT=lhsT, rhs=rhs, pse=pse, fc=fc: e.matmul(
                            pse[:, :], lhsT, rhs, start=(fc == 0), stop=(fc == 7)),
                            inc="t" if fc == 7 else None)
                    p.wait("v", tok_wo_last, tok_uadd[tp])
                    ev2[tp] = p.op("v", lambda e, tp=tp, oh=oh, pse=pse: e.tensor_tensor(
                        Atok[:, tp * D + oh * 512: tp * D + (oh + 1) * 512],
                        pse[:, :], bo_bc[:, oh * 512:(oh + 1) * 512],
                        OP.add), inc="v")
            if debug:
                dma("y", dbg["d_a"][:, :], Atok[:, :], "o", deps=[ev2[3]])
            stats_prev = layer_norm_stats([0, 1, 2, 3], Atok, lnA,
                                          src_tok=ev2[3], prev=stats_prev)
            rstd_tok = stats_prev["rstd"]
            for tc in range(4):
                cc = lt * 4 + tc
                p.wait("v", rstd_tok)
                p.op("v", lambda e, tc=tc: e.tensor_scalar(
                    lnA[:, :], Atok[:, tc * D:(tc + 1) * D],
                    stats[:, tc:tc + 1], stats[:, 32 + tc:33 + tc],
                    OP.subtract, OP.mult), inc="v")
                tok_uadd[tc] = p.op("v", lambda e, tc=tc, cc=cc: e.tensor_tensor(
                    U[:, cc * D:(cc + 1) * D], Xt[:, tc * D:(tc + 1) * D],
                    lnA[:, :], OP.add), inc="v")
                tok_u_ready = tok_uadd[tc]
        if debug:
            dma("y", dbg["d_u"][:, :], U[:, :], "o", deps=[tok_uadd[3]])
        if stage < 5:
            continue
        p.barrier_on("y")
        for c in range(32):
            dma("y", W2QT[:, c * D:(c + 1) * D], w2qt_h[c * 128:(c + 1) * 128, :], "d")
        tok_wload = p.latest("d")
        qld, qmm_last = {}, {}

        def w1_qload(gq):
            q4 = gq % 4
            slot = gq % 2
            tok = None
            for dcq in range(8):
                tok = dma("y", W1Q[:, slot * 8192 + dcq * 1024:
                                   slot * 8192 + (dcq + 1) * 1024],
                          w1t_h[dcq * 128:(dcq + 1) * 128, q4 * 1024:(q4 + 1) * 1024],
                          "d", deps=[qmm_last.get(gq - 2)] if dcq == 0 else ())
            qld[gq] = tok

        w1_qload(0)
        w1_qload(1)
        tok_b2_last = None
        tok_mm1_last = None
        tok_ost = {}
        tr_tok = None
        chain_hist = {si: None for si in range(6)}
        for bt in range(tb_per_sup):
            cc0 = bt * 4
            stats_prev = layer_norm_stats([cc0, cc0 + 1, cc0 + 2, cc0 + 3], U,
                                          ln2_tok, src_tok=tok_u_ready,
                                          prev=stats_prev, dump_war=tr_tok)
            rstd_tok = stats_prev["rstd"]
            ev_toks = []
            for k in range(4):
                p.wait("v", rstd_tok, tok_mm1_last)
                ap_tok = p.op("v", lambda e, k=k, cc0=cc0: e.tensor_scalar(
                    ln2_tok[:, :], U[:, (cc0 + k) * D:(cc0 + k + 1) * D],
                    stats[:, k:k + 1], stats[:, 32 + k:33 + k],
                    OP.subtract, OP.mult), inc="v")
                p.wait("t", ap_tok, ev_toks[-1] if ev_toks else None,
                       tok_b2_last if bt > 0 and k == 0 else None)
                for fc in range(8):
                    tr_tok = p.op("t", lambda e, fc=fc: e.transpose(
                        PT[:, fc * 128:(fc + 1) * 128],
                        ln2_tok[:, fc * 128:(fc + 1) * 128], ident[:, :]),
                        inc="t" if fc == 7 else None)
                p.wait("v", tr_tok)
                ev_toks.append(p.op("v", lambda e, k=k: e.tensor_copy(
                    _fmcols(ln2_fm, 512, k), PT[:, :]), inc="v"))
            relu_toks = {}
            pend = None
            last_v = None
            for g_ in range(11):
                lanes = []
                for ln_ in range(3):
                    hcn = 3 * g_ + ln_
                    if hcn >= 32:
                        break
                    gq = bt * 4 + hcn // 8
                    slot = gq % 2
                    psh = PS[hcn % 4]
                    p.wait("t", ev_toks[-1] if hcn == 0 else None,
                           tok_wload if hcn == 0 else None,
                           qld[gq] if hcn % 8 == 0 else None,
                           relu_toks.get(hcn - 4))
                    mmtok = None
                    for dc in range(8):
                        o0 = slot * 8192 + dc * 1024 + (hcn % 8) * 128
                        lhsT = W1Q[:, o0:o0 + 128]
                        rhs = ln2_fm[:, dc * 512:(dc + 1) * 512]
                        mmtok = p.op("t", lambda e, lhsT=lhsT, rhs=rhs, psh=psh, dc=dc: e.matmul(
                            psh[:, :], lhsT, rhs, start=(dc == 0), stop=(dc == 7)),
                            inc="t" if dc == 7 else None)
                    tok_mm1_last = mmtok
                    if hcn % 8 == 7:
                        qmm_last[gq] = mmtok
                        if gq + 2 < 4 * tb_per_sup:
                            w1_qload(gq + 2)
                    si = 3 * (g_ % 2) + ln_
                    hist = chain_hist[si]
                    lanes.append(dict(
                        width=512, bufset=qsets[si], signed=False,
                        bias_p=C_P35L2, bias_m=C_M35L2, clamp_lo=2.0 ** -6,
                        psum_src=psh[:, :], psum_tok=mmtok,
                        relu_bias=b1c[:, hcn:hcn + 1],
                        war_v=hist["end_v"] if hist else None,
                        war_s=hist["end_s"] if hist else None,
                        dst=Hq[:, hcn * 512:(hcn + 1) * 512],
                        off_c=C_HALF, off_sub=0, hcn=hcn, si=si))
                chains_stage1(lanes)
                for L in lanes:
                    relu_toks[L["hcn"]] = L["tk1"]
                if pend is not None:
                    chains_stage2(pend)
                    for L in pend:
                        chain_hist[L["si"]] = {"end_v": L["end_v"], "end_s": L["end_s"]}
                    last_v = pend[-1]["end_v"]
                pend = lanes
            chains_stage2(pend)
            for L in pend:
                chain_hist[L["si"]] = {"end_v": L["end_v"], "end_s": L["end_s"]}
            last_v = pend[-1]["end_v"]
            if debug and bt == 0:
                dma("y", dbg["d_h"][:, :], qsets[1]["hv"][:, :512], "o",
                    deps=[last_v])
                dma("y", dbg["d_hq"][:, :], Hq[:, :], "o", deps=[last_v])
            ev3 = {}
            gi = 0
            for tp in range(4):
                for dh in range(2):
                    pso = PS[2 + gi % 4]
                    if gi == 0:
                        p.wait("t", last_v, relu_toks[30])
                    elif gi == 1:
                        p.wait("t", relu_toks[31])
                    elif gi >= 4:
                        p.wait("t", ev3[gi - 4])
                    for hcn in range(32):
                        lhsT = Hq[:, hcn * 512 + tp * 128: hcn * 512 + (tp + 1) * 128]
                        rhs = W2QT[:, hcn * D + dh * 512: hcn * D + (dh + 1) * 512]
                        tok_b2_last = p.op("t", lambda e, lhsT=lhsT, rhs=rhs, pso=pso, hcn=hcn: e.matmul(
                            pso[:, :], lhsT, rhs, start=(hcn == 0), stop=(hcn == 31)),
                            inc="t" if hcn == 31 else None)
                    p.wait("v", tok_b2_last, tok_ost.get(gi // 2))
                    ev3[gi] = p.op("v", lambda e, tp=tp, dh=dh, pso=pso: e.tensor_tensor(
                        Otok[:, tp * D + dh * 512: tp * D + (dh + 1) * 512],
                        pso[:, :], b2q_bc[:, dh * 512:(dh + 1) * 512],
                        OP.add), inc="v")
                    gi += 1
            if debug and bt == 0:
                dma("y", dbg["d_o"][:, :], Otok[:, :], "o", deps=[ev3[7]])
            pend = None
            st2last = None
            for m_ in range(4):
                lanes = []
                for j in range(2):
                    seg = 2 * m_ + j
                    si = 2 * (m_ % 2) + j
                    hist = chain_hist[si]
                    lanes.append(dict(
                        width=512, bufset=qsets[si], signed=True,
                        bias_p=C_P26L2, bias_m=C_M26L2, clamp_lo=2.0 ** -14,
                        xsrc=Otok[:, seg * 512:(seg + 1) * 512],
                        src_tok=ev3[seg],
                        war_v=hist["end_v"] if hist else None,
                        war_s=hist["end_s"] if hist else None,
                        dst=Otok[:, seg * 512:(seg + 1) * 512],
                        off_c=C_4096_5, off_sub=4096, si=si))
                chains_stage1(lanes)
                if pend is not None:
                    chains_stage2(pend)
                    for L in pend:
                        chain_hist[L["si"]] = {"end_v": L["end_v"], "end_s": L["end_s"]}
                pend = lanes
            chains_stage2(pend)
            for L in pend:
                chain_hist[L["si"]] = {"end_v": L["end_v"], "end_s": L["end_s"]}
            st2last = pend[-1]["end_v"]
            if debug and bt == 0:
                dma("y", dbg["d_oq"][:, :], Otok[:, :], "o", deps=[st2last])
            stats_prev = layer_norm_stats([0, 1, 2, 3], Otok, ln2_tok,
                                          src_tok=st2last, prev=stats_prev,
                                          dump_war=tr_tok)
            rstd_tok = stats_prev["rstd"]
            for k in range(4):
                cc = cc0 + k
                p.wait("v", rstd_tok, tok_ost.get(k))
                p.op("v", lambda e, k=k: e.tensor_scalar(
                    Otok[:, k * D:(k + 1) * D], Otok[:, k * D:(k + 1) * D],
                    stats[:, k:k + 1], stats[:, 32 + k:33 + k],
                    OP.subtract, OP.mult), inc="v")
                ytok = p.op("v", lambda e, k=k, cc=cc: e.tensor_tensor(
                    Otok[:, k * D:(k + 1) * D], U[:, cc * D:(cc + 1) * D],
                    Otok[:, k * D:(k + 1) * D], OP.add), inc="v")
                r0 = base_tok + cc * 128
                tok_ost[k] = dma("y", y_h[r0:r0 + 128, :], Otok[:, k * D:(k + 1) * D],
                                 "o", deps=[ytok])

    p.wait("y", "o")
    return _finish(nc, p)


def _finish(nc, p):
    p.barrier_on("y")
    with nc.Block() as block:
        block.tensor(lambda e: [fn(e) for fn in p.q["t"]])
        block.vector(lambda e: [fn(e) for fn in p.q["v"]])
        block.scalar(lambda e: [fn(e) for fn in p.q["s"]])
        block.sync(lambda e: [fn(e) for fn in p.q["y"]])
        block.gpsimd(lambda e: [fn(e) for fn in p.q["g"]])
    return nc


def _make_const_inputs():
    hd = 64
    msel = np.zeros((D, 16), dtype=np.float32)
    for d_ in range(D):
        msel[d_, d_ // hd] = 1.0
    onesp = np.zeros((16, 128), dtype=np.float32)
    onesp[0, :] = 1.0
    return {
        "cident": np.eye(128, dtype=np.float32).astype(ml_dtypes.bfloat16),
        "msel": msel.astype(ml_dtypes.bfloat16),
        "mselT": msel.T.copy().astype(ml_dtypes.bfloat16),
        "onesp": onesp.astype(ml_dtypes.bfloat16),
    }


def kernel(L0_attn_x, L0_attn_Wq, L0_attn_bq, L0_attn_Wk, L0_attn_bk,
           L0_attn_Wv, L0_attn_bv, L0_attn_Wo, L0_attn_bo,
           L0_ffn_w1, L0_ffn_b1, L0_ffn_w2, L0_ffn_b2):
    x = np.ascontiguousarray(np.asarray(L0_attn_x, dtype=np.float32))
    B, S, Dm = x.shape
    tok = B * S
    assert Dm == D and tok % NCORES == 0
    tokc = tok // NCORES
    xf = x.reshape(tok, D)

    nc = build_program(tokc)
    consts = _make_const_inputs()
    common = {
        "wq": np.ascontiguousarray(L0_attn_Wq, dtype=np.float32),
        "wk": np.ascontiguousarray(L0_attn_Wk, dtype=np.float32),
        "wv": np.ascontiguousarray(L0_attn_Wv, dtype=np.float32),
        "wo": np.ascontiguousarray(L0_attn_Wo, dtype=np.float32),
        "bq": np.ascontiguousarray(L0_attn_bq, dtype=np.float32),
        "bk": np.ascontiguousarray(L0_attn_bk, dtype=np.float32),
        "bv": np.ascontiguousarray(L0_attn_bv, dtype=np.float32),
        "bo": np.ascontiguousarray(L0_attn_bo, dtype=np.float32).reshape(1, D),
        "w1": np.ascontiguousarray(L0_ffn_w1, dtype=np.float32),
        "b1": np.ascontiguousarray(L0_ffn_b1, dtype=np.float32),
        "w2": np.ascontiguousarray(L0_ffn_w2, dtype=np.float32),
        "b2": np.ascontiguousarray(L0_ffn_b2, dtype=np.float32).reshape(1, D),
        **consts,
    }
    in_maps = []
    for c in range(NCORES):
        m = dict(common)
        m["x"] = np.ascontiguousarray(xf[c * tokc:(c + 1) * tokc])
        in_maps.append(m)
    res = run_bass_kernel_spmd(nc, in_maps, core_ids=list(range(NCORES)))
    out = np.concatenate([r["y"] for r in res.results], axis=0)
    return out.reshape(B, S, D).astype(np.float32)


# version 11
# speedup vs baseline: 1.2622x; 1.0064x over previous
import os
import numpy as np
import ml_dtypes
import concourse.bass as bass
import concourse.mybir as mybir
from concourse.bass_utils import run_bass_kernel_spmd

F32 = mybir.dt.float32
BF16 = mybir.dt.bfloat16
FP8 = mybir.dt.float8e4
I32 = mybir.dt.int32
AF = mybir.ActivationFunctionType
OP = mybir.AluOpType

LN2 = 0.6931471805599453
D = 1024
H4 = 4096
NCORES = 8

C_EPS = 1e-5
C_HALF = 0.5
C_64_5 = 64.5
C_4096_5 = 4096.5
C_P35L2 = 35.0 * LN2
C_M35L2 = -35.0 * LN2
C_P19L2 = 19.0 * LN2
C_M19L2 = -19.0 * LN2
C_P26L2 = 26.0 * LN2
C_M26L2 = -26.0 * LN2
CONSTS = [C_EPS, C_HALF, C_64_5, C_4096_5, C_P35L2, C_M35L2,
          C_P19L2, C_M19L2, C_P26L2, C_M26L2]


class Prog:

    OWN = {"t": "t", "v": "v", "s": "s"}

    def __init__(self, nc):
        self.nc = nc
        self.q = {"t": [], "v": [], "s": [], "y": [], "g": []}
        self.sem = {}
        self.cnt = {}
        self.lastw = {e: {} for e in self.q}

    def mksem(self, name):
        self.sem[name] = self.nc.alloc_semaphore(name)
        self.cnt[name] = 0

    def latest(self, sem):
        return (sem, self.cnt[sem])

    def wait(self, eng, *deps):
        for dp in deps:
            if dp is None:
                continue
            if isinstance(dp, str):
                dp = (dp, self.cnt[dp])
            sem, c = dp
            if c <= 0:
                continue
            if self.OWN.get(eng) == sem:
                continue
            if self.lastw[eng].get(sem, 0) >= c:
                continue
            self.lastw[eng][sem] = c
            self.q[eng].append(lambda e, h=self.sem[sem], c=c: e.wait_ge(h, c))

    def op(self, eng, fn, inc=None, amt=1):
        if inc is None:
            self.q[eng].append(fn)
            return None
        h = self.sem[inc]
        self.cnt[inc] += amt
        self.q[eng].append(lambda e, fn=fn, h=h, a=amt: fn(e).then_inc(h, a))
        return (inc, self.cnt[inc])

    def barrier_on(self, eng):
        self.wait(eng, *list(self.sem.keys()))


def _fmcols(buf, w, tc):
    return buf.ap().rearrange("p (c w) -> p c w", w=w)[:, :, tc * 128:(tc + 1) * 128]


def build_program(tokc, debug=False, stage=99):
    assert tokc % 512 == 0
    ntile_a = tokc // 512
    nsup = 2 if ntile_a % 2 == 0 else 1
    ta_per_sup = ntile_a // nsup
    sup_tok = tokc // nsup
    ncc = sup_tok // 128
    tb_per_sup = ncc // 4

    nc = bass.Bass("TRN2", target_bir_lowering=False)

    x_h = nc.dram_tensor("x", [tokc, D], F32, kind="ExternalInput")
    y_h = nc.dram_tensor("y", [tokc, D], F32, kind="ExternalOutput")
    wq_h = nc.dram_tensor("wq", [D, D], F32, kind="ExternalInput")
    wk_h = nc.dram_tensor("wk", [D, D], F32, kind="ExternalInput")
    wv_h = nc.dram_tensor("wv", [D, D], F32, kind="ExternalInput")
    wo_h = nc.dram_tensor("wo", [D, D], F32, kind="ExternalInput")
    bq_h = nc.dram_tensor("bq", [D], F32, kind="ExternalInput")
    bk_h = nc.dram_tensor("bk", [D], F32, kind="ExternalInput")
    bv_h = nc.dram_tensor("bv", [D], F32, kind="ExternalInput")
    bo_h = nc.dram_tensor("bo", [1, D], F32, kind="ExternalInput")
    w1_h = nc.dram_tensor("w1", [H4, D], F32, kind="ExternalInput")
    b1_h = nc.dram_tensor("b1", [H4], F32, kind="ExternalInput")
    w2_h = nc.dram_tensor("w2", [D, H4], F32, kind="ExternalInput")
    b2_h = nc.dram_tensor("b2", [1, D], F32, kind="ExternalInput")
    id_h = nc.dram_tensor("cident", [128, 128], BF16, kind="ExternalInput")
    msel_h = nc.dram_tensor("msel", [D, 16], BF16, kind="ExternalInput")
    mselT_h = nc.dram_tensor("mselT", [16, D], BF16, kind="ExternalInput")
    onesp_h = nc.dram_tensor("onesp", [16, 128], BF16, kind="ExternalInput")
    wqb_h = nc.dram_tensor("wqb", [D, D], BF16)
    wkb_h = nc.dram_tensor("wkb", [D, D], BF16)
    wvb_h = nc.dram_tensor("wvb", [D, D], BF16)
    wob_h = nc.dram_tensor("wob", [D, D], BF16)
    w1t_h = nc.dram_tensor("w1t", [D, H4], BF16)
    w2qt_h = nc.dram_tensor("w2qt", [H4, D], FP8)
    dbg = {}
    if debug:
        for nm, shp, dt in [("d_ln1fm", [128, 8 * 512], BF16),
                            ("d_qkfm", [128, 8 * 512], BF16),
                            ("d_qkv", [128, 8 * 512], BF16),
                            ("d_a", [128, 4 * D], F32),
                            ("d_u", [128, ncc * D], BF16),
                            ("d_h", [128, 512], F32),
                            ("d_hq", [128, 32 * 512], FP8),
                            ("d_o", [128, 4 * D], F32),
                            ("d_oq", [128, 4 * D], F32),
                            ("d_w2q", [128, H4], BF16),
                            ("d_pg", [128, 2048], F32),
                            ("d_pr", [128, 2048], F32),
                            ("d_psrc", [128, H4], F32),
                            ("d_xt", [128, 4 * D], F32),
                            ("d_stats", [128, 16], F32)]:
            dbg[nm] = nc.dram_tensor(nm, shp, dt, kind="ExternalOutput")

    def isz(dt):
        return np.dtype(mybir.dt.np(dt)).itemsize

    A = lambda name, shape, dt: nc.alloc_sbuf_tensor(name, shape, dt)
    ident = A("ident", [128, 128], BF16)
    msel = A("mselb", [128, 8 * 16], BF16)
    mselT = A("mselTb", [16, 8 * 128], BF16)
    onesp = A("onespb", [16, 128], BF16)
    bqc = A("bqc", [128, 8], F32)
    bkc = A("bkc", [128, 8], F32)
    bvc = A("bvc", [128, 8], F32)
    b1c = A("b1c", [128, 32], F32)
    bo_bc = A("bo_bc", [128, D], F32)
    b2q_bc = A("b2q_bc", [128, D], F32)
    cpool = A("cpool", [128, len(CONSTS)], F32)
    stats = A("stats", [128, 64], F32)
    b2stage = A("b2stage", [128, D], F32)
    U = A("U", [128, ncc * D], BF16)

    ARENA_BYTES = 157696
    r = nc.bump_sbuf(ARENA_BYTES, align_bytes=512)
    assert r is not None, "SBUF arena allocation failed"
    ar_lo = r[0]

    class Arena:
        def __init__(self, base):
            self.base, self.off = base, 0
            self.map = {}

        def al(self, name, shape, dt):
            n = int(np.prod(shape[1:])) * isz(dt)
            self.off = (self.off + 63) // 64 * 64
            t = nc.alloc_sbuf_tensor_at(name, shape, dt, offset=self.base + self.off)
            self.map[name] = self.base + self.off
            self.off += n
            assert self.off <= ARENA_BYTES, f"arena overflow at {name}: {self.off}"
            return t

        def at(self, name, shape, dt, abs_off):
            return nc.alloc_sbuf_tensor_at(name, shape, dt, offset=abs_off)

    pa = Arena(ar_lo)
    p_stage = pa.al("p_stage", [128, H4], F32)
    p_stage2 = pa.al("p_stage2", [128, H4], F32)
    p_wbf = pa.al("p_wbf", [128, 2 * H4], BF16)
    p_qb = [pa.al(f"p_q{i}", [128, 2048], F32) for i in range(5)]
    p_qi = pa.al("p_qi", [128, 2048], I32)
    p_tr8 = pa.al("p_tr8", [128, 2 * 8 * 128], FP8)
    p_trb = pa.al("p_trb", [128, 2 * 8 * 128], BF16)
    p_row = pa.al("p_row", [16, D], F32)
    p_rowb = pa.al("p_rowb", [16, D], BF16)
    p_w1s = pa.al("p_w1s", [128, 2 * H4], BF16)

    aa = Arena(ar_lo)
    WQ = aa.al("WQ", [128, 8 * D], BF16)
    WK = aa.al("WK", [128, 8 * D], BF16)
    WV = aa.al("WV", [128, 8 * D], BF16)
    WO = aa.al("WO", [128, 8 * D], BF16)
    Xt = aa.al("Xt", [128, 4 * D], F32)
    ln1_tok = aa.al("ln1_tok", [128, D], BF16)
    ln1_fm = aa.al("ln1_fm", [128, 8 * 512], BF16)
    sqq = aa.al("sqq", [128, 512], F32)
    sqk = aa.al("sqk", [128, 512], F32)
    sqs = aa.al("sqs", [128, 512], F32)
    Vfm = aa.al("Vfm", [128, 8 * 512], BF16)
    QKfm = aa.al("QKfm", [128, 8 * 512], BF16)
    qkvfm = aa.al("qkvfm", [128, 8 * 512], BF16)
    coefb = aa.al("coefb", [16, 512], BF16)
    Atok = aa.al("Atok", [128, 4 * D], F32)
    lnA = aa.al("lnA", [128, D], F32)

    ba = Arena(ar_lo)
    W1Q = ba.al("W1Q", [128, 2 * 8 * 1024], BF16)
    W2QT = ba.al("W2QT", [128, 32 * D], FP8)
    ln2_fm = ba.al("ln2_fm", [128, 8 * 512], BF16)
    ln2_tok = ba.al("ln2_tok", [128, D], BF16)
    Hq = ba.al("Hq", [128, 32 * 512], FP8)
    Otok = ba.al("Otok", [128, 4 * D], F32)
    qsets = []
    for si in range(4):
        slots = {nm: ba.al(f"q{si}_{nm}", [128, 512], F32)
                 for nm in ["hv", "hc", "t64", "d2y"]}
        slots["g"] = ba.al(f"q{si}_g", [128, 512], BF16)
        slots["ii"] = ba.al(f"q{si}_ii", [128, 512], I32)
        qsets.append(slots)
    for si in (4, 5):
        k = si - 4
        ob = ba.map["Otok"] + k * 8192
        slots = {nm: ba.at(f"q{si}_{nm}", [128, 512], F32, ob + i * 2048)
                 for i, nm in enumerate(["hv", "hc", "t64", "d2y"])}
        slots["g"] = ba.at(f"q{si}_g", [128, 512], BF16,
                           ba.map["ln2_tok"] + k * 1024)
        slots["ii"] = ba.al(f"q{si}_ii", [128, 512], I32)
        qsets.append(slots)

    PT = nc.alloc_psum_tensor("psTR", [128, 1024], BF16)
    PS = [nc.alloc_psum_tensor(f"ps{i}", [128, 512], F32) for i in range(6)]
    PC = nc.alloc_psum_tensor("psC", [128, 512], F32)

    p = Prog(nc)
    for s in ["d", "g", "t", "v", "s", "o"]:
        p.mksem(s)
    for i, v in enumerate(CONSTS):
        nc.const_aps.register(F32, float(v), cpool[:, i:i + 1])

    def dma(q_, dst, src, inc, deps=(), **kw):
        p.wait(q_, *deps)
        return p.op(q_, lambda e, d=dst, s=src: e.dma_start(d, s, **kw),
                    inc=inc, amt=16)

    def ts1(e, out, in_, s1, o1):
        return e.tensor_single_scalar(out, in_, s1, o1)

    for i, v in enumerate(CONSTS):
        p.op("v", lambda e, i=i, v=v: e.memset(cpool[:, i:i + 1], float(v)))
    p.op("v", lambda e: e.memset(stats[:, :], 1.0), inc="v")

    dma("y", ident[:, :], id_h[:, :], "d")
    dma("y", msel.ap().rearrange("p (c h) -> p c h", h=16),
        msel_h.ap().rearrange("(c p) h -> p c h", p=128), "d")
    dma("y", mselT[:, :], mselT_h[:, :], "d")
    dma("y", onesp[:, :], onesp_h[:, :], "d")
    for dst, src in [(bqc, bq_h), (bkc, bk_h), (bvc, bv_h)]:
        p.op("y", lambda e, d=dst, s=src: e.dma_start(
            d[:, :], s.ap().rearrange("(c p) -> p c", p=128),
            allow_slow_non_contiguous=True), inc="d", amt=16)
    p.op("y", lambda e: e.dma_start(
        b1c[:, :], b1_h.ap().rearrange("(c p) -> p c", p=128),
        allow_slow_non_contiguous=True), inc="d", amt=16)

    def chains_stage1(lanes):
        for L in lanes:
            w = L["width"]; b = L["bufset"]
            for nm in ["hv", "hc", "t64", "d2y", "g"]:
                L[nm] = b[nm][:, :w]
            L["ii"] = b["ii"][:, :w]
            L["sl"] = 2.0 * LN2 if L["signed"] else LN2
        for L in lanes:
            if L.get("relu_bias") is not None:
                p.wait("s", L.get("psum_tok"), L.get("war_v"))
                L["tk1"] = p.op("s", lambda e, L=L: e.activation(
                    L["hv"], L["psum_src"], AF.Relu, bias=L["relu_bias"],
                    scale=1.0), inc="s")
                L["xs"] = L["hv"]
            elif L["signed"]:
                p.wait("s", L.get("src_tok"), L.get("war_v"))
                L["tk1"] = p.op("s", lambda e, L=L: e.activation(
                    L["d2y"], L["xsrc"], AF.Square, bias=0.0,
                    scale=2.0 ** 16), inc="s")
                L["xs"] = L["xsrc"]
            else:
                L["tk1"] = None
                L["xs"] = L["xsrc"]
        for L in lanes:
            if L.get("relu_bias") is not None:
                p.wait("v", L["tk1"], L.get("war_s"))
                L["tk2"] = p.op("v", lambda e, L=L: ts1(
                    e, L["hc"], L["hv"], L["clamp_lo"], OP.max), inc="v")
            elif L["signed"]:
                csq = float(L["clamp_lo"]) ** 2 * 2.0 ** 32
                p.wait("v", L["tk1"], L.get("war_s"))
                L["tk2"] = p.op("v", lambda e, L=L, c=csq: ts1(
                    e, L["hc"], L["d2y"], c, OP.max), inc="v")
            else:
                p.wait("v", L.get("src_tok"), L.get("war_s"))
                L["tk2"] = p.op("v", lambda e, L=L: ts1(
                    e, L["hc"], L["xsrc"], L["clamp_lo"], OP.max), inc="v")
        for L in lanes:
            p.wait("s", L["tk2"], L.get("war_v") if L["tk1"] is None else None)
            sc = 1.0 if L["signed"] else 2.0 ** 32
            p.op("s", lambda e, L=L, sc=sc: e.activation(
                L["t64"], L["hc"], AF.Ln, bias=0.0, scale=sc), inc="s")
        for L in lanes:
            L["tk4"] = p.op("s", lambda e, L=L: e.activation(
                L["ii"], L["t64"], AF.Copy, bias=0.0, scale=1.0 / L["sl"]), inc="s")
        for L in lanes:
            p.wait("v", L["tk4"])
            p.op("v", lambda e, L=L: e.scalar_tensor_tensor(
                L["d2y"], L["ii"], L["sl"], L["t64"], OP.mult, OP.is_gt), inc="v")
        for L in lanes:
            L["tk7"] = p.op("v", lambda e, L=L: e.tensor_tensor(
                L["hc"], L["ii"], L["d2y"], OP.subtract), inc="v")
        for L in lanes:
            p.wait("s", L["tk7"])
            p.op("s", lambda e, L=L: e.activation(
                L["t64"], L["hc"], AF.Exp, bias=L["bias_p"], scale=-LN2), inc="s")
        for L in lanes:
            L["g_tok"] = p.op("s", lambda e, L=L: e.activation(
                L["g"], L["hc"], AF.Exp, bias=L["bias_m"], scale=LN2), inc="s")
        return lanes

    def chains_stage2(lanes):
        for L in lanes:
            p.wait("v", L["g_tok"])
            p.op("v", lambda e, L=L: e.tensor_tensor(
                L["d2y"], L["xs"], L["t64"], OP.mult), inc="v")
        for L in lanes:
            L["tk11"] = p.op("v", lambda e, L=L: ts1(
                e, L["ii"], L["d2y"], L["off_c"], OP.add), inc="v")
        for L in lanes:
            L["end_s"] = p.op("v", lambda e, L=L: e.scalar_tensor_tensor(
                L["hc"], L["d2y"], L["off_c"], L["ii"], OP.add, OP.is_lt), inc="v")
        for L in lanes:
            p.op("v", lambda e, L=L: e.tensor_tensor(
                L["t64"], L["ii"], L["hc"], OP.subtract), inc="v")
        for L in lanes:
            if L["off_sub"]:
                L["end_v"] = p.op("v", lambda e, L=L: e.scalar_tensor_tensor(
                    L["dst"], L["t64"], -float(L["off_sub"]), L["g"],
                    OP.add, OP.mult), inc="v")
            else:
                L["end_v"] = p.op("v", lambda e, L=L: e.tensor_tensor(
                    L["dst"], L["t64"], L["g"], OP.mult), inc="v")
        return lanes

    def quant_chain(width, xsrc, dst_ap, bias_p, bias_m, off_c, off_sub,
                    clamp_lo, signed, bufset, psum_tok=None, psum_src=None,
                    relu_bias=None, war_v=None, war_s=None, src_tok=None):
        L = dict(width=width, bufset=bufset, signed=signed, bias_p=bias_p,
                 bias_m=bias_m, clamp_lo=clamp_lo, war_v=war_v, war_s=war_s,
                 xsrc=xsrc, src_tok=src_tok, psum_src=psum_src,
                 psum_tok=psum_tok, relu_bias=relu_bias,
                 dst=dst_ap, off_c=off_c, off_sub=off_sub)
        chains_stage1([L])
        chains_stage2([L])
        return {"relu": L["tk1"], "end_v": L["end_v"], "end_s": L["end_s"]}

    def layer_norm_stats(idx_list, src, dump, src_tok=None, prev=None,
                         dump_war=None):
        n = len(idx_list)
        prev = prev or {}
        acc_tok = None
        for i, cidx in enumerate(idx_list):
            sl = src[:, cidx * D:(cidx + 1) * D]
            p.wait("v", src_tok if i == 0 else None,
                   prev.get("rstd") if i == 0 else None)
            p.op("v", lambda e, sl=sl, i=i: e.reduce_sum(
                stats[:, i:i + 1], sl, axis=mybir.AxisListType.X), inc="v")
            p.wait("s", src_tok if i == 0 else None,
                   prev.get("var") if i == 0 else None,
                   dump_war if i == 0 else None)
            acc_tok = p.op("s", lambda e, sl=sl, i=i: e.activation(
                dump[:, :], sl, AF.Square, bias=0.0, scale=1.0,
                accum_out=stats[:, 16 + i:17 + i]), inc="s")
        p.wait("v", acc_tok)
        m_tok = p.op("v", lambda e, n=n: ts1(e, stats[:, 0:n], stats[:, 0:n],
                                             1.0 / D, OP.mult), inc="v")
        p.op("v", lambda e, n=n: ts1(e, stats[:, 16:16 + n], stats[:, 16:16 + n],
                                     1.0 / D, OP.mult), inc="v")
        p.wait("s", m_tok)
        msq_tok = p.op("s", lambda e, n=n: e.activation(
            stats[:, 48:48 + n], stats[:, 0:n], AF.Square, bias=0.0,
            scale=1.0), inc="s")
        p.wait("v", msq_tok)
        var_tok = p.op("v", lambda e, n=n: e.tensor_tensor(
            stats[:, 16:16 + n], stats[:, 16:16 + n], stats[:, 48:48 + n],
            OP.subtract), inc="v")
        p.wait("s", var_tok)
        p.op("s", lambda e, n=n: e.activation(stats[:, 48:48 + n], stats[:, 16:16 + n],
                                              AF.Ln, bias=C_EPS, scale=1.0), inc="s")
        rstd_tok = p.op("s", lambda e, n=n: e.activation(
            stats[:, 32:32 + n], stats[:, 48:48 + n], AF.Exp, bias=0.0,
            scale=-0.5), inc="s")
        return {"rstd": rstd_tok, "var": var_tok}

    w1_ld, w1_ev, w1_st = {}, {}, {}

    def w1_prefetch(dc):
        for hg in range(4):
            gidx = dc * 4 + hg
            slot = gidx % 8
            tok = None
            for j in range(8):
                hcn = hg * 8 + j
                tok = dma("g", p_w1s[:, slot * 1024 + j * 128: slot * 1024 + (j + 1) * 128],
                          w1_h[hcn * 128:(hcn + 1) * 128, dc * 128:(dc + 1) * 128],
                          "g", deps=[w1_ev.get(gidx - 8)] if j == 0 else ())
            w1_ld[gidx] = tok

    w1_prefetch(0)
    w1_prefetch(1)
    for wsrc, wdst in [(wq_h, wqb_h), (wk_h, wkb_h), (wv_h, wvb_h), (wo_h, wob_h)]:
        for c in range(4):
            dma("g", wdst[c * 256:(c + 1) * 256, :], wsrc[c * 256:(c + 1) * 256, :], "g")
    if stage < 1:
        return _finish(nc, p)
    p.op("v", lambda e: e.memset(p_row[:, :], 0.0), inc="v")
    tok_bo = dma("y", p_row[0:1, :], bo_h[:, :], "d", deps=["v"])
    p.wait("v", tok_bo)
    p.op("v", lambda e: e.tensor_copy(p_rowb[:, :], p_row[:, :]), inc="v")
    p.wait("t", "v")
    for half in range(2):
        p.op("t", lambda e, half=half: e.matmul(
            PS[0][:, :], onesp[:, :], p_rowb[:, half * 512:(half + 1) * 512],
            start=True, stop=True), inc="t")
        p.wait("v", "t")
        p.op("v", lambda e, half=half: e.tensor_copy(
            bo_bc[:, half * 512:(half + 1) * 512], PS[0][:, :]), inc="v")
        p.wait("t", "v")
    tok_b2 = dma("y", p_row[0:1, :], b2_h[:, :], "d", deps=["v"])
    p.wait("v", tok_b2)
    p.op("v", lambda e: e.tensor_copy(p_rowb[:, :], p_row[:, :]), inc="v")
    p.wait("t", "v")
    for half in range(2):
        p.op("t", lambda e, half=half: e.matmul(
            PS[0][:, :], onesp[:, :], p_rowb[:, half * 512:(half + 1) * 512],
            start=True, stop=True), inc="t")
        p.wait("v", "t")
        p.op("v", lambda e, half=half: e.tensor_copy(
            b2stage[:, half * 512:(half + 1) * 512], PS[0][:, :]), inc="v")
        p.wait("t", "v")
    pset = {nm: p_qb[k] for k, nm in enumerate(["hv", "hc", "t64", "d2y", "g"])}
    pset["ii"] = p_qi
    b2tok = quant_chain(D, b2stage[:, :D], b2q_bc[:, :], C_P26L2, C_M26L2,
                        C_4096_5, 4096, 2.0 ** -14, signed=True, bufset=pset,
                        war_v=p.latest("v"), war_s=p.latest("s"),
                        src_tok=p.latest("v"))
    if stage < 2:
        return _finish(nc, p)
    prep_sets = []
    for si in range(4):
        ps_ = {nm: p_qb[k].ap()[:, si * 512:(si + 1) * 512]
               for k, nm in enumerate(["hv", "hc", "t64", "d2y", "g"])}
        ps_["ii"] = p_qi.ap()[:, si * 512:(si + 1) * 512]
        prep_sets.append(ps_)
    prep_hist = {0: {"end_v": b2tok["end_v"], "end_s": b2tok["end_s"]},
                 1: {"end_v": b2tok["end_v"], "end_s": b2tok["end_s"]},
                 2: None, 3: None}
    st8_tok = {}

    tr_last = {}

    def w2_transposes(dc):
        p.wait("t", prep_hist[2]["end_v"], prep_hist[3]["end_v"])
        for hg in range(4):
            gidx = dc * 4 + hg
            buf = gidx % 2
            tr_tok = None
            for j in range(8):
                hcn = hg * 8 + j
                src = p_wbf[:, (dc % 2) * H4 + hcn * 128: (dc % 2) * H4 + (hcn + 1) * 128]
                tr_tok = p.op("t", lambda e, src=src, j=j: e.transpose(
                    PT[:, j * 128:(j + 1) * 128], src, ident[:, :]),
                    inc="t" if j == 7 else None)
            tr_last[dc] = tr_tok
            p.wait("v", tr_tok, st8_tok.get(gidx - 2))
            ev = p.op("v", lambda e, buf=buf: e.tensor_copy(
                p_tr8[:, buf * 1024:(buf + 1) * 1024], PT[:, :]), inc="v")
            p.wait("t", ev)
            for j in range(8):
                hcn = hg * 8 + j
                st8_tok[gidx] = dma(
                    "y", w2qt_h[hcn * 128:(hcn + 1) * 128, dc * 128:(dc + 1) * 128],
                    p_tr8[:, buf * 1024 + j * 128: buf * 1024 + (j + 1) * 128],
                    "d", deps=[ev] if j == 0 else ())
            slot = gidx % 8
            p.wait("t", w1_ld[gidx])
            trb = None
            for j in range(8):
                src = p_w1s[:, slot * 1024 + j * 128: slot * 1024 + (j + 1) * 128]
                trb = p.op("t", lambda e, src=src, j=j: e.transpose(
                    PT[:, j * 128:(j + 1) * 128], src, ident[:, :]),
                    inc="t" if j == 7 else None)
            p.wait("v", trb, w1_st.get(gidx - 2))
            evb = p.op("v", lambda e, buf=buf: e.tensor_copy(
                p_trb[:, buf * 1024:(buf + 1) * 1024], PT[:, :]), inc="v")
            w1_ev[gidx] = evb
            p.wait("t", evb)
            p.wait("g", evb)
            w1_st[gidx] = dma("g", w1t_h[dc * 128:(dc + 1) * 128,
                                         hg * 1024:(hg + 1) * 1024],
                              p_trb[:, buf * 1024:(buf + 1) * 1024], "g")
        if 2 <= dc + 1 <= 7:
            w1_prefetch(dc + 1)

    stg = [p_stage, p_stage2]
    dtoks = {0: dma("y", p_stage[:, :], w2_h[0:128, :], "d"),
             1: dma("y", p_stage2[:, :], w2_h[128:256, :], "d")}
    dc_flush = {}
    for dc in range(8):
        if 1 <= dc <= 6:
            dtoks[dc + 1] = dma("y", stg[(dc + 1) % 2][:, :],
                                w2_h[(dc + 1) * 128:(dc + 2) * 128, :], "d",
                                deps=dc_flush[dc - 1])
        p.wait("v", tr_last.get(dc - 2))
        pend = None
        for m2 in range(4):
            lanes = []
            for j in range(2):
                sl8 = 2 * m2 + j
                si = 2 * (m2 % 2) + j
                hist = prep_hist[si]
                lanes.append(dict(
                    width=512, bufset=prep_sets[si], signed=True,
                    bias_p=C_P19L2, bias_m=C_M19L2, clamp_lo=2.0 ** -6,
                    xsrc=stg[dc % 2][:, sl8 * 512:(sl8 + 1) * 512],
                    src_tok=dtoks[dc],
                    war_v=hist["end_v"] if hist else None,
                    war_s=hist["end_s"] if hist else None,
                    dst=p_wbf[:, (dc % 2) * H4 + sl8 * 512: (dc % 2) * H4 + (sl8 + 1) * 512],
                    off_c=C_64_5, off_sub=64, si=si))
            chains_stage1(lanes)
            if pend is not None:
                chains_stage2(pend)
                for L in pend:
                    prep_hist[L["si"]] = {"end_v": L["end_v"], "end_s": L["end_s"]}
            if dc > 0 and m2 == 1:
                w2_transposes(dc - 1)
            pend = lanes
        chains_stage2(pend)
        for L in pend:
            prep_hist[L["si"]] = {"end_v": L["end_v"], "end_s": L["end_s"]}
        dc_flush[dc] = [prep_hist[2]["end_v"], prep_hist[3]["end_v"],
                        prep_hist[2]["end_s"], prep_hist[3]["end_s"]]
    w2_transposes(7)
    if debug:
        p.wait("y", "v")
        dma("y", dbg["d_w2q"][:, :], p_wbf[:, 0:H4], "o")
        dma("y", dbg["d_pg"][:, :], p_qb[4][:, :], "o")
        dma("y", dbg["d_pr"][:, :], p_qb[2][:, :], "o")
        dma("y", dbg["d_psrc"][:, :], p_stage[:, :], "o")
    if stage < 3:
        return _finish(nc, p)

    if stage < 4:
        return _finish(nc, p)
    stats_prev = None
    tok_u_ready = None
    for sup in range(nsup):
        base_tok = sup * sup_tok
        p.barrier_on("y")
        p.barrier_on("g")
        for Wd, Ws in [(WQ, wqb_h), (WK, wkb_h), (WV, wvb_h), (WO, wob_h)]:
            for c in range(8):
                dma("g", Wd[:, c * D:(c + 1) * D], Ws[c * 128:(c + 1) * 128, :], "d")
        tok_wload = p.latest("d")
        tok_qkv_last = None
        tok_wo_last = None
        tok_coef_last = None
        tok_bc_last = None
        tok_uadd = [None] * 4
        for lt in range(ta_per_sup):
            t0 = base_tok + lt * 512
            xt_tok = []
            for tc in range(4):
                xt_tok.append(dma("y", Xt[:, tc * D:(tc + 1) * D],
                                  x_h[t0 + tc * 128:t0 + tc * 128 + 128, :], "d",
                                  deps=[tok_uadd[tc],
                                        stats_prev["rstd"] if stats_prev else None]))
            stats_prev = layer_norm_stats([0, 1, 2, 3], Xt, lnA,
                                          src_tok=xt_tok[3], prev=stats_prev,
                                          dump_war=tok_uadd[3])
            rstd_tok = stats_prev["rstd"]
            tr_tok = None
            ev_toks = []
            for tc in range(4):
                p.wait("v", rstd_tok, tok_qkv_last)
                ap_tok = p.op("v", lambda e, tc=tc: e.tensor_scalar(
                    ln1_tok[:, :], Xt[:, tc * D:(tc + 1) * D],
                    stats[:, tc:tc + 1], stats[:, 32 + tc:33 + tc],
                    OP.subtract, OP.mult), inc="v")
                p.wait("t", ap_tok, ev_toks[-1] if ev_toks else None)
                for fc in range(8):
                    tr_tok = p.op("t", lambda e, fc=fc: e.transpose(
                        PT[:, fc * 128:(fc + 1) * 128],
                        ln1_tok[:, fc * 128:(fc + 1) * 128], ident[:, :]),
                        inc="t" if fc == 7 else None)
                p.wait("v", tr_tok)
                ev_toks.append(p.op("v", lambda e, tc=tc: e.tensor_copy(
                    _fmcols(ln1_fm, 512, tc), PT[:, :]), inc="v"))
            if debug:
                dma("y", dbg["d_ln1fm"][:, :], ln1_fm[:, :], "o", deps=[ev_toks[-1]])
            rd = {}
            exp_tok = {}
            vfm_tok = {}
            sqs_tok = None
            for oc in range(8):
                par = oc % 2
                psq, psk, psv = PS[3 * par], PS[3 * par + 1], PS[3 * par + 2]
                if oc >= 2:
                    p.wait("t", *rd[oc - 2])
                else:
                    p.wait("t", ev_toks[-1], tok_wload, tok_wo_last)
                mm_toks = []
                for W_, psx in [(WQ, psq), (WK, psk), (WV, psv)]:
                    for fc in range(8):
                        lhsT = W_[:, fc * D + oc * 128: fc * D + (oc + 1) * 128]
                        rhs = ln1_fm[:, fc * 512:(fc + 1) * 512]
                        tk = p.op("t", lambda e, lhsT=lhsT, rhs=rhs, psx=psx, fc=fc: e.matmul(
                            psx[:, :], lhsT, rhs, start=(fc == 0), stop=(fc == 7)),
                            inc="t" if fc == 7 else None)
                        if tk:
                            mm_toks.append(tk)
                tok_qkv_last = mm_toks[-1]
                p.wait("v", mm_toks[2])
                vfm_tok[oc] = p.op("v", lambda e, oc=oc, psv=psv: ts1(
                    e, Vfm[:, oc * 512:(oc + 1) * 512], psv[:, :],
                    bvc[:, oc:oc + 1], OP.add), inc="v")
                p.wait("s", mm_toks[0], sqs_tok)
                p.op("s", lambda e, oc=oc, psq=psq: e.activation(
                    sqq[:, :], psq[:, :], AF.Square, bias=bqc[:, oc:oc + 1],
                    scale=1.0), inc="s")
                p.wait("s", mm_toks[1])
                sq_tok = p.op("s", lambda e, oc=oc, psk=psk: e.activation(
                    sqk[:, :], psk[:, :], AF.Square, bias=bkc[:, oc:oc + 1],
                    scale=1.0), inc="s")
                p.wait("v", sq_tok, exp_tok.get(oc - 1), tok_coef_last if oc == 0 else None)
                sqs_tok = p.op("v", lambda e: e.tensor_add(sqs[:, :], sqq[:, :],
                                                           sqk[:, :]), inc="v")
                p.wait("s", sqs_tok)
                exp_tok[oc] = p.op("s", lambda e, oc=oc: e.activation(
                    QKfm[:, oc * 512:(oc + 1) * 512], sqs[:, :], AF.Exp,
                    bias=0.0, scale=-0.5), inc="s")
                rd[oc] = (vfm_tok[oc], sq_tok)
            if debug:
                dma("y", dbg["d_qkfm"][:, :], QKfm[:, :], "o", deps=[exp_tok[7]])
            p.wait("t", tok_bc_last)
            for oc in range(8):
                p.wait("t", exp_tok[oc])
                tok_coef_last = p.op("t", lambda e, oc=oc: e.matmul(
                    PC[0:16, :], msel[:, oc * 16:(oc + 1) * 16],
                    QKfm[:, oc * 512:(oc + 1) * 512],
                    start=(oc == 0), stop=(oc == 7)),
                    inc="t" if oc == 7 else None)
            p.wait("v", tok_coef_last)
            coefb_tok = p.op("v", lambda e: e.tensor_copy(coefb[:, :], PC[0:16, :]), inc="v")
            stt_toks = {}
            for oc in range(8):
                bank = PS[3] if oc % 2 == 0 else PC
                p.wait("t", coefb_tok if oc < 2 else stt_toks[oc - 2],
                       rd[7][1] if oc == 0 else None)
                bc_tok = p.op("t", lambda e, oc=oc, bank=bank: e.matmul(
                    bank[0:128, :], mselT[:, oc * 128:(oc + 1) * 128],
                    coefb[:, :], start=True, stop=True), inc="t")
                p.wait("v", bc_tok, tok_wo_last if oc == 0 else None)
                stt_toks[oc] = p.op("v", lambda e, oc=oc, bank=bank: e.scalar_tensor_tensor(
                    qkvfm[:, oc * 512:(oc + 1) * 512], bank[0:128, :], 1.0,
                    Vfm[:, oc * 512:(oc + 1) * 512], OP.bypass, OP.mult), inc="v")
            tok_bc_last_list = []
            ev2 = {}
            for tp in range(4):
                for oh in range(2):
                    pse = PS[4 + oh]
                    p.wait("t", stt_toks[7], ev2.get(tp - 1),
                           rd[7][1] if tp == 0 else None,
                           rd[7][0] if tp == 0 else None)
                    for fc in range(8):
                        lhsT = qkvfm[:, fc * 512 + tp * 128: fc * 512 + (tp + 1) * 128]
                        rhs = WO[:, fc * D + oh * 512: fc * D + (oh + 1) * 512]
                        tok_wo_last = p.op("t", lambda e, lhsT=lhsT, rhs=rhs, pse=pse, fc=fc: e.matmul(
                            pse[:, :], lhsT, rhs, start=(fc == 0), stop=(fc == 7)),
                            inc="t" if fc == 7 else None)
                    p.wait("v", tok_wo_last, tok_uadd[tp])
                    ev2[tp] = p.op("v", lambda e, tp=tp, oh=oh, pse=pse: e.tensor_tensor(
                        Atok[:, tp * D + oh * 512: tp * D + (oh + 1) * 512],
                        pse[:, :], bo_bc[:, oh * 512:(oh + 1) * 512],
                        OP.add), inc="v")
            if debug:
                dma("y", dbg["d_a"][:, :], Atok[:, :], "o", deps=[ev2[3]])
            stats_prev = layer_norm_stats([0, 1, 2, 3], Atok, lnA,
                                          src_tok=ev2[3], prev=stats_prev)
            rstd_tok = stats_prev["rstd"]
            for tc in range(4):
                cc = lt * 4 + tc
                p.wait("v", rstd_tok)
                p.op("v", lambda e, tc=tc: e.tensor_scalar(
                    lnA[:, :], Atok[:, tc * D:(tc + 1) * D],
                    stats[:, tc:tc + 1], stats[:, 32 + tc:33 + tc],
                    OP.subtract, OP.mult), inc="v")
                tok_uadd[tc] = p.op("v", lambda e, tc=tc, cc=cc: e.tensor_tensor(
                    U[:, cc * D:(cc + 1) * D], Xt[:, tc * D:(tc + 1) * D],
                    lnA[:, :], OP.add), inc="v")
                tok_u_ready = tok_uadd[tc]
        if debug:
            dma("y", dbg["d_u"][:, :], U[:, :], "o", deps=[tok_uadd[3]])
        if stage < 5:
            continue
        for c in range(32):
            dma("g", W2QT[:, c * D:(c + 1) * D], w2qt_h[c * 128:(c + 1) * 128, :],
                "d", deps=[tok_qkv_last, tok_wo_last] if c == 0 else ())
        tok_wload = p.latest("d")
        qld, qmm_last = {}, {}

        def w1_qload(gq):
            q4 = gq % 4
            slot = gq % 2
            tok = None
            for dcq in range(8):
                tok = dma("g", W1Q[:, slot * 8192 + dcq * 1024:
                                   slot * 8192 + (dcq + 1) * 1024],
                          w1t_h[dcq * 128:(dcq + 1) * 128, q4 * 1024:(q4 + 1) * 1024],
                          "d", deps=[qmm_last.get(gq - 2)] if dcq == 0 else ())
            qld[gq] = tok

        w1_qload(0)
        w1_qload(1)
        tok_b2_last = None
        tok_mm1_last = None
        tok_ost = {}
        tr_tok = None
        chain_hist = {si: None for si in range(6)}
        for bt in range(tb_per_sup):
            cc0 = bt * 4
            stats_prev = layer_norm_stats([cc0, cc0 + 1, cc0 + 2, cc0 + 3], U,
                                          ln2_tok, src_tok=tok_u_ready,
                                          prev=stats_prev, dump_war=tr_tok)
            rstd_tok = stats_prev["rstd"]
            ev_toks = []
            for k in range(4):
                p.wait("v", rstd_tok, tok_mm1_last)
                ap_tok = p.op("v", lambda e, k=k, cc0=cc0: e.tensor_scalar(
                    ln2_tok[:, :], U[:, (cc0 + k) * D:(cc0 + k + 1) * D],
                    stats[:, k:k + 1], stats[:, 32 + k:33 + k],
                    OP.subtract, OP.mult), inc="v")
                p.wait("t", ap_tok, ev_toks[-1] if ev_toks else None,
                       tok_b2_last if bt > 0 and k == 0 else None)
                for fc in range(8):
                    tr_tok = p.op("t", lambda e, fc=fc: e.transpose(
                        PT[:, fc * 128:(fc + 1) * 128],
                        ln2_tok[:, fc * 128:(fc + 1) * 128], ident[:, :]),
                        inc="t" if fc == 7 else None)
                p.wait("v", tr_tok)
                ev_toks.append(p.op("v", lambda e, k=k: e.tensor_copy(
                    _fmcols(ln2_fm, 512, k), PT[:, :]), inc="v"))
            relu_toks = {}
            pend = None
            last_v = None
            for g_ in range(11):
                lanes = []
                for ln_ in range(3):
                    hcn = 3 * g_ + ln_
                    if hcn >= 32:
                        break
                    gq = bt * 4 + hcn // 8
                    slot = gq % 2
                    psh = PS[hcn % 4]
                    p.wait("t", ev_toks[-1] if hcn == 0 else None,
                           tok_wload if hcn == 0 else None,
                           qld[gq] if hcn % 8 == 0 else None,
                           relu_toks.get(hcn - 4))
                    mmtok = None
                    for dc in range(8):
                        o0 = slot * 8192 + dc * 1024 + (hcn % 8) * 128
                        lhsT = W1Q[:, o0:o0 + 128]
                        rhs = ln2_fm[:, dc * 512:(dc + 1) * 512]
                        mmtok = p.op("t", lambda e, lhsT=lhsT, rhs=rhs, psh=psh, dc=dc: e.matmul(
                            psh[:, :], lhsT, rhs, start=(dc == 0), stop=(dc == 7)),
                            inc="t" if dc == 7 else None)
                    tok_mm1_last = mmtok
                    if hcn % 8 == 7:
                        qmm_last[gq] = mmtok
                        if gq + 2 < 4 * tb_per_sup:
                            w1_qload(gq + 2)
                    si = 3 * (g_ % 2) + ln_
                    hist = chain_hist[si]
                    lanes.append(dict(
                        width=512, bufset=qsets[si], signed=False,
                        bias_p=C_P35L2, bias_m=C_M35L2, clamp_lo=2.0 ** -6,
                        psum_src=psh[:, :], psum_tok=mmtok,
                        relu_bias=b1c[:, hcn:hcn + 1],
                        war_v=hist["end_v"] if hist else None,
                        war_s=hist["end_s"] if hist else None,
                        dst=Hq[:, hcn * 512:(hcn + 1) * 512],
                        off_c=C_HALF, off_sub=0, hcn=hcn, si=si))
                chains_stage1(lanes)
                for L in lanes:
                    relu_toks[L["hcn"]] = L["tk1"]
                if pend is not None:
                    chains_stage2(pend)
                    for L in pend:
                        chain_hist[L["si"]] = {"end_v": L["end_v"], "end_s": L["end_s"]}
                    last_v = pend[-1]["end_v"]
                pend = lanes
            chains_stage2(pend)
            for L in pend:
                chain_hist[L["si"]] = {"end_v": L["end_v"], "end_s": L["end_s"]}
            last_v = pend[-1]["end_v"]
            if debug and bt == 0:
                dma("y", dbg["d_h"][:, :], qsets[1]["hv"][:, :512], "o",
                    deps=[last_v])
                dma("y", dbg["d_hq"][:, :], Hq[:, :], "o", deps=[last_v])
            ev3 = {}
            gi = 0
            for tp in range(4):
                for dh in range(2):
                    pso = PS[2 + gi % 4]
                    if gi == 0:
                        p.wait("t", last_v, relu_toks[30])
                    elif gi == 1:
                        p.wait("t", relu_toks[31])
                    elif gi >= 4:
                        p.wait("t", ev3[gi - 4])
                    for hcn in range(32):
                        lhsT = Hq[:, hcn * 512 + tp * 128: hcn * 512 + (tp + 1) * 128]
                        rhs = W2QT[:, hcn * D + dh * 512: hcn * D + (dh + 1) * 512]
                        tok_b2_last = p.op("t", lambda e, lhsT=lhsT, rhs=rhs, pso=pso, hcn=hcn: e.matmul(
                            pso[:, :], lhsT, rhs, start=(hcn == 0), stop=(hcn == 31)),
                            inc="t" if hcn == 31 else None)
                    p.wait("v", tok_b2_last, tok_ost.get(gi // 2))
                    ev3[gi] = p.op("v", lambda e, tp=tp, dh=dh, pso=pso: e.tensor_tensor(
                        Otok[:, tp * D + dh * 512: tp * D + (dh + 1) * 512],
                        pso[:, :], b2q_bc[:, dh * 512:(dh + 1) * 512],
                        OP.add), inc="v")
                    gi += 1
            if debug and bt == 0:
                dma("y", dbg["d_o"][:, :], Otok[:, :], "o", deps=[ev3[7]])
            pend = None
            st2last = None
            for m_ in range(4):
                lanes = []
                for j in range(2):
                    seg = 2 * m_ + j
                    si = 2 * (m_ % 2) + j
                    hist = chain_hist[si]
                    lanes.append(dict(
                        width=512, bufset=qsets[si], signed=True,
                        bias_p=C_P26L2, bias_m=C_M26L2, clamp_lo=2.0 ** -14,
                        xsrc=Otok[:, seg * 512:(seg + 1) * 512],
                        src_tok=ev3[seg],
                        war_v=hist["end_v"] if hist else None,
                        war_s=hist["end_s"] if hist else None,
                        dst=Otok[:, seg * 512:(seg + 1) * 512],
                        off_c=C_4096_5, off_sub=4096, si=si))
                chains_stage1(lanes)
                if pend is not None:
                    chains_stage2(pend)
                    for L in pend:
                        chain_hist[L["si"]] = {"end_v": L["end_v"], "end_s": L["end_s"]}
                pend = lanes
            chains_stage2(pend)
            for L in pend:
                chain_hist[L["si"]] = {"end_v": L["end_v"], "end_s": L["end_s"]}
            st2last = pend[-1]["end_v"]
            if debug and bt == 0:
                dma("y", dbg["d_oq"][:, :], Otok[:, :], "o", deps=[st2last])
            stats_prev = layer_norm_stats([0, 1, 2, 3], Otok, ln2_tok,
                                          src_tok=st2last, prev=stats_prev,
                                          dump_war=tr_tok)
            rstd_tok = stats_prev["rstd"]
            for k in range(4):
                cc = cc0 + k
                p.wait("v", rstd_tok, tok_ost.get(k))
                p.op("v", lambda e, k=k: e.tensor_scalar(
                    Otok[:, k * D:(k + 1) * D], Otok[:, k * D:(k + 1) * D],
                    stats[:, k:k + 1], stats[:, 32 + k:33 + k],
                    OP.subtract, OP.mult), inc="v")
                ytok = p.op("v", lambda e, k=k, cc=cc: e.tensor_tensor(
                    Otok[:, k * D:(k + 1) * D], U[:, cc * D:(cc + 1) * D],
                    Otok[:, k * D:(k + 1) * D], OP.add), inc="v")
                r0 = base_tok + cc * 128
                tok_ost[k] = dma("y", y_h[r0:r0 + 128, :], Otok[:, k * D:(k + 1) * D],
                                 "o", deps=[ytok])

    p.wait("y", "o")
    return _finish(nc, p)


def _finish(nc, p):
    p.barrier_on("y")
    with nc.Block() as block:
        block.tensor(lambda e: [fn(e) for fn in p.q["t"]])
        block.vector(lambda e: [fn(e) for fn in p.q["v"]])
        block.scalar(lambda e: [fn(e) for fn in p.q["s"]])
        block.sync(lambda e: [fn(e) for fn in p.q["y"]])
        block.gpsimd(lambda e: [fn(e) for fn in p.q["g"]])
    return nc


def _make_const_inputs():
    hd = 64
    msel = np.zeros((D, 16), dtype=np.float32)
    for d_ in range(D):
        msel[d_, d_ // hd] = 1.0
    onesp = np.zeros((16, 128), dtype=np.float32)
    onesp[0, :] = 1.0
    return {
        "cident": np.eye(128, dtype=np.float32).astype(ml_dtypes.bfloat16),
        "msel": msel.astype(ml_dtypes.bfloat16),
        "mselT": msel.T.copy().astype(ml_dtypes.bfloat16),
        "onesp": onesp.astype(ml_dtypes.bfloat16),
    }


def kernel(L0_attn_x, L0_attn_Wq, L0_attn_bq, L0_attn_Wk, L0_attn_bk,
           L0_attn_Wv, L0_attn_bv, L0_attn_Wo, L0_attn_bo,
           L0_ffn_w1, L0_ffn_b1, L0_ffn_w2, L0_ffn_b2):
    x = np.ascontiguousarray(np.asarray(L0_attn_x, dtype=np.float32))
    B, S, Dm = x.shape
    tok = B * S
    assert Dm == D and tok % NCORES == 0
    tokc = tok // NCORES
    xf = x.reshape(tok, D)

    nc = build_program(tokc)
    consts = _make_const_inputs()
    common = {
        "wq": np.ascontiguousarray(L0_attn_Wq, dtype=np.float32),
        "wk": np.ascontiguousarray(L0_attn_Wk, dtype=np.float32),
        "wv": np.ascontiguousarray(L0_attn_Wv, dtype=np.float32),
        "wo": np.ascontiguousarray(L0_attn_Wo, dtype=np.float32),
        "bq": np.ascontiguousarray(L0_attn_bq, dtype=np.float32),
        "bk": np.ascontiguousarray(L0_attn_bk, dtype=np.float32),
        "bv": np.ascontiguousarray(L0_attn_bv, dtype=np.float32),
        "bo": np.ascontiguousarray(L0_attn_bo, dtype=np.float32).reshape(1, D),
        "w1": np.ascontiguousarray(L0_ffn_w1, dtype=np.float32),
        "b1": np.ascontiguousarray(L0_ffn_b1, dtype=np.float32),
        "w2": np.ascontiguousarray(L0_ffn_w2, dtype=np.float32),
        "b2": np.ascontiguousarray(L0_ffn_b2, dtype=np.float32).reshape(1, D),
        **consts,
    }
    in_maps = []
    for c in range(NCORES):
        m = dict(common)
        m["x"] = np.ascontiguousarray(xf[c * tokc:(c + 1) * tokc])
        in_maps.append(m)
    res = run_bass_kernel_spmd(nc, in_maps, core_ids=list(range(NCORES)))
    out = np.concatenate([r["y"] for r in res.results], axis=0)
    return out.reshape(B, S, D).astype(np.float32)


# version 12
# speedup vs baseline: 1.2799x; 1.0140x over previous
import os
import numpy as np
import ml_dtypes
import concourse.bass as bass
import concourse.mybir as mybir
from concourse.bass_utils import run_bass_kernel_spmd

F32 = mybir.dt.float32
BF16 = mybir.dt.bfloat16
FP8 = mybir.dt.float8e4
I32 = mybir.dt.int32
AF = mybir.ActivationFunctionType
OP = mybir.AluOpType

LN2 = 0.6931471805599453
D = 1024
H4 = 4096
NCORES = 8

C_EPS = 1e-5
C_HALF = 0.5
C_64_5 = 64.5
C_4096_5 = 4096.5
C_P35L2 = 35.0 * LN2
C_M35L2 = -35.0 * LN2
C_P19L2 = 19.0 * LN2
C_M19L2 = -19.0 * LN2
C_P26L2 = 26.0 * LN2
C_M26L2 = -26.0 * LN2
CONSTS = [C_EPS, C_HALF, C_64_5, C_4096_5, C_P35L2, C_M35L2,
          C_P19L2, C_M19L2, C_P26L2, C_M26L2]


class Prog:

    OWN = {"t": "t", "v": "v", "s": "s"}

    def __init__(self, nc):
        self.nc = nc
        self.q = {"t": [], "v": [], "s": [], "y": [], "g": []}
        self.sem = {}
        self.cnt = {}
        self.lastw = {e: {} for e in self.q}

    def mksem(self, name):
        self.sem[name] = self.nc.alloc_semaphore(name)
        self.cnt[name] = 0

    def latest(self, sem):
        return (sem, self.cnt[sem])

    def wait(self, eng, *deps):
        for dp in deps:
            if dp is None:
                continue
            if isinstance(dp, str):
                dp = (dp, self.cnt[dp])
            sem, c = dp
            if c <= 0:
                continue
            if self.OWN.get(eng) == sem:
                continue
            if self.lastw[eng].get(sem, 0) >= c:
                continue
            self.lastw[eng][sem] = c
            self.q[eng].append(lambda e, h=self.sem[sem], c=c: e.wait_ge(h, c))

    def op(self, eng, fn, inc=None, amt=1):
        if inc is None:
            self.q[eng].append(fn)
            return None
        h = self.sem[inc]
        self.cnt[inc] += amt
        self.q[eng].append(lambda e, fn=fn, h=h, a=amt: fn(e).then_inc(h, a))
        return (inc, self.cnt[inc])

    def barrier_on(self, eng):
        self.wait(eng, *list(self.sem.keys()))


def _fmcols(buf, w, tc):
    return buf.ap().rearrange("p (c w) -> p c w", w=w)[:, :, tc * 128:(tc + 1) * 128]


def build_program(tokc, debug=False, stage=99):
    assert tokc % 512 == 0
    ntile_a = tokc // 512
    nsup = 2 if ntile_a % 2 == 0 else 1
    ta_per_sup = ntile_a // nsup
    sup_tok = tokc // nsup
    ncc = sup_tok // 128
    tb_per_sup = ncc // 4

    nc = bass.Bass("TRN2", target_bir_lowering=False)

    x_h = nc.dram_tensor("x", [tokc, D], F32, kind="ExternalInput")
    y_h = nc.dram_tensor("y", [tokc, D], F32, kind="ExternalOutput")
    wq_h = nc.dram_tensor("wq", [D, D], F32, kind="ExternalInput")
    wk_h = nc.dram_tensor("wk", [D, D], F32, kind="ExternalInput")
    wv_h = nc.dram_tensor("wv", [D, D], F32, kind="ExternalInput")
    wo_h = nc.dram_tensor("wo", [D, D], F32, kind="ExternalInput")
    bq_h = nc.dram_tensor("bq", [D], F32, kind="ExternalInput")
    bk_h = nc.dram_tensor("bk", [D], F32, kind="ExternalInput")
    bv_h = nc.dram_tensor("bv", [D], F32, kind="ExternalInput")
    bo_h = nc.dram_tensor("bo", [1, D], F32, kind="ExternalInput")
    w1_h = nc.dram_tensor("w1", [H4, D], F32, kind="ExternalInput")
    b1_h = nc.dram_tensor("b1", [H4], F32, kind="ExternalInput")
    w2_h = nc.dram_tensor("w2", [D, H4], F32, kind="ExternalInput")
    b2_h = nc.dram_tensor("b2", [1, D], F32, kind="ExternalInput")
    id_h = nc.dram_tensor("cident", [128, 128], BF16, kind="ExternalInput")
    msel_h = nc.dram_tensor("msel", [D, 16], BF16, kind="ExternalInput")
    mselT_h = nc.dram_tensor("mselT", [16, D], BF16, kind="ExternalInput")
    onesp_h = nc.dram_tensor("onesp", [16, 128], BF16, kind="ExternalInput")
    wqb_h = nc.dram_tensor("wqb", [D, D], BF16)
    wkb_h = nc.dram_tensor("wkb", [D, D], BF16)
    wvb_h = nc.dram_tensor("wvb", [D, D], BF16)
    wob_h = nc.dram_tensor("wob", [D, D], BF16)
    w1t_h = nc.dram_tensor("w1t", [D, H4], BF16)
    w2qt_h = nc.dram_tensor("w2qt", [H4, D], FP8)
    dbg = {}
    if debug:
        for nm, shp, dt in [("d_ln1fm", [128, 8 * 512], BF16),
                            ("d_qkfm", [128, 8 * 512], BF16),
                            ("d_qkv", [128, 8 * 512], BF16),
                            ("d_a", [128, 4 * D], F32),
                            ("d_u", [128, ncc * D], BF16),
                            ("d_h", [128, 512], F32),
                            ("d_hq", [128, 32 * 512], FP8),
                            ("d_o", [128, 4 * D], F32),
                            ("d_oq", [128, 4 * D], F32),
                            ("d_w2q", [128, H4], BF16),
                            ("d_pg", [128, 2048], F32),
                            ("d_pr", [128, 2048], F32),
                            ("d_psrc", [128, H4], F32),
                            ("d_xt", [128, 4 * D], F32),
                            ("d_stats", [128, 16], F32)]:
            dbg[nm] = nc.dram_tensor(nm, shp, dt, kind="ExternalOutput")

    def isz(dt):
        return np.dtype(mybir.dt.np(dt)).itemsize

    A = lambda name, shape, dt: nc.alloc_sbuf_tensor(name, shape, dt)
    ident = A("ident", [128, 128], BF16)
    msel = A("mselb", [128, 8 * 16], BF16)
    mselT = A("mselTb", [16, 8 * 128], BF16)
    onesp = A("onespb", [16, 128], BF16)
    bqc = A("bqc", [128, 8], F32)
    bkc = A("bkc", [128, 8], F32)
    bvc = A("bvc", [128, 8], F32)
    b1c = A("b1c", [128, 32], F32)
    bo_bc = A("bo_bc", [128, D], F32)
    b2q_bc = A("b2q_bc", [128, D], F32)
    cpool = A("cpool", [128, len(CONSTS)], F32)
    stats = A("stats", [128, 64], F32)
    b2stage = A("b2stage", [128, D], F32)
    U = A("U", [128, ncc * D], BF16)

    ARENA_BYTES = 157696
    r = nc.bump_sbuf(ARENA_BYTES, align_bytes=512)
    assert r is not None, "SBUF arena allocation failed"
    ar_lo = r[0]

    class Arena:
        def __init__(self, base):
            self.base, self.off = base, 0
            self.map = {}

        def al(self, name, shape, dt):
            n = int(np.prod(shape[1:])) * isz(dt)
            self.off = (self.off + 63) // 64 * 64
            t = nc.alloc_sbuf_tensor_at(name, shape, dt, offset=self.base + self.off)
            self.map[name] = self.base + self.off
            self.off += n
            assert self.off <= ARENA_BYTES, f"arena overflow at {name}: {self.off}"
            return t

        def at(self, name, shape, dt, abs_off):
            return nc.alloc_sbuf_tensor_at(name, shape, dt, offset=abs_off)

    pa = Arena(ar_lo)
    p_stage = pa.al("p_stage", [128, H4], F32)
    p_stage2 = pa.al("p_stage2", [128, H4], F32)
    p_wbf = pa.al("p_wbf", [128, 2 * H4], BF16)
    p_qb = [pa.al(f"p_q{i}", [128, 2048], F32) for i in range(5)]
    p_qi = pa.al("p_qi", [128, 2048], I32)
    p_tr8 = pa.al("p_tr8", [128, 2 * 8 * 128], FP8)
    p_trb = pa.al("p_trb", [128, 2 * 8 * 128], BF16)
    p_row = pa.al("p_row", [16, D], F32)
    p_rowb = pa.al("p_rowb", [16, D], BF16)
    p_w1s = pa.al("p_w1s", [128, 2 * H4], BF16)

    aa = Arena(ar_lo)
    WQ = aa.al("WQ", [128, 8 * D], BF16)
    WK = aa.al("WK", [128, 8 * D], BF16)
    WV = aa.al("WV", [128, 8 * D], BF16)
    WO = aa.al("WO", [128, 8 * D], BF16)
    Xt = aa.al("Xt", [128, 4 * D], F32)
    ln1_tok = aa.al("ln1_tok", [128, D], BF16)
    ln1_fm = aa.al("ln1_fm", [128, 8 * 512], BF16)
    sqq = aa.al("sqq", [128, 512], F32)
    sqk = aa.al("sqk", [128, 512], F32)
    sqs = aa.al("sqs", [128, 512], F32)
    Vfm = aa.al("Vfm", [128, 8 * 512], BF16)
    QKfm = aa.al("QKfm", [128, 8 * 512], BF16)
    qkvfm = aa.al("qkvfm", [128, 8 * 512], BF16)
    coefb = aa.al("coefb", [16, 512], BF16)
    Atok = aa.al("Atok", [128, 4 * D], F32)
    lnA = aa.al("lnA", [128, D], F32)

    ba = Arena(ar_lo)
    W1Q = ba.al("W1Q", [128, 2 * 8 * 1024], BF16)
    W2QT = ba.al("W2QT", [128, 32 * D], FP8)
    ln2_fm = ba.al("ln2_fm", [128, 8 * 512], BF16)
    ln2_tok = ba.al("ln2_tok", [128, D], BF16)
    Hq = ba.al("Hq", [128, 32 * 512], FP8)
    Otok = ba.al("Otok", [128, 4 * D], F32)
    qsets = []
    for si in range(4):
        slots = {nm: ba.al(f"q{si}_{nm}", [128, 512], F32)
                 for nm in ["hv", "hc", "t64", "d2y"]}
        slots["g"] = ba.al(f"q{si}_g", [128, 512], BF16)
        slots["ii"] = ba.al(f"q{si}_ii", [128, 512], I32)
        qsets.append(slots)
    for si in (4, 5):
        k = si - 4
        ob = ba.map["Otok"] + k * 8192
        slots = {nm: ba.at(f"q{si}_{nm}", [128, 512], F32, ob + i * 2048)
                 for i, nm in enumerate(["hv", "hc", "t64", "d2y"])}
        slots["g"] = ba.at(f"q{si}_g", [128, 512], BF16,
                           ba.map["ln2_tok"] + k * 1024)
        slots["ii"] = ba.al(f"q{si}_ii", [128, 512], I32)
        qsets.append(slots)

    PT = nc.alloc_psum_tensor("psTR", [128, 1024], BF16)
    PS = [nc.alloc_psum_tensor(f"ps{i}", [128, 512], F32) for i in range(6)]
    PC = nc.alloc_psum_tensor("psC", [128, 512], F32)

    p = Prog(nc)
    for s in ["d", "g", "t", "v", "s", "o"]:
        p.mksem(s)
    for i, v in enumerate(CONSTS):
        nc.const_aps.register(F32, float(v), cpool[:, i:i + 1])

    def dma(q_, dst, src, inc, deps=(), **kw):
        p.wait(q_, *deps)
        return p.op(q_, lambda e, d=dst, s=src: e.dma_start(d, s, **kw),
                    inc=inc, amt=16)

    def ts1(e, out, in_, s1, o1):
        return e.tensor_single_scalar(out, in_, s1, o1)

    for i, v in enumerate(CONSTS):
        p.op("v", lambda e, i=i, v=v: e.memset(cpool[:, i:i + 1], float(v)))
    p.op("v", lambda e: e.memset(stats[:, :], 1.0), inc="v")

    dma("y", ident[:, :], id_h[:, :], "d")
    dma("y", msel.ap().rearrange("p (c h) -> p c h", h=16),
        msel_h.ap().rearrange("(c p) h -> p c h", p=128), "d")
    dma("y", mselT[:, :], mselT_h[:, :], "d")
    dma("y", onesp[:, :], onesp_h[:, :], "d")
    for dst, src in [(bqc, bq_h), (bkc, bk_h), (bvc, bv_h)]:
        p.op("y", lambda e, d=dst, s=src: e.dma_start(
            d[:, :], s.ap().rearrange("(c p) -> p c", p=128),
            allow_slow_non_contiguous=True), inc="d", amt=16)
    p.op("y", lambda e: e.dma_start(
        b1c[:, :], b1_h.ap().rearrange("(c p) -> p c", p=128),
        allow_slow_non_contiguous=True), inc="d", amt=16)

    def chains_stage1(lanes):
        for L in lanes:
            w = L["width"]; b = L["bufset"]
            for nm in ["hv", "hc", "t64", "d2y", "g"]:
                L[nm] = b[nm][:, :w]
            L["ii"] = b["ii"][:, :w]
            L["sl"] = 2.0 * LN2 if L["signed"] else LN2
        for L in lanes:
            if L.get("relu_bias") is not None:
                p.wait("s", L.get("psum_tok"), L.get("war_v"))
                L["tk1"] = p.op("s", lambda e, L=L: e.activation(
                    L["hv"], L["psum_src"], AF.Relu, bias=L["relu_bias"],
                    scale=1.0), inc="s")
                L["xs"] = L["hv"]
            elif L["signed"]:
                p.wait("s", L.get("src_tok"), L.get("war_v"))
                L["tk1"] = p.op("s", lambda e, L=L: e.activation(
                    L["d2y"], L["xsrc"], AF.Square, bias=0.0,
                    scale=2.0 ** 16), inc="s")
                L["xs"] = L["xsrc"]
            else:
                L["tk1"] = None
                L["xs"] = L["xsrc"]
        for L in lanes:
            if L.get("relu_bias") is not None:
                p.wait("v", L["tk1"], L.get("war_s"))
                L["tk2"] = p.op("v", lambda e, L=L: ts1(
                    e, L["hc"], L["hv"], L["clamp_lo"], OP.max), inc="v")
            elif L["signed"]:
                csq = float(L["clamp_lo"]) ** 2 * 2.0 ** 32
                p.wait("v", L["tk1"], L.get("war_s"))
                L["tk2"] = p.op("v", lambda e, L=L, c=csq: ts1(
                    e, L["hc"], L["d2y"], c, OP.max), inc="v")
            else:
                p.wait("v", L.get("src_tok"), L.get("war_s"))
                L["tk2"] = p.op("v", lambda e, L=L: ts1(
                    e, L["hc"], L["xsrc"], L["clamp_lo"], OP.max), inc="v")
        for L in lanes:
            p.wait("s", L["tk2"], L.get("war_v") if L["tk1"] is None else None)
            sc = 1.0 if L["signed"] else 2.0 ** 32
            p.op("s", lambda e, L=L, sc=sc: e.activation(
                L["t64"], L["hc"], AF.Ln, bias=0.0, scale=sc), inc="s")
        for L in lanes:
            L["tk4"] = p.op("s", lambda e, L=L: e.activation(
                L["ii"], L["t64"], AF.Copy, bias=0.0, scale=1.0 / L["sl"]), inc="s")
        for L in lanes:
            p.wait("v", L["tk4"])
            p.op("v", lambda e, L=L: e.scalar_tensor_tensor(
                L["d2y"], L["ii"], L["sl"], L["t64"], OP.mult, OP.is_gt), inc="v")
        for L in lanes:
            L["tk7"] = p.op("v", lambda e, L=L: e.tensor_tensor(
                L["hc"], L["ii"], L["d2y"], OP.subtract), inc="v")
        for L in lanes:
            p.wait("s", L["tk7"])
            p.op("s", lambda e, L=L: e.activation(
                L["t64"], L["hc"], AF.Exp, bias=L["bias_p"], scale=-LN2), inc="s")
        for L in lanes:
            L["g_tok"] = p.op("s", lambda e, L=L: e.activation(
                L["g"], L["hc"], AF.Exp, bias=L["bias_m"], scale=LN2), inc="s")
        return lanes

    def chains_stage2(lanes):
        for L in lanes:
            p.wait("v", L["g_tok"])
            p.op("v", lambda e, L=L: e.tensor_tensor(
                L["d2y"], L["xs"], L["t64"], OP.mult), inc="v")
        for L in lanes:
            L["tk11"] = p.op("v", lambda e, L=L: ts1(
                e, L["ii"], L["d2y"], L["off_c"], OP.add), inc="v")
        for L in lanes:
            L["end_s"] = p.op("v", lambda e, L=L: e.scalar_tensor_tensor(
                L["hc"], L["d2y"], L["off_c"], L["ii"], OP.add, OP.is_lt), inc="v")
        for L in lanes:
            p.op("v", lambda e, L=L: e.tensor_tensor(
                L["t64"], L["ii"], L["hc"], OP.subtract), inc="v")
        for L in lanes:
            if L["off_sub"]:
                L["end_v"] = p.op("v", lambda e, L=L: e.scalar_tensor_tensor(
                    L["dst"], L["t64"], -float(L["off_sub"]), L["g"],
                    OP.add, OP.mult), inc="v")
            else:
                L["end_v"] = p.op("v", lambda e, L=L: e.tensor_tensor(
                    L["dst"], L["t64"], L["g"], OP.mult), inc="v")
        return lanes

    def quant_chain(width, xsrc, dst_ap, bias_p, bias_m, off_c, off_sub,
                    clamp_lo, signed, bufset, psum_tok=None, psum_src=None,
                    relu_bias=None, war_v=None, war_s=None, src_tok=None):
        L = dict(width=width, bufset=bufset, signed=signed, bias_p=bias_p,
                 bias_m=bias_m, clamp_lo=clamp_lo, war_v=war_v, war_s=war_s,
                 xsrc=xsrc, src_tok=src_tok, psum_src=psum_src,
                 psum_tok=psum_tok, relu_bias=relu_bias,
                 dst=dst_ap, off_c=off_c, off_sub=off_sub)
        chains_stage1([L])
        chains_stage2([L])
        return {"relu": L["tk1"], "end_v": L["end_v"], "end_s": L["end_s"]}

    def layer_norm_stats(idx_list, src, dump, src_tok=None, prev=None,
                         dump_war=None):
        n = len(idx_list)
        prev = prev or {}
        dwl = dump_war if isinstance(dump_war, list) else [dump_war]
        acc_tok = None
        for i, cidx in enumerate(idx_list):
            sl = src[:, cidx * D:(cidx + 1) * D]
            p.wait("v", src_tok if i == 0 else None,
                   prev.get("rstd") if i == 0 else None)
            p.op("v", lambda e, sl=sl, i=i: e.reduce_sum(
                stats[:, i:i + 1], sl, axis=mybir.AxisListType.X), inc="v")
            p.wait("s", src_tok if i == 0 else None,
                   prev.get("var") if i == 0 else None,
                   *(dwl if i == 0 else []))
            acc_tok = p.op("s", lambda e, sl=sl, i=i: e.activation(
                dump[:, :], sl, AF.Square, bias=0.0, scale=1.0,
                accum_out=stats[:, 16 + i:17 + i]), inc="s")
        p.wait("v", acc_tok)
        m_tok = p.op("v", lambda e, n=n: ts1(e, stats[:, 0:n], stats[:, 0:n],
                                             1.0 / D, OP.mult), inc="v")
        p.op("v", lambda e, n=n: ts1(e, stats[:, 16:16 + n], stats[:, 16:16 + n],
                                     1.0 / D, OP.mult), inc="v")
        p.wait("s", m_tok)
        msq_tok = p.op("s", lambda e, n=n: e.activation(
            stats[:, 48:48 + n], stats[:, 0:n], AF.Square, bias=0.0,
            scale=1.0), inc="s")
        p.wait("v", msq_tok)
        var_tok = p.op("v", lambda e, n=n: e.tensor_tensor(
            stats[:, 16:16 + n], stats[:, 16:16 + n], stats[:, 48:48 + n],
            OP.subtract), inc="v")
        p.wait("s", var_tok)
        p.op("s", lambda e, n=n: e.activation(stats[:, 48:48 + n], stats[:, 16:16 + n],
                                              AF.Ln, bias=C_EPS, scale=1.0), inc="s")
        rstd_tok = p.op("s", lambda e, n=n: e.activation(
            stats[:, 32:32 + n], stats[:, 48:48 + n], AF.Exp, bias=0.0,
            scale=-0.5), inc="s")
        return {"rstd": rstd_tok, "var": var_tok}

    w1_ld, w1_ev, w1_st = {}, {}, {}

    def w1_prefetch(dc):
        for hg in range(4):
            gidx = dc * 4 + hg
            slot = gidx % 8
            tok = None
            for j in range(8):
                hcn = hg * 8 + j
                tok = dma("g", p_w1s[:, slot * 1024 + j * 128: slot * 1024 + (j + 1) * 128],
                          w1_h[hcn * 128:(hcn + 1) * 128, dc * 128:(dc + 1) * 128],
                          "g", deps=[w1_ev.get(gidx - 8)] if j == 0 else ())
            w1_ld[gidx] = tok

    w1_prefetch(0)
    w1_prefetch(1)
    for wsrc, wdst in [(wq_h, wqb_h), (wk_h, wkb_h), (wv_h, wvb_h), (wo_h, wob_h)]:
        for c in range(4):
            dma("g", wdst[c * 256:(c + 1) * 256, :], wsrc[c * 256:(c + 1) * 256, :], "g")
    if stage < 1:
        return _finish(nc, p)
    p.op("v", lambda e: e.memset(p_row[:, :], 0.0), inc="v")
    tok_bo = dma("y", p_row[0:1, :], bo_h[:, :], "d", deps=["v"])
    p.wait("v", tok_bo)
    p.op("v", lambda e: e.tensor_copy(p_rowb[:, :], p_row[:, :]), inc="v")
    p.wait("t", "v")
    for half in range(2):
        p.op("t", lambda e, half=half: e.matmul(
            PS[0][:, :], onesp[:, :], p_rowb[:, half * 512:(half + 1) * 512],
            start=True, stop=True), inc="t")
        p.wait("v", "t")
        p.op("v", lambda e, half=half: e.tensor_copy(
            bo_bc[:, half * 512:(half + 1) * 512], PS[0][:, :]), inc="v")
        p.wait("t", "v")
    tok_b2 = dma("y", p_row[0:1, :], b2_h[:, :], "d", deps=["v"])
    p.wait("v", tok_b2)
    p.op("v", lambda e: e.tensor_copy(p_rowb[:, :], p_row[:, :]), inc="v")
    p.wait("t", "v")
    for half in range(2):
        p.op("t", lambda e, half=half: e.matmul(
            PS[0][:, :], onesp[:, :], p_rowb[:, half * 512:(half + 1) * 512],
            start=True, stop=True), inc="t")
        p.wait("v", "t")
        p.op("v", lambda e, half=half: e.tensor_copy(
            b2stage[:, half * 512:(half + 1) * 512], PS[0][:, :]), inc="v")
        p.wait("t", "v")
    pset = {nm: p_qb[k] for k, nm in enumerate(["hv", "hc", "t64", "d2y", "g"])}
    pset["ii"] = p_qi
    b2tok = quant_chain(D, b2stage[:, :D], b2q_bc[:, :], C_P26L2, C_M26L2,
                        C_4096_5, 4096, 2.0 ** -14, signed=True, bufset=pset,
                        war_v=p.latest("v"), war_s=p.latest("s"),
                        src_tok=p.latest("v"))
    if stage < 2:
        return _finish(nc, p)
    prep_sets = []
    for si in range(4):
        ps_ = {nm: p_qb[k].ap()[:, si * 512:(si + 1) * 512]
               for k, nm in enumerate(["hv", "hc", "t64", "d2y", "g"])}
        ps_["ii"] = p_qi.ap()[:, si * 512:(si + 1) * 512]
        prep_sets.append(ps_)
    prep_hist = {0: {"end_v": b2tok["end_v"], "end_s": b2tok["end_s"]},
                 1: {"end_v": b2tok["end_v"], "end_s": b2tok["end_s"]},
                 2: None, 3: None}
    st8_tok = {}

    tr_last = {}

    def w2_transposes(dc):
        p.wait("t", prep_hist[2]["end_v"], prep_hist[3]["end_v"])
        for hg in range(4):
            gidx = dc * 4 + hg
            buf = gidx % 2
            tr_tok = None
            for j in range(8):
                hcn = hg * 8 + j
                src = p_wbf[:, (dc % 2) * H4 + hcn * 128: (dc % 2) * H4 + (hcn + 1) * 128]
                tr_tok = p.op("t", lambda e, src=src, j=j: e.transpose(
                    PT[:, j * 128:(j + 1) * 128], src, ident[:, :]),
                    inc="t" if j == 7 else None)
            tr_last[dc] = tr_tok
            p.wait("v", tr_tok, st8_tok.get(gidx - 2))
            ev = p.op("v", lambda e, buf=buf: e.tensor_copy(
                p_tr8[:, buf * 1024:(buf + 1) * 1024], PT[:, :]), inc="v")
            p.wait("t", ev)
            for j in range(8):
                hcn = hg * 8 + j
                st8_tok[gidx] = dma(
                    "y", w2qt_h[hcn * 128:(hcn + 1) * 128, dc * 128:(dc + 1) * 128],
                    p_tr8[:, buf * 1024 + j * 128: buf * 1024 + (j + 1) * 128],
                    "d", deps=[ev] if j == 0 else ())
            slot = gidx % 8
            p.wait("t", w1_ld[gidx])
            trb = None
            for j in range(8):
                src = p_w1s[:, slot * 1024 + j * 128: slot * 1024 + (j + 1) * 128]
                trb = p.op("t", lambda e, src=src, j=j: e.transpose(
                    PT[:, j * 128:(j + 1) * 128], src, ident[:, :]),
                    inc="t" if j == 7 else None)
            p.wait("v", trb, w1_st.get(gidx - 2))
            evb = p.op("v", lambda e, buf=buf: e.tensor_copy(
                p_trb[:, buf * 1024:(buf + 1) * 1024], PT[:, :]), inc="v")
            w1_ev[gidx] = evb
            p.wait("t", evb)
            p.wait("g", evb)
            w1_st[gidx] = dma("g", w1t_h[dc * 128:(dc + 1) * 128,
                                         hg * 1024:(hg + 1) * 1024],
                              p_trb[:, buf * 1024:(buf + 1) * 1024], "g")
        if 2 <= dc + 1 <= 7:
            w1_prefetch(dc + 1)

    stg = [p_stage, p_stage2]
    dtoks = {0: dma("y", p_stage[:, :], w2_h[0:128, :], "d"),
             1: dma("y", p_stage2[:, :], w2_h[128:256, :], "d")}
    dc_flush = {}
    for dc in range(8):
        if 1 <= dc <= 6:
            dtoks[dc + 1] = dma("y", stg[(dc + 1) % 2][:, :],
                                w2_h[(dc + 1) * 128:(dc + 2) * 128, :], "d",
                                deps=dc_flush[dc - 1])
        p.wait("v", tr_last.get(dc - 2))
        pend = None
        for m2 in range(4):
            lanes = []
            for j in range(2):
                sl8 = 2 * m2 + j
                si = 2 * (m2 % 2) + j
                hist = prep_hist[si]
                lanes.append(dict(
                    width=512, bufset=prep_sets[si], signed=True,
                    bias_p=C_P19L2, bias_m=C_M19L2, clamp_lo=2.0 ** -6,
                    xsrc=stg[dc % 2][:, sl8 * 512:(sl8 + 1) * 512],
                    src_tok=dtoks[dc],
                    war_v=hist["end_v"] if hist else None,
                    war_s=hist["end_s"] if hist else None,
                    dst=p_wbf[:, (dc % 2) * H4 + sl8 * 512: (dc % 2) * H4 + (sl8 + 1) * 512],
                    off_c=C_64_5, off_sub=64, si=si))
            chains_stage1(lanes)
            if pend is not None:
                chains_stage2(pend)
                for L in pend:
                    prep_hist[L["si"]] = {"end_v": L["end_v"], "end_s": L["end_s"]}
            if dc > 0 and m2 == 1:
                w2_transposes(dc - 1)
            pend = lanes
        chains_stage2(pend)
        for L in pend:
            prep_hist[L["si"]] = {"end_v": L["end_v"], "end_s": L["end_s"]}
        dc_flush[dc] = [prep_hist[2]["end_v"], prep_hist[3]["end_v"],
                        prep_hist[2]["end_s"], prep_hist[3]["end_s"]]
    w2_transposes(7)
    if debug:
        p.wait("y", "v")
        dma("y", dbg["d_w2q"][:, :], p_wbf[:, 0:H4], "o")
        dma("y", dbg["d_pg"][:, :], p_qb[4][:, :], "o")
        dma("y", dbg["d_pr"][:, :], p_qb[2][:, :], "o")
        dma("y", dbg["d_psrc"][:, :], p_stage[:, :], "o")
    if stage < 3:
        return _finish(nc, p)

    if stage < 4:
        return _finish(nc, p)
    stats_prev = None
    tok_u_ready = None
    for sup in range(nsup):
        base_tok = sup * sup_tok
        p.barrier_on("y")
        p.barrier_on("g")
        for Wd, Ws in [(WQ, wqb_h), (WK, wkb_h), (WV, wvb_h), (WO, wob_h)]:
            for c in range(8):
                dma("g", Wd[:, c * D:(c + 1) * D], Ws[c * 128:(c + 1) * 128, :], "d")
        tok_wload = p.latest("d")
        tok_qkv_last = None
        tok_wo_last = None
        tok_coef_last = None
        tok_bc_last = None
        tok_uadd = [None] * 4
        for lt in range(ta_per_sup):
            t0 = base_tok + lt * 512
            xt_tok = []
            for tc in range(4):
                xt_tok.append(dma("y", Xt[:, tc * D:(tc + 1) * D],
                                  x_h[t0 + tc * 128:t0 + tc * 128 + 128, :], "d",
                                  deps=[tok_uadd[tc],
                                        stats_prev["rstd"] if stats_prev else None]))
            stats_prev = layer_norm_stats([0, 1, 2, 3], Xt, lnA,
                                          src_tok=xt_tok[3], prev=stats_prev,
                                          dump_war=tok_uadd[3])
            rstd_tok = stats_prev["rstd"]
            tr_tok = None
            ev_toks = []
            for tc in range(4):
                p.wait("v", rstd_tok, tok_qkv_last)
                ap_tok = p.op("v", lambda e, tc=tc: e.tensor_scalar(
                    ln1_tok[:, :], Xt[:, tc * D:(tc + 1) * D],
                    stats[:, tc:tc + 1], stats[:, 32 + tc:33 + tc],
                    OP.subtract, OP.mult), inc="v")
                p.wait("t", ap_tok, ev_toks[-1] if ev_toks else None)
                for fc in range(8):
                    tr_tok = p.op("t", lambda e, fc=fc: e.transpose(
                        PT[:, fc * 128:(fc + 1) * 128],
                        ln1_tok[:, fc * 128:(fc + 1) * 128], ident[:, :]),
                        inc="t" if fc == 7 else None)
                p.wait("v", tr_tok)
                ev_toks.append(p.op("v", lambda e, tc=tc: e.tensor_copy(
                    _fmcols(ln1_fm, 512, tc), PT[:, :]), inc="v"))
            if debug:
                dma("y", dbg["d_ln1fm"][:, :], ln1_fm[:, :], "o", deps=[ev_toks[-1]])
            rd = {}
            exp_tok = {}
            vfm_tok = {}
            sqs_tok = None
            for oc in range(8):
                par = oc % 2
                psq, psk, psv = PS[3 * par], PS[3 * par + 1], PS[3 * par + 2]
                if oc >= 2:
                    p.wait("t", *rd[oc - 2])
                else:
                    p.wait("t", ev_toks[-1], tok_wload, tok_wo_last)
                mm_toks = []
                for W_, psx in [(WQ, psq), (WK, psk), (WV, psv)]:
                    for fc in range(8):
                        lhsT = W_[:, fc * D + oc * 128: fc * D + (oc + 1) * 128]
                        rhs = ln1_fm[:, fc * 512:(fc + 1) * 512]
                        tk = p.op("t", lambda e, lhsT=lhsT, rhs=rhs, psx=psx, fc=fc: e.matmul(
                            psx[:, :], lhsT, rhs, start=(fc == 0), stop=(fc == 7)),
                            inc="t" if fc == 7 else None)
                        if tk:
                            mm_toks.append(tk)
                tok_qkv_last = mm_toks[-1]
                p.wait("v", mm_toks[2])
                vfm_tok[oc] = p.op("v", lambda e, oc=oc, psv=psv: ts1(
                    e, Vfm[:, oc * 512:(oc + 1) * 512], psv[:, :],
                    bvc[:, oc:oc + 1], OP.add), inc="v")
                p.wait("s", mm_toks[0], sqs_tok)
                p.op("s", lambda e, oc=oc, psq=psq: e.activation(
                    sqq[:, :], psq[:, :], AF.Square, bias=bqc[:, oc:oc + 1],
                    scale=1.0), inc="s")
                p.wait("s", mm_toks[1])
                sq_tok = p.op("s", lambda e, oc=oc, psk=psk: e.activation(
                    sqk[:, :], psk[:, :], AF.Square, bias=bkc[:, oc:oc + 1],
                    scale=1.0), inc="s")
                p.wait("v", sq_tok, exp_tok.get(oc - 1), tok_coef_last if oc == 0 else None)
                sqs_tok = p.op("v", lambda e: e.tensor_add(sqs[:, :], sqq[:, :],
                                                           sqk[:, :]), inc="v")
                p.wait("s", sqs_tok)
                exp_tok[oc] = p.op("s", lambda e, oc=oc: e.activation(
                    QKfm[:, oc * 512:(oc + 1) * 512], sqs[:, :], AF.Exp,
                    bias=0.0, scale=-0.5), inc="s")
                rd[oc] = (vfm_tok[oc], sq_tok)
            if debug:
                dma("y", dbg["d_qkfm"][:, :], QKfm[:, :], "o", deps=[exp_tok[7]])
            p.wait("t", tok_bc_last)
            for oc in range(8):
                p.wait("t", exp_tok[oc])
                tok_coef_last = p.op("t", lambda e, oc=oc: e.matmul(
                    PC[0:16, :], msel[:, oc * 16:(oc + 1) * 16],
                    QKfm[:, oc * 512:(oc + 1) * 512],
                    start=(oc == 0), stop=(oc == 7)),
                    inc="t" if oc == 7 else None)
            p.wait("v", tok_coef_last)
            coefb_tok = p.op("v", lambda e: e.tensor_copy(coefb[:, :], PC[0:16, :]), inc="v")
            stt_toks = {}
            for oc in range(8):
                bank = PS[3] if oc % 2 == 0 else PC
                p.wait("t", coefb_tok if oc < 2 else stt_toks[oc - 2],
                       rd[7][1] if oc == 0 else None)
                bc_tok = p.op("t", lambda e, oc=oc, bank=bank: e.matmul(
                    bank[0:128, :], mselT[:, oc * 128:(oc + 1) * 128],
                    coefb[:, :], start=True, stop=True), inc="t")
                p.wait("v", bc_tok, tok_wo_last if oc == 0 else None)
                stt_toks[oc] = p.op("v", lambda e, oc=oc, bank=bank: e.scalar_tensor_tensor(
                    qkvfm[:, oc * 512:(oc + 1) * 512], bank[0:128, :], 1.0,
                    Vfm[:, oc * 512:(oc + 1) * 512], OP.bypass, OP.mult), inc="v")
            tok_bc_last_list = []
            ev2 = {}
            for tp in range(4):
                for oh in range(2):
                    pse = PS[4 + oh]
                    p.wait("t", stt_toks[7], ev2.get(tp - 1),
                           rd[7][1] if tp == 0 else None,
                           rd[7][0] if tp == 0 else None)
                    for fc in range(8):
                        lhsT = qkvfm[:, fc * 512 + tp * 128: fc * 512 + (tp + 1) * 128]
                        rhs = WO[:, fc * D + oh * 512: fc * D + (oh + 1) * 512]
                        tok_wo_last = p.op("t", lambda e, lhsT=lhsT, rhs=rhs, pse=pse, fc=fc: e.matmul(
                            pse[:, :], lhsT, rhs, start=(fc == 0), stop=(fc == 7)),
                            inc="t" if fc == 7 else None)
                    p.wait("v", tok_wo_last, tok_uadd[tp])
                    ev2[tp] = p.op("v", lambda e, tp=tp, oh=oh, pse=pse: e.tensor_tensor(
                        Atok[:, tp * D + oh * 512: tp * D + (oh + 1) * 512],
                        pse[:, :], bo_bc[:, oh * 512:(oh + 1) * 512],
                        OP.add), inc="v")
            if debug:
                dma("y", dbg["d_a"][:, :], Atok[:, :], "o", deps=[ev2[3]])
            stats_prev = layer_norm_stats([0, 1, 2, 3], Atok, lnA,
                                          src_tok=ev2[3], prev=stats_prev)
            rstd_tok = stats_prev["rstd"]
            for tc in range(4):
                cc = lt * 4 + tc
                p.wait("v", rstd_tok)
                p.op("v", lambda e, tc=tc: e.tensor_scalar(
                    lnA[:, :], Atok[:, tc * D:(tc + 1) * D],
                    stats[:, tc:tc + 1], stats[:, 32 + tc:33 + tc],
                    OP.subtract, OP.mult), inc="v")
                tok_uadd[tc] = p.op("v", lambda e, tc=tc, cc=cc: e.tensor_tensor(
                    U[:, cc * D:(cc + 1) * D], Xt[:, tc * D:(tc + 1) * D],
                    lnA[:, :], OP.add), inc="v")
                tok_u_ready = tok_uadd[tc]
        if debug:
            dma("y", dbg["d_u"][:, :], U[:, :], "o", deps=[tok_uadd[3]])
        if stage < 5:
            continue
        for c in range(32):
            dma("g", W2QT[:, c * D:(c + 1) * D], w2qt_h[c * 128:(c + 1) * 128, :],
                "d", deps=[tok_qkv_last, tok_wo_last] if c == 0 else ())
        tok_wload = p.latest("d")
        qld, qmm_last = {}, {}

        def w1_qload(gq):
            q4 = gq % 4
            slot = gq % 2
            tok = None
            for dcq in range(8):
                tok = dma("g", W1Q[:, slot * 8192 + dcq * 1024:
                                   slot * 8192 + (dcq + 1) * 1024],
                          w1t_h[dcq * 128:(dcq + 1) * 128, q4 * 1024:(q4 + 1) * 1024],
                          "d", deps=[qmm_last.get(gq - 2)] if dcq == 0 else ())
            qld[gq] = tok

        w1_qload(0)
        w1_qload(1)
        tok_b2_last = None
        tok_mm1_last = None
        tok_ost = {}
        tr_tok = None
        ev_toks = []
        ev3_prev = {}
        chain_hist = {si: None for si in range(6)}

        def b_head(bt_):
            nonlocal stats_prev, tr_tok, ev_toks
            cc0_ = bt_ * 4
            dw = [tr_tok]
            if chain_hist[4]:
                dw += [chain_hist[4]["end_v"], chain_hist[5]["end_v"]]
            stats_prev = layer_norm_stats([cc0_, cc0_ + 1, cc0_ + 2, cc0_ + 3],
                                          U, ln2_tok, src_tok=tok_u_ready,
                                          prev=stats_prev, dump_war=dw)
            rstd_tok = stats_prev["rstd"]
            ev_toks = []
            for k in range(4):
                p.wait("v", rstd_tok, tok_mm1_last)
                ap_tok = p.op("v", lambda e, k=k, cc0_=cc0_: e.tensor_scalar(
                    ln2_tok[:, :], U[:, (cc0_ + k) * D:(cc0_ + k + 1) * D],
                    stats[:, k:k + 1], stats[:, 32 + k:33 + k],
                    OP.subtract, OP.mult), inc="v")
                p.wait("t", ap_tok, ev_toks[-1] if ev_toks else None)
                for fc in range(8):
                    tr_tok = p.op("t", lambda e, fc=fc: e.transpose(
                        PT[:, fc * 128:(fc + 1) * 128],
                        ln2_tok[:, fc * 128:(fc + 1) * 128], ident[:, :]),
                        inc="t" if fc == 7 else None)
                p.wait("v", tr_tok)
                ev_toks.append(p.op("v", lambda e, k=k: e.tensor_copy(
                    _fmcols(ln2_fm, 512, k), PT[:, :]), inc="v"))

        b_head(0)
        for bt in range(tb_per_sup):
            cc0 = bt * 4
            pe3 = ev3_prev
            relu_toks = {}
            pend = None
            last_v = None
            for g_ in range(11):
                lanes = []
                for ln_ in range(3):
                    hcn = 3 * g_ + ln_
                    if hcn >= 32:
                        break
                    gq = bt * 4 + hcn // 8
                    slot = gq % 2
                    psh = PS[hcn % 4]
                    p.wait("t", ev_toks[-1] if hcn == 0 else None,
                           tok_wload if hcn == 0 else None,
                           qld[gq] if hcn % 8 == 0 else None,
                           pe3.get(4) if hcn == 2 else None,
                           pe3.get(5) if hcn == 3 else None,
                           relu_toks.get(hcn - 4))
                    mmtok = None
                    for dc in range(8):
                        o0 = slot * 8192 + dc * 1024 + (hcn % 8) * 128
                        lhsT = W1Q[:, o0:o0 + 128]
                        rhs = ln2_fm[:, dc * 512:(dc + 1) * 512]
                        mmtok = p.op("t", lambda e, lhsT=lhsT, rhs=rhs, psh=psh, dc=dc: e.matmul(
                            psh[:, :], lhsT, rhs, start=(dc == 0), stop=(dc == 7)),
                            inc="t" if dc == 7 else None)
                    tok_mm1_last = mmtok
                    if hcn % 8 == 7:
                        qmm_last[gq] = mmtok
                        if gq + 2 < 4 * tb_per_sup:
                            w1_qload(gq + 2)
                    si = 3 * (g_ % 2) + ln_
                    hist = chain_hist[si]
                    lanes.append(dict(
                        width=512, bufset=qsets[si], signed=False,
                        bias_p=C_P35L2, bias_m=C_M35L2, clamp_lo=2.0 ** -6,
                        psum_src=psh[:, :], psum_tok=mmtok,
                        relu_bias=b1c[:, hcn:hcn + 1],
                        war_v=hist["end_v"] if hist else None,
                        war_s=hist["end_s"] if hist else None,
                        dst=Hq[:, hcn * 512:(hcn + 1) * 512],
                        off_c=C_HALF, off_sub=0, hcn=hcn, si=si))
                chains_stage1(lanes)
                for L in lanes:
                    relu_toks[L["hcn"]] = L["tk1"]
                if pend is not None:
                    chains_stage2(pend)
                    for L in pend:
                        chain_hist[L["si"]] = {"end_v": L["end_v"], "end_s": L["end_s"]}
                    last_v = pend[-1]["end_v"]
                pend = lanes
            chains_stage2(pend)
            for L in pend:
                chain_hist[L["si"]] = {"end_v": L["end_v"], "end_s": L["end_s"]}
            last_v = pend[-1]["end_v"]
            if debug and bt == 0:
                dma("y", dbg["d_h"][:, :], qsets[1]["hv"][:, :512], "o",
                    deps=[last_v])
                dma("y", dbg["d_hq"][:, :], Hq[:, :], "o", deps=[last_v])
            ev3 = {}
            gi = 0
            for tp in range(4):
                for dh in range(2):
                    pso = PS[2 + gi % 4]
                    if gi == 0:
                        p.wait("t", last_v, relu_toks[30])
                    elif gi == 1:
                        p.wait("t", relu_toks[31])
                    elif gi >= 4:
                        p.wait("t", ev3[gi - 4])
                    for hcn in range(32):
                        lhsT = Hq[:, hcn * 512 + tp * 128: hcn * 512 + (tp + 1) * 128]
                        rhs = W2QT[:, hcn * D + dh * 512: hcn * D + (dh + 1) * 512]
                        tok_b2_last = p.op("t", lambda e, lhsT=lhsT, rhs=rhs, pso=pso, hcn=hcn: e.matmul(
                            pso[:, :], lhsT, rhs, start=(hcn == 0), stop=(hcn == 31)),
                            inc="t" if hcn == 31 else None)
                    p.wait("v", tok_b2_last, tok_ost.get(gi // 2))
                    ev3[gi] = p.op("v", lambda e, tp=tp, dh=dh, pso=pso: e.tensor_tensor(
                        Otok[:, tp * D + dh * 512: tp * D + (dh + 1) * 512],
                        pso[:, :], b2q_bc[:, dh * 512:(dh + 1) * 512],
                        OP.add), inc="v")
                    gi += 1
            ev3_prev = ev3
            if bt + 1 < tb_per_sup:
                b_head(bt + 1)
            if debug and bt == 0:
                dma("y", dbg["d_o"][:, :], Otok[:, :], "o", deps=[ev3[7]])
            pend = None
            st2last = None
            for m_ in range(4):
                lanes = []
                for j in range(2):
                    seg = 2 * m_ + j
                    si = 2 * (m_ % 2) + j
                    hist = chain_hist[si]
                    lanes.append(dict(
                        width=512, bufset=qsets[si], signed=True,
                        bias_p=C_P26L2, bias_m=C_M26L2, clamp_lo=2.0 ** -14,
                        xsrc=Otok[:, seg * 512:(seg + 1) * 512],
                        src_tok=ev3[seg],
                        war_v=hist["end_v"] if hist else None,
                        war_s=hist["end_s"] if hist else None,
                        dst=Otok[:, seg * 512:(seg + 1) * 512],
                        off_c=C_4096_5, off_sub=4096, si=si))
                chains_stage1(lanes)
                if pend is not None:
                    chains_stage2(pend)
                    for L in pend:
                        chain_hist[L["si"]] = {"end_v": L["end_v"], "end_s": L["end_s"]}
                pend = lanes
            chains_stage2(pend)
            for L in pend:
                chain_hist[L["si"]] = {"end_v": L["end_v"], "end_s": L["end_s"]}
            st2last = pend[-1]["end_v"]
            if debug and bt == 0:
                dma("y", dbg["d_oq"][:, :], Otok[:, :], "o", deps=[st2last])
            stats_prev = layer_norm_stats([0, 1, 2, 3], Otok, b2stage,
                                          src_tok=st2last, prev=stats_prev,
                                          dump_war=None)
            rstd_tok = stats_prev["rstd"]
            for k in range(4):
                cc = cc0 + k
                p.wait("v", rstd_tok, tok_ost.get(k))
                p.op("v", lambda e, k=k: e.tensor_scalar(
                    Otok[:, k * D:(k + 1) * D], Otok[:, k * D:(k + 1) * D],
                    stats[:, k:k + 1], stats[:, 32 + k:33 + k],
                    OP.subtract, OP.mult), inc="v")
                ytok = p.op("v", lambda e, k=k, cc=cc: e.tensor_tensor(
                    Otok[:, k * D:(k + 1) * D], U[:, cc * D:(cc + 1) * D],
                    Otok[:, k * D:(k + 1) * D], OP.add), inc="v")
                r0 = base_tok + cc * 128
                tok_ost[k] = dma("y", y_h[r0:r0 + 128, :], Otok[:, k * D:(k + 1) * D],
                                 "o", deps=[ytok])

    p.wait("y", "o")
    return _finish(nc, p)


def _finish(nc, p):
    p.barrier_on("y")
    with nc.Block() as block:
        block.tensor(lambda e: [fn(e) for fn in p.q["t"]])
        block.vector(lambda e: [fn(e) for fn in p.q["v"]])
        block.scalar(lambda e: [fn(e) for fn in p.q["s"]])
        block.sync(lambda e: [fn(e) for fn in p.q["y"]])
        block.gpsimd(lambda e: [fn(e) for fn in p.q["g"]])
    return nc


def _make_const_inputs():
    hd = 64
    msel = np.zeros((D, 16), dtype=np.float32)
    for d_ in range(D):
        msel[d_, d_ // hd] = 1.0
    onesp = np.zeros((16, 128), dtype=np.float32)
    onesp[0, :] = 1.0
    return {
        "cident": np.eye(128, dtype=np.float32).astype(ml_dtypes.bfloat16),
        "msel": msel.astype(ml_dtypes.bfloat16),
        "mselT": msel.T.copy().astype(ml_dtypes.bfloat16),
        "onesp": onesp.astype(ml_dtypes.bfloat16),
    }


def kernel(L0_attn_x, L0_attn_Wq, L0_attn_bq, L0_attn_Wk, L0_attn_bk,
           L0_attn_Wv, L0_attn_bv, L0_attn_Wo, L0_attn_bo,
           L0_ffn_w1, L0_ffn_b1, L0_ffn_w2, L0_ffn_b2):
    x = np.ascontiguousarray(np.asarray(L0_attn_x, dtype=np.float32))
    B, S, Dm = x.shape
    tok = B * S
    assert Dm == D and tok % NCORES == 0
    tokc = tok // NCORES
    xf = x.reshape(tok, D)

    nc = build_program(tokc)
    consts = _make_const_inputs()
    common = {
        "wq": np.ascontiguousarray(L0_attn_Wq, dtype=np.float32),
        "wk": np.ascontiguousarray(L0_attn_Wk, dtype=np.float32),
        "wv": np.ascontiguousarray(L0_attn_Wv, dtype=np.float32),
        "wo": np.ascontiguousarray(L0_attn_Wo, dtype=np.float32),
        "bq": np.ascontiguousarray(L0_attn_bq, dtype=np.float32),
        "bk": np.ascontiguousarray(L0_attn_bk, dtype=np.float32),
        "bv": np.ascontiguousarray(L0_attn_bv, dtype=np.float32),
        "bo": np.ascontiguousarray(L0_attn_bo, dtype=np.float32).reshape(1, D),
        "w1": np.ascontiguousarray(L0_ffn_w1, dtype=np.float32),
        "b1": np.ascontiguousarray(L0_ffn_b1, dtype=np.float32),
        "w2": np.ascontiguousarray(L0_ffn_w2, dtype=np.float32),
        "b2": np.ascontiguousarray(L0_ffn_b2, dtype=np.float32).reshape(1, D),
        **consts,
    }
    in_maps = []
    for c in range(NCORES):
        m = dict(common)
        m["x"] = np.ascontiguousarray(xf[c * tokc:(c + 1) * tokc])
        in_maps.append(m)
    res = run_bass_kernel_spmd(nc, in_maps, core_ids=list(range(NCORES)))
    out = np.concatenate([r["y"] for r in res.results], axis=0)
    return out.reshape(B, S, D).astype(np.float32)


# version 13 (final)
# speedup vs baseline: 1.3134x; 1.0262x over previous
import os
import numpy as np
import ml_dtypes
import concourse.bass as bass
import concourse.mybir as mybir
from concourse.bass_utils import run_bass_kernel_spmd

F32 = mybir.dt.float32
BF16 = mybir.dt.bfloat16
FP8 = mybir.dt.float8e4
I32 = mybir.dt.int32
AF = mybir.ActivationFunctionType
OP = mybir.AluOpType

LN2 = 0.6931471805599453
D = 1024
H4 = 4096
NCORES = 8

C_EPS = 1e-5
C_HALF = 0.5
C_64_5 = 64.5
C_4096_5 = 4096.5
C_P35L2 = 35.0 * LN2
C_M35L2 = -35.0 * LN2
C_P19L2 = 19.0 * LN2
C_M19L2 = -19.0 * LN2
C_P26L2 = 26.0 * LN2
C_M26L2 = -26.0 * LN2
CONSTS = [C_EPS, C_HALF, C_64_5, C_4096_5, C_P35L2, C_M35L2,
          C_P19L2, C_M19L2, C_P26L2, C_M26L2]


class Prog:

    OWN = {"t": "t", "v": "v", "s": "s"}

    def __init__(self, nc):
        self.nc = nc
        self.q = {"t": [], "v": [], "s": [], "y": [], "g": []}
        self.sem = {}
        self.cnt = {}
        self.lastw = {e: {} for e in self.q}

    def mksem(self, name):
        self.sem[name] = self.nc.alloc_semaphore(name)
        self.cnt[name] = 0

    def latest(self, sem):
        return (sem, self.cnt[sem])

    def wait(self, eng, *deps):
        for dp in deps:
            if dp is None:
                continue
            if isinstance(dp, str):
                dp = (dp, self.cnt[dp])
            sem, c = dp
            if c <= 0:
                continue
            if self.OWN.get(eng) == sem:
                continue
            if self.lastw[eng].get(sem, 0) >= c:
                continue
            self.lastw[eng][sem] = c
            self.q[eng].append(lambda e, h=self.sem[sem], c=c: e.wait_ge(h, c))

    def op(self, eng, fn, inc=None, amt=1):
        if inc is None:
            self.q[eng].append(fn)
            return None
        h = self.sem[inc]
        self.cnt[inc] += amt
        self.q[eng].append(lambda e, fn=fn, h=h, a=amt: fn(e).then_inc(h, a))
        return (inc, self.cnt[inc])

    def barrier_on(self, eng):
        self.wait(eng, *list(self.sem.keys()))


def _fmcols(buf, w, tc):
    return buf.ap().rearrange("p (c w) -> p c w", w=w)[:, :, tc * 128:(tc + 1) * 128]


def build_program(tokc, debug=False, stage=99):
    assert tokc % 512 == 0
    ntile_a = tokc // 512
    nsup = 2 if ntile_a % 2 == 0 else 1
    ta_per_sup = ntile_a // nsup
    sup_tok = tokc // nsup
    ncc = sup_tok // 128
    tb_per_sup = ncc // 4

    nc = bass.Bass("TRN2", target_bir_lowering=False)

    x_h = nc.dram_tensor("x", [tokc, D], F32, kind="ExternalInput")
    y_h = nc.dram_tensor("y", [tokc, D], F32, kind="ExternalOutput")
    wq_h = nc.dram_tensor("wq", [D, D], F32, kind="ExternalInput")
    wk_h = nc.dram_tensor("wk", [D, D], F32, kind="ExternalInput")
    wv_h = nc.dram_tensor("wv", [D, D], F32, kind="ExternalInput")
    wo_h = nc.dram_tensor("wo", [D, D], F32, kind="ExternalInput")
    bq_h = nc.dram_tensor("bq", [D], F32, kind="ExternalInput")
    bk_h = nc.dram_tensor("bk", [D], F32, kind="ExternalInput")
    bv_h = nc.dram_tensor("bv", [D], F32, kind="ExternalInput")
    bo_h = nc.dram_tensor("bo", [1, D], F32, kind="ExternalInput")
    w1_h = nc.dram_tensor("w1", [H4, D], F32, kind="ExternalInput")
    b1_h = nc.dram_tensor("b1", [H4], F32, kind="ExternalInput")
    w2_h = nc.dram_tensor("w2", [D, H4], F32, kind="ExternalInput")
    b2_h = nc.dram_tensor("b2", [1, D], F32, kind="ExternalInput")
    id_h = nc.dram_tensor("cident", [128, 128], BF16, kind="ExternalInput")
    msel_h = nc.dram_tensor("msel", [D, 16], BF16, kind="ExternalInput")
    mselT_h = nc.dram_tensor("mselT", [16, D], BF16, kind="ExternalInput")
    onesp_h = nc.dram_tensor("onesp", [16, 128], BF16, kind="ExternalInput")
    wqb_h = nc.dram_tensor("wqb", [D, D], BF16)
    wkb_h = nc.dram_tensor("wkb", [D, D], BF16)
    wvb_h = nc.dram_tensor("wvb", [D, D], BF16)
    wob_h = nc.dram_tensor("wob", [D, D], BF16)
    w1t_h = nc.dram_tensor("w1t", [D, H4], BF16)
    w2qt_h = nc.dram_tensor("w2qt", [H4, D], FP8)
    dbg = {}
    if debug:
        for nm, shp, dt in [("d_ln1fm", [128, 8 * 512], BF16),
                            ("d_qkfm", [128, 8 * 512], BF16),
                            ("d_qkv", [128, 8 * 512], BF16),
                            ("d_a", [128, 4 * D], F32),
                            ("d_u", [128, ncc * D], BF16),
                            ("d_h", [128, 512], F32),
                            ("d_hq", [128, 32 * 512], FP8),
                            ("d_o", [128, 4 * D], F32),
                            ("d_oq", [128, 4 * D], F32),
                            ("d_w2q", [128, H4], BF16),
                            ("d_pg", [128, 2048], F32),
                            ("d_pr", [128, 2048], F32),
                            ("d_psrc", [128, H4], F32),
                            ("d_xt", [128, 4 * D], F32),
                            ("d_stats", [128, 16], F32)]:
            dbg[nm] = nc.dram_tensor(nm, shp, dt, kind="ExternalOutput")

    def isz(dt):
        return np.dtype(mybir.dt.np(dt)).itemsize

    A = lambda name, shape, dt: nc.alloc_sbuf_tensor(name, shape, dt)
    ident = A("ident", [128, 128], BF16)
    msel = A("mselb", [128, 8 * 16], BF16)
    mselT = A("mselTb", [16, 8 * 128], BF16)
    onesp = A("onespb", [16, 128], BF16)
    bqc = A("bqc", [128, 8], F32)
    bkc = A("bkc", [128, 8], F32)
    bvc = A("bvc", [128, 8], F32)
    b1c = A("b1c", [128, 32], F32)
    bo_bc = A("bo_bc", [128, D], F32)
    b2q_bc = A("b2q_bc", [128, D], F32)
    cpool = A("cpool", [128, len(CONSTS)], F32)
    stats = A("stats", [128, 64], F32)
    b2stage = A("b2stage", [128, D], F32)
    ln2_tok2 = A("ln2_tok2", [128, D], BF16)
    U = A("U", [128, ncc * D], BF16)

    ARENA_BYTES = 157696
    r = nc.bump_sbuf(ARENA_BYTES, align_bytes=512)
    assert r is not None, "SBUF arena allocation failed"
    ar_lo = r[0]

    class Arena:
        def __init__(self, base):
            self.base, self.off = base, 0
            self.map = {}

        def al(self, name, shape, dt):
            n = int(np.prod(shape[1:])) * isz(dt)
            self.off = (self.off + 63) // 64 * 64
            t = nc.alloc_sbuf_tensor_at(name, shape, dt, offset=self.base + self.off)
            self.map[name] = self.base + self.off
            self.off += n
            assert self.off <= ARENA_BYTES, f"arena overflow at {name}: {self.off}"
            return t

        def at(self, name, shape, dt, abs_off):
            return nc.alloc_sbuf_tensor_at(name, shape, dt, offset=abs_off)

    pa = Arena(ar_lo)
    p_stage = pa.al("p_stage", [128, H4], F32)
    p_stage2 = pa.al("p_stage2", [128, H4], F32)
    p_wbf = pa.al("p_wbf", [128, 2 * H4], BF16)
    p_qb = [pa.al(f"p_q{i}", [128, 2048], F32) for i in range(5)]
    p_qi = pa.al("p_qi", [128, 2048], I32)
    p_tr8 = pa.al("p_tr8", [128, 2 * 8 * 128], FP8)
    p_trb = pa.al("p_trb", [128, 2 * 8 * 128], BF16)
    p_row = pa.al("p_row", [16, D], F32)
    p_rowb = pa.al("p_rowb", [16, D], BF16)
    p_w1s = pa.al("p_w1s", [128, 2 * H4], BF16)

    aa = Arena(ar_lo)
    WQ = aa.al("WQ", [128, 8 * D], BF16)
    WK = aa.al("WK", [128, 8 * D], BF16)
    WV = aa.al("WV", [128, 8 * D], BF16)
    WO = aa.al("WO", [128, 8 * D], BF16)
    Xt = aa.al("Xt", [128, 4 * D], F32)
    ln1_tok = aa.al("ln1_tok", [128, D], BF16)
    ln1_fm = aa.al("ln1_fm", [128, 8 * 512], BF16)
    sqq = aa.al("sqq", [128, 512], F32)
    sqk = aa.al("sqk", [128, 512], F32)
    sqs = aa.al("sqs", [128, 512], F32)
    Vfm = aa.al("Vfm", [128, 8 * 512], BF16)
    QKfm = aa.al("QKfm", [128, 8 * 512], BF16)
    qkvfm = aa.al("qkvfm", [128, 8 * 512], BF16)
    coefb = aa.al("coefb", [16, 512], BF16)
    Atok = aa.al("Atok", [128, 4 * D], F32)
    lnA = aa.al("lnA", [128, D], F32)

    ba = Arena(ar_lo)
    W1Q = ba.al("W1Q", [128, 2 * 8 * 1024], BF16)
    W2QT = ba.al("W2QT", [128, 32 * D], FP8)
    ln2_fm = ba.al("ln2_fm", [128, 8 * 512], BF16)
    ln2_tok = ba.al("ln2_tok", [128, D], BF16)
    Hq = ba.al("Hq", [128, 32 * 512], FP8)
    Otok = ba.al("Otok", [128, 4 * D], F32)
    qsets = []
    for si in range(4):
        slots = {nm: ba.al(f"q{si}_{nm}", [128, 512], F32)
                 for nm in ["hv", "hc", "t64", "d2y"]}
        slots["g"] = ba.al(f"q{si}_g", [128, 512], BF16)
        slots["ii"] = ba.al(f"q{si}_ii", [128, 512], I32)
        qsets.append(slots)
    for si in (4, 5):
        k = si - 4
        ob = ba.map["Otok"] + k * 8192
        slots = {nm: ba.at(f"q{si}_{nm}", [128, 512], F32, ob + i * 2048)
                 for i, nm in enumerate(["hv", "hc", "t64", "d2y"])}
        slots["g"] = ba.at(f"q{si}_g", [128, 512], BF16,
                           ba.map["ln2_tok"] + k * 1024)
        slots["ii"] = ba.al(f"q{si}_ii", [128, 512], I32)
        qsets.append(slots)

    PT = nc.alloc_psum_tensor("psTR", [128, 1024], BF16)
    PS = [nc.alloc_psum_tensor(f"ps{i}", [128, 512], F32) for i in range(6)]
    PC = nc.alloc_psum_tensor("psC", [128, 512], F32)

    p = Prog(nc)
    for s in ["d", "g", "t", "v", "s", "o"]:
        p.mksem(s)
    for i, v in enumerate(CONSTS):
        nc.const_aps.register(F32, float(v), cpool[:, i:i + 1])

    def dma(q_, dst, src, inc, deps=(), **kw):
        p.wait(q_, *deps)
        return p.op(q_, lambda e, d=dst, s=src: e.dma_start(d, s, **kw),
                    inc=inc, amt=16)

    def ts1(e, out, in_, s1, o1):
        return e.tensor_single_scalar(out, in_, s1, o1)

    for i, v in enumerate(CONSTS):
        p.op("v", lambda e, i=i, v=v: e.memset(cpool[:, i:i + 1], float(v)))
    p.op("v", lambda e: e.memset(stats[:, :], 1.0), inc="v")

    dma("y", ident[:, :], id_h[:, :], "d")
    dma("y", msel.ap().rearrange("p (c h) -> p c h", h=16),
        msel_h.ap().rearrange("(c p) h -> p c h", p=128), "d")
    dma("y", mselT[:, :], mselT_h[:, :], "d")
    dma("y", onesp[:, :], onesp_h[:, :], "d")
    for dst, src in [(bqc, bq_h), (bkc, bk_h), (bvc, bv_h)]:
        p.op("y", lambda e, d=dst, s=src: e.dma_start(
            d[:, :], s.ap().rearrange("(c p) -> p c", p=128),
            allow_slow_non_contiguous=True), inc="d", amt=16)
    p.op("y", lambda e: e.dma_start(
        b1c[:, :], b1_h.ap().rearrange("(c p) -> p c", p=128),
        allow_slow_non_contiguous=True), inc="d", amt=16)

    def chains_stage1(lanes):
        for L in lanes:
            w = L["width"]; b = L["bufset"]
            for nm in ["hv", "hc", "t64", "d2y", "g"]:
                L[nm] = b[nm][:, :w]
            L["ii"] = b["ii"][:, :w]
            L["sl"] = 2.0 * LN2 if L["signed"] else LN2
        for L in lanes:
            if L.get("relu_bias") is not None:
                p.wait("s", L.get("psum_tok"), L.get("war_v"))
                L["tk1"] = p.op("s", lambda e, L=L: e.activation(
                    L["hv"], L["psum_src"], AF.Relu, bias=L["relu_bias"],
                    scale=1.0), inc="s")
                L["xs"] = L["hv"]
            elif L["signed"]:
                p.wait("s", L.get("src_tok"), L.get("war_v"))
                L["tk1"] = p.op("s", lambda e, L=L: e.activation(
                    L["d2y"], L["xsrc"], AF.Square, bias=0.0,
                    scale=2.0 ** 16), inc="s")
                L["xs"] = L["xsrc"]
            else:
                L["tk1"] = None
                L["xs"] = L["xsrc"]
        for L in lanes:
            if L.get("relu_bias") is not None:
                p.wait("v", L["tk1"], L.get("war_s"))
                L["tk2"] = p.op("v", lambda e, L=L: ts1(
                    e, L["hc"], L["hv"], L["clamp_lo"], OP.max), inc="v")
            elif L["signed"]:
                csq = float(L["clamp_lo"]) ** 2 * 2.0 ** 32
                p.wait("v", L["tk1"], L.get("war_s"))
                L["tk2"] = p.op("v", lambda e, L=L, c=csq: ts1(
                    e, L["hc"], L["d2y"], c, OP.max), inc="v")
            else:
                p.wait("v", L.get("src_tok"), L.get("war_s"))
                L["tk2"] = p.op("v", lambda e, L=L: ts1(
                    e, L["hc"], L["xsrc"], L["clamp_lo"], OP.max), inc="v")
        for L in lanes:
            p.wait("s", L["tk2"], L.get("war_v") if L["tk1"] is None else None)
            sc = 1.0 if L["signed"] else 2.0 ** 32
            p.op("s", lambda e, L=L, sc=sc: e.activation(
                L["t64"], L["hc"], AF.Ln, bias=0.0, scale=sc), inc="s")
        for L in lanes:
            L["tk4"] = p.op("s", lambda e, L=L: e.activation(
                L["ii"], L["t64"], AF.Copy, bias=0.0, scale=1.0 / L["sl"]), inc="s")
        for L in lanes:
            p.wait("v", L["tk4"])
            p.op("v", lambda e, L=L: e.scalar_tensor_tensor(
                L["d2y"], L["ii"], L["sl"], L["t64"], OP.mult, OP.is_gt), inc="v")
        for L in lanes:
            L["tk7"] = p.op("v", lambda e, L=L: e.tensor_tensor(
                L["hc"], L["ii"], L["d2y"], OP.subtract), inc="v")
        for L in lanes:
            p.wait("s", L["tk7"])
            p.op("s", lambda e, L=L: e.activation(
                L["t64"], L["hc"], AF.Exp, bias=L["bias_p"], scale=-LN2), inc="s")
        for L in lanes:
            L["g_tok"] = p.op("s", lambda e, L=L: e.activation(
                L["g"], L["hc"], AF.Exp, bias=L["bias_m"], scale=LN2), inc="s")
        return lanes

    def chains_stage2(lanes):
        for L in lanes:
            p.wait("v", L["g_tok"])
            p.op("v", lambda e, L=L: e.tensor_tensor(
                L["d2y"], L["xs"], L["t64"], OP.mult), inc="v")
        for L in lanes:
            L["tk11"] = p.op("v", lambda e, L=L: ts1(
                e, L["ii"], L["d2y"], L["off_c"], OP.add), inc="v")
        for L in lanes:
            L["end_s"] = p.op("v", lambda e, L=L: e.scalar_tensor_tensor(
                L["hc"], L["d2y"], L["off_c"], L["ii"], OP.add, OP.is_lt), inc="v")
        for L in lanes:
            p.op("v", lambda e, L=L: e.tensor_tensor(
                L["t64"], L["ii"], L["hc"], OP.subtract), inc="v")
        for L in lanes:
            if L["off_sub"]:
                L["end_v"] = p.op("v", lambda e, L=L: e.scalar_tensor_tensor(
                    L["dst"], L["t64"], -float(L["off_sub"]), L["g"],
                    OP.add, OP.mult), inc="v")
            else:
                L["end_v"] = p.op("v", lambda e, L=L: e.tensor_tensor(
                    L["dst"], L["t64"], L["g"], OP.mult), inc="v")
        return lanes

    def quant_chain(width, xsrc, dst_ap, bias_p, bias_m, off_c, off_sub,
                    clamp_lo, signed, bufset, psum_tok=None, psum_src=None,
                    relu_bias=None, war_v=None, war_s=None, src_tok=None):
        L = dict(width=width, bufset=bufset, signed=signed, bias_p=bias_p,
                 bias_m=bias_m, clamp_lo=clamp_lo, war_v=war_v, war_s=war_s,
                 xsrc=xsrc, src_tok=src_tok, psum_src=psum_src,
                 psum_tok=psum_tok, relu_bias=relu_bias,
                 dst=dst_ap, off_c=off_c, off_sub=off_sub)
        chains_stage1([L])
        chains_stage2([L])
        return {"relu": L["tk1"], "end_v": L["end_v"], "end_s": L["end_s"]}

    def layer_norm_stats(idx_list, src, dump, src_tok=None, prev=None,
                         dump_war=None):
        n = len(idx_list)
        prev = prev or {}
        dwl = dump_war if isinstance(dump_war, list) else [dump_war]
        acc_tok = None
        for i, cidx in enumerate(idx_list):
            sl = src[:, cidx * D:(cidx + 1) * D]
            p.wait("v", src_tok if i == 0 else None,
                   prev.get("rstd") if i == 0 else None)
            p.op("v", lambda e, sl=sl, i=i: e.reduce_sum(
                stats[:, i:i + 1], sl, axis=mybir.AxisListType.X), inc="v")
            p.wait("s", src_tok if i == 0 else None,
                   prev.get("var") if i == 0 else None,
                   *(dwl if i == 0 else []))
            acc_tok = p.op("s", lambda e, sl=sl, i=i: e.activation(
                dump[:, :], sl, AF.Square, bias=0.0, scale=1.0,
                accum_out=stats[:, 16 + i:17 + i]), inc="s")
        p.wait("v", acc_tok)
        m_tok = p.op("v", lambda e, n=n: ts1(e, stats[:, 0:n], stats[:, 0:n],
                                             1.0 / D, OP.mult), inc="v")
        p.op("v", lambda e, n=n: ts1(e, stats[:, 16:16 + n], stats[:, 16:16 + n],
                                     1.0 / D, OP.mult), inc="v")
        p.wait("s", m_tok)
        msq_tok = p.op("s", lambda e, n=n: e.activation(
            stats[:, 48:48 + n], stats[:, 0:n], AF.Square, bias=0.0,
            scale=1.0), inc="s")
        p.wait("v", msq_tok)
        var_tok = p.op("v", lambda e, n=n: e.tensor_tensor(
            stats[:, 16:16 + n], stats[:, 16:16 + n], stats[:, 48:48 + n],
            OP.subtract), inc="v")
        p.wait("s", var_tok)
        p.op("s", lambda e, n=n: e.activation(stats[:, 48:48 + n], stats[:, 16:16 + n],
                                              AF.Ln, bias=C_EPS, scale=1.0), inc="s")
        rstd_tok = p.op("s", lambda e, n=n: e.activation(
            stats[:, 32:32 + n], stats[:, 48:48 + n], AF.Exp, bias=0.0,
            scale=-0.5), inc="s")
        return {"rstd": rstd_tok, "var": var_tok}

    w1_ld, w1_ev, w1_st = {}, {}, {}

    def w1_prefetch(dc):
        for hg in range(4):
            gidx = dc * 4 + hg
            slot = gidx % 8
            tok = None
            for j in range(8):
                hcn = hg * 8 + j
                tok = dma("g", p_w1s[:, slot * 1024 + j * 128: slot * 1024 + (j + 1) * 128],
                          w1_h[hcn * 128:(hcn + 1) * 128, dc * 128:(dc + 1) * 128],
                          "g", deps=[w1_ev.get(gidx - 8)] if j == 0 else ())
            w1_ld[gidx] = tok

    w1_prefetch(0)
    w1_prefetch(1)
    for wsrc, wdst in [(wq_h, wqb_h), (wk_h, wkb_h), (wv_h, wvb_h), (wo_h, wob_h)]:
        for c in range(4):
            dma("g", wdst[c * 256:(c + 1) * 256, :], wsrc[c * 256:(c + 1) * 256, :], "g")
    if stage < 1:
        return _finish(nc, p)
    p.op("v", lambda e: e.memset(p_row[:, :], 0.0), inc="v")
    tok_bo = dma("y", p_row[0:1, :], bo_h[:, :], "d", deps=["v"])
    p.wait("v", tok_bo)
    p.op("v", lambda e: e.tensor_copy(p_rowb[:, :], p_row[:, :]), inc="v")
    p.wait("t", "v")
    for half in range(2):
        p.op("t", lambda e, half=half: e.matmul(
            PS[0][:, :], onesp[:, :], p_rowb[:, half * 512:(half + 1) * 512],
            start=True, stop=True), inc="t")
        p.wait("v", "t")
        p.op("v", lambda e, half=half: e.tensor_copy(
            bo_bc[:, half * 512:(half + 1) * 512], PS[0][:, :]), inc="v")
        p.wait("t", "v")
    tok_b2 = dma("y", p_row[0:1, :], b2_h[:, :], "d", deps=["v"])
    p.wait("v", tok_b2)
    p.op("v", lambda e: e.tensor_copy(p_rowb[:, :], p_row[:, :]), inc="v")
    p.wait("t", "v")
    for half in range(2):
        p.op("t", lambda e, half=half: e.matmul(
            PS[0][:, :], onesp[:, :], p_rowb[:, half * 512:(half + 1) * 512],
            start=True, stop=True), inc="t")
        p.wait("v", "t")
        p.op("v", lambda e, half=half: e.tensor_copy(
            b2stage[:, half * 512:(half + 1) * 512], PS[0][:, :]), inc="v")
        p.wait("t", "v")
    pset = {nm: p_qb[k] for k, nm in enumerate(["hv", "hc", "t64", "d2y", "g"])}
    pset["ii"] = p_qi
    b2tok = quant_chain(D, b2stage[:, :D], b2q_bc[:, :], C_P26L2, C_M26L2,
                        C_4096_5, 4096, 2.0 ** -14, signed=True, bufset=pset,
                        war_v=p.latest("v"), war_s=p.latest("s"),
                        src_tok=p.latest("v"))
    if stage < 2:
        return _finish(nc, p)
    prep_sets = []
    for si in range(4):
        ps_ = {nm: p_qb[k].ap()[:, si * 512:(si + 1) * 512]
               for k, nm in enumerate(["hv", "hc", "t64", "d2y", "g"])}
        ps_["ii"] = p_qi.ap()[:, si * 512:(si + 1) * 512]
        prep_sets.append(ps_)
    prep_hist = {0: {"end_v": b2tok["end_v"], "end_s": b2tok["end_s"]},
                 1: {"end_v": b2tok["end_v"], "end_s": b2tok["end_s"]},
                 2: None, 3: None}
    st8_tok = {}

    tr_last = {}

    def w2_transposes(dc):
        p.wait("t", prep_hist[2]["end_v"], prep_hist[3]["end_v"])
        for hg in range(4):
            gidx = dc * 4 + hg
            buf = gidx % 2
            tr_tok = None
            for j in range(8):
                hcn = hg * 8 + j
                src = p_wbf[:, (dc % 2) * H4 + hcn * 128: (dc % 2) * H4 + (hcn + 1) * 128]
                tr_tok = p.op("t", lambda e, src=src, j=j: e.transpose(
                    PT[:, j * 128:(j + 1) * 128], src, ident[:, :]),
                    inc="t" if j == 7 else None)
            tr_last[dc] = tr_tok
            p.wait("v", tr_tok, st8_tok.get(gidx - 2))
            ev = p.op("v", lambda e, buf=buf: e.tensor_copy(
                p_tr8[:, buf * 1024:(buf + 1) * 1024], PT[:, :]), inc="v")
            p.wait("t", ev)
            for j in range(8):
                hcn = hg * 8 + j
                st8_tok[gidx] = dma(
                    "y", w2qt_h[hcn * 128:(hcn + 1) * 128, dc * 128:(dc + 1) * 128],
                    p_tr8[:, buf * 1024 + j * 128: buf * 1024 + (j + 1) * 128],
                    "d", deps=[ev] if j == 0 else ())
            slot = gidx % 8
            p.wait("t", w1_ld[gidx])
            trb = None
            for j in range(8):
                src = p_w1s[:, slot * 1024 + j * 128: slot * 1024 + (j + 1) * 128]
                trb = p.op("t", lambda e, src=src, j=j: e.transpose(
                    PT[:, j * 128:(j + 1) * 128], src, ident[:, :]),
                    inc="t" if j == 7 else None)
            p.wait("v", trb, w1_st.get(gidx - 2))
            evb = p.op("v", lambda e, buf=buf: e.tensor_copy(
                p_trb[:, buf * 1024:(buf + 1) * 1024], PT[:, :]), inc="v")
            w1_ev[gidx] = evb
            p.wait("t", evb)
            p.wait("g", evb)
            w1_st[gidx] = dma("g", w1t_h[dc * 128:(dc + 1) * 128,
                                         hg * 1024:(hg + 1) * 1024],
                              p_trb[:, buf * 1024:(buf + 1) * 1024], "g")
        if 2 <= dc + 1 <= 7:
            w1_prefetch(dc + 1)

    stg = [p_stage, p_stage2]
    dtoks = {0: dma("y", p_stage[:, :], w2_h[0:128, :], "d"),
             1: dma("y", p_stage2[:, :], w2_h[128:256, :], "d")}
    dc_flush = {}
    for dc in range(8):
        if 1 <= dc <= 6:
            dtoks[dc + 1] = dma("y", stg[(dc + 1) % 2][:, :],
                                w2_h[(dc + 1) * 128:(dc + 2) * 128, :], "d",
                                deps=dc_flush[dc - 1])
        p.wait("v", tr_last.get(dc - 2))
        pend = None
        for m2 in range(4):
            lanes = []
            for j in range(2):
                sl8 = 2 * m2 + j
                si = 2 * (m2 % 2) + j
                hist = prep_hist[si]
                lanes.append(dict(
                    width=512, bufset=prep_sets[si], signed=True,
                    bias_p=C_P19L2, bias_m=C_M19L2, clamp_lo=2.0 ** -6,
                    xsrc=stg[dc % 2][:, sl8 * 512:(sl8 + 1) * 512],
                    src_tok=dtoks[dc],
                    war_v=hist["end_v"] if hist else None,
                    war_s=hist["end_s"] if hist else None,
                    dst=p_wbf[:, (dc % 2) * H4 + sl8 * 512: (dc % 2) * H4 + (sl8 + 1) * 512],
                    off_c=C_64_5, off_sub=64, si=si))
            chains_stage1(lanes)
            if pend is not None:
                chains_stage2(pend)
                for L in pend:
                    prep_hist[L["si"]] = {"end_v": L["end_v"], "end_s": L["end_s"]}
            if dc > 0 and m2 == 1:
                w2_transposes(dc - 1)
            pend = lanes
        chains_stage2(pend)
        for L in pend:
            prep_hist[L["si"]] = {"end_v": L["end_v"], "end_s": L["end_s"]}
        dc_flush[dc] = [prep_hist[2]["end_v"], prep_hist[3]["end_v"],
                        prep_hist[2]["end_s"], prep_hist[3]["end_s"]]
    w2_transposes(7)
    if debug:
        p.wait("y", "v")
        dma("y", dbg["d_w2q"][:, :], p_wbf[:, 0:H4], "o")
        dma("y", dbg["d_pg"][:, :], p_qb[4][:, :], "o")
        dma("y", dbg["d_pr"][:, :], p_qb[2][:, :], "o")
        dma("y", dbg["d_psrc"][:, :], p_stage[:, :], "o")
    if stage < 3:
        return _finish(nc, p)

    if stage < 4:
        return _finish(nc, p)
    stats_prev = None
    tok_u_ready = None
    for sup in range(nsup):
        base_tok = sup * sup_tok
        p.barrier_on("y")
        p.barrier_on("g")
        for Wd, Ws in [(WQ, wqb_h), (WK, wkb_h), (WV, wvb_h), (WO, wob_h)]:
            for c in range(8):
                dma("g", Wd[:, c * D:(c + 1) * D], Ws[c * 128:(c + 1) * 128, :], "d")
        tok_wload = p.latest("d")
        tok_qkv_last = None
        tok_wo_last = None
        tok_coef_last = None
        tok_bc_last = None
        tok_uadd = [None] * 4
        for lt in range(ta_per_sup):
            t0 = base_tok + lt * 512
            xt_tok = []
            for tc in range(4):
                xt_tok.append(dma("y", Xt[:, tc * D:(tc + 1) * D],
                                  x_h[t0 + tc * 128:t0 + tc * 128 + 128, :], "d",
                                  deps=[tok_uadd[tc],
                                        stats_prev["rstd"] if stats_prev else None]))
            stats_prev = layer_norm_stats([0, 1, 2, 3], Xt, lnA,
                                          src_tok=xt_tok[3], prev=stats_prev,
                                          dump_war=tok_uadd[3])
            rstd_tok = stats_prev["rstd"]
            tr_tok = None
            ev_toks = []
            for tc in range(4):
                p.wait("v", rstd_tok, tok_qkv_last)
                ap_tok = p.op("v", lambda e, tc=tc: e.tensor_scalar(
                    ln1_tok[:, :], Xt[:, tc * D:(tc + 1) * D],
                    stats[:, tc:tc + 1], stats[:, 32 + tc:33 + tc],
                    OP.subtract, OP.mult), inc="v")
                p.wait("t", ap_tok, ev_toks[-1] if ev_toks else None)
                for fc in range(8):
                    tr_tok = p.op("t", lambda e, fc=fc: e.transpose(
                        PT[:, fc * 128:(fc + 1) * 128],
                        ln1_tok[:, fc * 128:(fc + 1) * 128], ident[:, :]),
                        inc="t" if fc == 7 else None)
                p.wait("v", tr_tok)
                ev_toks.append(p.op("v", lambda e, tc=tc: e.tensor_copy(
                    _fmcols(ln1_fm, 512, tc), PT[:, :]), inc="v"))
            if debug:
                dma("y", dbg["d_ln1fm"][:, :], ln1_fm[:, :], "o", deps=[ev_toks[-1]])
            rd = {}
            exp_tok = {}
            vfm_tok = {}
            sqs_tok = None
            for oc in range(8):
                par = oc % 2
                psq, psk, psv = PS[3 * par], PS[3 * par + 1], PS[3 * par + 2]
                if oc >= 2:
                    p.wait("t", *rd[oc - 2])
                else:
                    p.wait("t", ev_toks[-1], tok_wload, tok_wo_last)
                mm_toks = []
                for W_, psx in [(WQ, psq), (WK, psk), (WV, psv)]:
                    for fc in range(8):
                        lhsT = W_[:, fc * D + oc * 128: fc * D + (oc + 1) * 128]
                        rhs = ln1_fm[:, fc * 512:(fc + 1) * 512]
                        tk = p.op("t", lambda e, lhsT=lhsT, rhs=rhs, psx=psx, fc=fc: e.matmul(
                            psx[:, :], lhsT, rhs, start=(fc == 0), stop=(fc == 7)),
                            inc="t" if fc == 7 else None)
                        if tk:
                            mm_toks.append(tk)
                tok_qkv_last = mm_toks[-1]
                p.wait("v", mm_toks[2])
                vfm_tok[oc] = p.op("v", lambda e, oc=oc, psv=psv: ts1(
                    e, Vfm[:, oc * 512:(oc + 1) * 512], psv[:, :],
                    bvc[:, oc:oc + 1], OP.add), inc="v")
                p.wait("s", mm_toks[0], sqs_tok)
                p.op("s", lambda e, oc=oc, psq=psq: e.activation(
                    sqq[:, :], psq[:, :], AF.Square, bias=bqc[:, oc:oc + 1],
                    scale=1.0), inc="s")
                p.wait("s", mm_toks[1])
                sq_tok = p.op("s", lambda e, oc=oc, psk=psk: e.activation(
                    sqk[:, :], psk[:, :], AF.Square, bias=bkc[:, oc:oc + 1],
                    scale=1.0), inc="s")
                p.wait("v", sq_tok, exp_tok.get(oc - 1), tok_coef_last if oc == 0 else None)
                sqs_tok = p.op("v", lambda e: e.tensor_add(sqs[:, :], sqq[:, :],
                                                           sqk[:, :]), inc="v")
                p.wait("s", sqs_tok)
                exp_tok[oc] = p.op("s", lambda e, oc=oc: e.activation(
                    QKfm[:, oc * 512:(oc + 1) * 512], sqs[:, :], AF.Exp,
                    bias=0.0, scale=-0.5), inc="s")
                rd[oc] = (vfm_tok[oc], sq_tok)
            if debug:
                dma("y", dbg["d_qkfm"][:, :], QKfm[:, :], "o", deps=[exp_tok[7]])
            p.wait("t", tok_bc_last)
            for oc in range(8):
                p.wait("t", exp_tok[oc])
                tok_coef_last = p.op("t", lambda e, oc=oc: e.matmul(
                    PC[0:16, :], msel[:, oc * 16:(oc + 1) * 16],
                    QKfm[:, oc * 512:(oc + 1) * 512],
                    start=(oc == 0), stop=(oc == 7)),
                    inc="t" if oc == 7 else None)
            p.wait("v", tok_coef_last)
            coefb_tok = p.op("v", lambda e: e.tensor_copy(coefb[:, :], PC[0:16, :]), inc="v")
            stt_toks = {}
            for oc in range(8):
                bank = PS[3] if oc % 2 == 0 else PC
                p.wait("t", coefb_tok if oc < 2 else stt_toks[oc - 2],
                       rd[7][1] if oc == 0 else None)
                bc_tok = p.op("t", lambda e, oc=oc, bank=bank: e.matmul(
                    bank[0:128, :], mselT[:, oc * 128:(oc + 1) * 128],
                    coefb[:, :], start=True, stop=True), inc="t")
                p.wait("v", bc_tok, tok_wo_last if oc == 0 else None)
                stt_toks[oc] = p.op("v", lambda e, oc=oc, bank=bank: e.scalar_tensor_tensor(
                    qkvfm[:, oc * 512:(oc + 1) * 512], bank[0:128, :], 1.0,
                    Vfm[:, oc * 512:(oc + 1) * 512], OP.bypass, OP.mult), inc="v")
            tok_bc_last_list = []
            ev2 = {}
            for tp in range(4):
                for oh in range(2):
                    pse = PS[4 + oh]
                    p.wait("t", stt_toks[7], ev2.get(tp - 1),
                           rd[7][1] if tp == 0 else None,
                           rd[7][0] if tp == 0 else None)
                    for fc in range(8):
                        lhsT = qkvfm[:, fc * 512 + tp * 128: fc * 512 + (tp + 1) * 128]
                        rhs = WO[:, fc * D + oh * 512: fc * D + (oh + 1) * 512]
                        tok_wo_last = p.op("t", lambda e, lhsT=lhsT, rhs=rhs, pse=pse, fc=fc: e.matmul(
                            pse[:, :], lhsT, rhs, start=(fc == 0), stop=(fc == 7)),
                            inc="t" if fc == 7 else None)
                    p.wait("v", tok_wo_last, tok_uadd[tp])
                    ev2[tp] = p.op("v", lambda e, tp=tp, oh=oh, pse=pse: e.tensor_tensor(
                        Atok[:, tp * D + oh * 512: tp * D + (oh + 1) * 512],
                        pse[:, :], bo_bc[:, oh * 512:(oh + 1) * 512],
                        OP.add), inc="v")
            if debug:
                dma("y", dbg["d_a"][:, :], Atok[:, :], "o", deps=[ev2[3]])
            stats_prev = layer_norm_stats([0, 1, 2, 3], Atok, lnA,
                                          src_tok=ev2[3], prev=stats_prev)
            rstd_tok = stats_prev["rstd"]
            for tc in range(4):
                cc = lt * 4 + tc
                p.wait("v", rstd_tok)
                p.op("v", lambda e, tc=tc: e.tensor_scalar(
                    lnA[:, :], Atok[:, tc * D:(tc + 1) * D],
                    stats[:, tc:tc + 1], stats[:, 32 + tc:33 + tc],
                    OP.subtract, OP.mult), inc="v")
                tok_uadd[tc] = p.op("v", lambda e, tc=tc, cc=cc: e.tensor_tensor(
                    U[:, cc * D:(cc + 1) * D], Xt[:, tc * D:(tc + 1) * D],
                    lnA[:, :], OP.add), inc="v")
                tok_u_ready = tok_uadd[tc]
        if debug:
            dma("y", dbg["d_u"][:, :], U[:, :], "o", deps=[tok_uadd[3]])
        if stage < 5:
            continue
        for c in range(32):
            dma("g", W2QT[:, c * D:(c + 1) * D], w2qt_h[c * 128:(c + 1) * 128, :],
                "d", deps=[tok_qkv_last, tok_wo_last] if c == 0 else ())
        tok_wload = p.latest("d")
        qld, qmm_last = {}, {}

        def w1_qload(gq):
            q4 = gq % 4
            slot = gq % 2
            tok = None
            for dcq in range(8):
                tok = dma("g", W1Q[:, slot * 8192 + dcq * 1024:
                                   slot * 8192 + (dcq + 1) * 1024],
                          w1t_h[dcq * 128:(dcq + 1) * 128, q4 * 1024:(q4 + 1) * 1024],
                          "d", deps=[qmm_last.get(gq - 2)] if dcq == 0 else ())
            qld[gq] = tok

        w1_qload(0)
        w1_qload(1)
        tok_b2_last = None
        tok_mm1_last = None
        tok_ost = {}
        tr_tok = None
        ev_toks = []
        ev3_prev = {}
        chain_hist = {si: None for si in range(6)}

        lnslots = [ln2_tok, ln2_tok2]
        rstd_u = None
        ap_toks = []

        def b_stats(bt_):
            nonlocal stats_prev, rstd_u, ev_toks
            cc0_ = bt_ * 4
            stats_prev = layer_norm_stats([cc0_, cc0_ + 1, cc0_ + 2, cc0_ + 3],
                                          U, b2stage, src_tok=tok_u_ready,
                                          prev=stats_prev, dump_war=None)
            rstd_u = stats_prev["rstd"]
            ev_toks = []

        def b_apply(k, bt_):
            cc0_ = bt_ * 4
            p.wait("v", rstd_u, tok_mm1_last if k == 0 else None)
            return p.op("v", lambda e, k=k, cc0_=cc0_: e.tensor_scalar(
                lnslots[k % 2][:, :], U[:, (cc0_ + k) * D:(cc0_ + k + 1) * D],
                stats[:, k:k + 1], stats[:, 32 + k:33 + k],
                OP.subtract, OP.mult), inc="v")

        def b_trev(k):
            nonlocal tr_tok
            p.wait("t", ap_toks[k], ev_toks[k - 1] if k > 0 else None)
            for fc in range(8):
                tr_tok = p.op("t", lambda e, fc=fc, k=k: e.transpose(
                    PT[:, fc * 128:(fc + 1) * 128],
                    lnslots[k % 2][:, fc * 128:(fc + 1) * 128], ident[:, :]),
                    inc="t" if fc == 7 else None)
            p.wait("v", tr_tok)
            ev_toks.append(p.op("v", lambda e, k=k: e.tensor_copy(
                _fmcols(ln2_fm, 512, k), PT[:, :]), inc="v"))

        b_stats(0)
        ap_toks = [b_apply(0, 0), b_apply(1, 0)]
        b_trev(0)
        ap_toks.append(b_apply(2, 0))
        b_trev(1)
        ap_toks.append(b_apply(3, 0))
        b_trev(2)
        b_trev(3)
        for bt in range(tb_per_sup):
            cc0 = bt * 4
            pe3 = ev3_prev
            hq_done = {}
            relu_toks = {}
            pend = None
            last_v = None
            for g_ in range(11):
                lanes = []
                for ln_ in range(3):
                    hcn = 3 * g_ + ln_
                    if hcn >= 32:
                        break
                    gq = bt * 4 + hcn // 8
                    slot = gq % 2
                    psh = PS[hcn % 4]
                    p.wait("t", ev_toks[-1] if hcn == 0 else None,
                           tok_wload if hcn == 0 else None,
                           qld[gq] if hcn % 8 == 0 else None,
                           pe3.get(4) if hcn == 2 else None,
                           pe3.get(5) if hcn == 3 else None,
                           relu_toks.get(hcn - 4))
                    mmtok = None
                    for dc in range(8):
                        o0 = slot * 8192 + dc * 1024 + (hcn % 8) * 128
                        lhsT = W1Q[:, o0:o0 + 128]
                        rhs = ln2_fm[:, dc * 512:(dc + 1) * 512]
                        mmtok = p.op("t", lambda e, lhsT=lhsT, rhs=rhs, psh=psh, dc=dc: e.matmul(
                            psh[:, :], lhsT, rhs, start=(dc == 0), stop=(dc == 7)),
                            inc="t" if dc == 7 else None)
                    tok_mm1_last = mmtok
                    if hcn % 8 == 7:
                        qmm_last[gq] = mmtok
                        if gq + 2 < 4 * tb_per_sup:
                            w1_qload(gq + 2)
                    si = 3 * (g_ % 2) + ln_
                    hist = chain_hist[si]
                    lanes.append(dict(
                        width=512, bufset=qsets[si], signed=False,
                        bias_p=C_P35L2, bias_m=C_M35L2, clamp_lo=2.0 ** -6,
                        psum_src=psh[:, :], psum_tok=mmtok,
                        relu_bias=b1c[:, hcn:hcn + 1],
                        war_v=hist["end_v"] if hist else None,
                        war_s=hist["end_s"] if hist else None,
                        dst=Hq[:, hcn * 512:(hcn + 1) * 512],
                        off_c=C_HALF, off_sub=0, hcn=hcn, si=si))
                chains_stage1(lanes)
                for L in lanes:
                    relu_toks[L["hcn"]] = L["tk1"]
                if pend is not None:
                    chains_stage2(pend)
                    for L in pend:
                        chain_hist[L["si"]] = {"end_v": L["end_v"], "end_s": L["end_s"]}
                        hq_done[L["hcn"]] = L["end_v"]
                    last_v = pend[-1]["end_v"]
                pend = lanes
            chains_stage2(pend)
            for L in pend:
                chain_hist[L["si"]] = {"end_v": L["end_v"], "end_s": L["end_s"]}
                hq_done[L["hcn"]] = L["end_v"]
            last_v = pend[-1]["end_v"]
            if debug and bt == 0:
                dma("y", dbg["d_h"][:, :], qsets[1]["hv"][:, :512], "o",
                    deps=[last_v])
                dma("y", dbg["d_hq"][:, :], Hq[:, :], "o", deps=[last_v])
            has_next = bt + 1 < tb_per_sup
            if has_next:
                b_stats(bt + 1)
                ap_toks = [b_apply(0, bt + 1), b_apply(1, bt + 1)]
            ev3 = {}
            gi = 0
            for tp in range(4):
                for dh in range(2):
                    pso = PS[2 + gi % 4]
                    if gi == 0:
                        p.wait("t", relu_toks[30])
                    elif gi == 1:
                        p.wait("t", relu_toks[31])
                    elif gi >= 4:
                        p.wait("t", ev3[gi - 4])
                    for hcn in range(32):
                        if gi == 0:
                            p.wait("t", hq_done[hcn])
                        lhsT = Hq[:, hcn * 512 + tp * 128: hcn * 512 + (tp + 1) * 128]
                        rhs = W2QT[:, hcn * D + dh * 512: hcn * D + (dh + 1) * 512]
                        tok_b2_last = p.op("t", lambda e, lhsT=lhsT, rhs=rhs, pso=pso, hcn=hcn: e.matmul(
                            pso[:, :], lhsT, rhs, start=(hcn == 0), stop=(hcn == 31)),
                            inc="t" if hcn == 31 else None)
                    p.wait("v", tok_b2_last, tok_ost.get(gi // 2))
                    ev3[gi] = p.op("v", lambda e, tp=tp, dh=dh, pso=pso: e.tensor_tensor(
                        Otok[:, tp * D + dh * 512: tp * D + (dh + 1) * 512],
                        pso[:, :], b2q_bc[:, dh * 512:(dh + 1) * 512],
                        OP.add), inc="v")
                    if has_next and gi <= 3:
                        b_trev(gi)
                        if gi <= 1:
                            ap_toks.append(b_apply(gi + 2, bt + 1))
                    gi += 1
            ev3_prev = ev3
            if debug and bt == 0:
                dma("y", dbg["d_o"][:, :], Otok[:, :], "o", deps=[ev3[7]])
            pend = None
            st2last = None
            for m_ in range(4):
                lanes = []
                for j in range(2):
                    seg = 2 * m_ + j
                    si = 2 * (m_ % 2) + j
                    hist = chain_hist[si]
                    lanes.append(dict(
                        width=512, bufset=qsets[si], signed=True,
                        bias_p=C_P26L2, bias_m=C_M26L2, clamp_lo=2.0 ** -14,
                        xsrc=Otok[:, seg * 512:(seg + 1) * 512],
                        src_tok=ev3[seg],
                        war_v=hist["end_v"] if hist else None,
                        war_s=hist["end_s"] if hist else None,
                        dst=Otok[:, seg * 512:(seg + 1) * 512],
                        off_c=C_4096_5, off_sub=4096, si=si))
                chains_stage1(lanes)
                if pend is not None:
                    chains_stage2(pend)
                    for L in pend:
                        chain_hist[L["si"]] = {"end_v": L["end_v"], "end_s": L["end_s"]}
                pend = lanes
            chains_stage2(pend)
            for L in pend:
                chain_hist[L["si"]] = {"end_v": L["end_v"], "end_s": L["end_s"]}
            st2last = pend[-1]["end_v"]
            if debug and bt == 0:
                dma("y", dbg["d_oq"][:, :], Otok[:, :], "o", deps=[st2last])
            stats_prev = layer_norm_stats([0, 1, 2, 3], Otok, b2stage,
                                          src_tok=st2last, prev=stats_prev,
                                          dump_war=None)
            rstd_tok = stats_prev["rstd"]
            for k in range(4):
                cc = cc0 + k
                p.wait("v", rstd_tok, tok_ost.get(k))
                p.op("v", lambda e, k=k: e.tensor_scalar(
                    Otok[:, k * D:(k + 1) * D], Otok[:, k * D:(k + 1) * D],
                    stats[:, k:k + 1], stats[:, 32 + k:33 + k],
                    OP.subtract, OP.mult), inc="v")
                ytok = p.op("v", lambda e, k=k, cc=cc: e.tensor_tensor(
                    Otok[:, k * D:(k + 1) * D], U[:, cc * D:(cc + 1) * D],
                    Otok[:, k * D:(k + 1) * D], OP.add), inc="v")
                r0 = base_tok + cc * 128
                tok_ost[k] = dma("y", y_h[r0:r0 + 128, :], Otok[:, k * D:(k + 1) * D],
                                 "o", deps=[ytok])

    p.wait("y", "o")
    return _finish(nc, p)


def _finish(nc, p):
    p.barrier_on("y")
    with nc.Block() as block:
        block.tensor(lambda e: [fn(e) for fn in p.q["t"]])
        block.vector(lambda e: [fn(e) for fn in p.q["v"]])
        block.scalar(lambda e: [fn(e) for fn in p.q["s"]])
        block.sync(lambda e: [fn(e) for fn in p.q["y"]])
        block.gpsimd(lambda e: [fn(e) for fn in p.q["g"]])
    return nc


def _make_const_inputs():
    hd = 64
    msel = np.zeros((D, 16), dtype=np.float32)
    for d_ in range(D):
        msel[d_, d_ // hd] = 1.0
    onesp = np.zeros((16, 128), dtype=np.float32)
    onesp[0, :] = 1.0
    return {
        "cident": np.eye(128, dtype=np.float32).astype(ml_dtypes.bfloat16),
        "msel": msel.astype(ml_dtypes.bfloat16),
        "mselT": msel.T.copy().astype(ml_dtypes.bfloat16),
        "onesp": onesp.astype(ml_dtypes.bfloat16),
    }


def kernel(L0_attn_x, L0_attn_Wq, L0_attn_bq, L0_attn_Wk, L0_attn_bk,
           L0_attn_Wv, L0_attn_bv, L0_attn_Wo, L0_attn_bo,
           L0_ffn_w1, L0_ffn_b1, L0_ffn_w2, L0_ffn_b2):
    x = np.ascontiguousarray(np.asarray(L0_attn_x, dtype=np.float32))
    B, S, Dm = x.shape
    tok = B * S
    assert Dm == D and tok % NCORES == 0
    tokc = tok // NCORES
    xf = x.reshape(tok, D)

    nc = build_program(tokc)
    consts = _make_const_inputs()
    common = {
        "wq": np.ascontiguousarray(L0_attn_Wq, dtype=np.float32),
        "wk": np.ascontiguousarray(L0_attn_Wk, dtype=np.float32),
        "wv": np.ascontiguousarray(L0_attn_Wv, dtype=np.float32),
        "wo": np.ascontiguousarray(L0_attn_Wo, dtype=np.float32),
        "bq": np.ascontiguousarray(L0_attn_bq, dtype=np.float32),
        "bk": np.ascontiguousarray(L0_attn_bk, dtype=np.float32),
        "bv": np.ascontiguousarray(L0_attn_bv, dtype=np.float32),
        "bo": np.ascontiguousarray(L0_attn_bo, dtype=np.float32).reshape(1, D),
        "w1": np.ascontiguousarray(L0_ffn_w1, dtype=np.float32),
        "b1": np.ascontiguousarray(L0_ffn_b1, dtype=np.float32),
        "w2": np.ascontiguousarray(L0_ffn_w2, dtype=np.float32),
        "b2": np.ascontiguousarray(L0_ffn_b2, dtype=np.float32).reshape(1, D),
        **consts,
    }
    in_maps = []
    for c in range(NCORES):
        m = dict(common)
        m["x"] = np.ascontiguousarray(xf[c * tokc:(c + 1) * tokc])
        in_maps.append(m)
    res = run_bass_kernel_spmd(nc, in_maps, core_ids=list(range(NCORES)))
    out = np.concatenate([r["y"] for r in res.results], axis=0)
    return out.reshape(B, S, D).astype(np.float32)
